# version 02
# baseline (speedup 1.0000x reference)
import contextlib
import numpy as np
import ml_dtypes
import concourse.bass as bass
import concourse.mybir as mybir
from concourse.bass_utils import run_bass_kernel_spmd

F32 = mybir.dt.float32
BF16 = mybir.dt.bfloat16
AF = mybir.ActivationFunctionType
OP = mybir.AluOpType
BF = ml_dtypes.bfloat16

D = 2048
T = 2048
NBK = 4
BS = 512
KC = 16
LNEPS = 1e-5
RMSEPS = float(np.finfo(np.float32).eps)
FWHT_SCALE = float(1.0 / np.sqrt(8192.0))
NH = 16
NE = 16
PAIRS = [[0, 1], [2, 3], [4, 5], [6, 7]]


def lhsT_tiles(w_out_in):
    M, K = w_out_in.shape
    Mc, Kcc = M // 128, K // 128
    t = np.asarray(w_out_in).reshape(Mc, 128, Kcc, 128).transpose(0, 3, 2, 1)
    return np.ascontiguousarray(t.reshape(Mc, 128, Kcc * 128).astype(np.float32))


def core_blocks(r):
    return [2 * p + 1 - r for p in range(NBK)]


def make_masks(r):
    tq = np.arange(BS)
    tril = np.zeros((4, 128, BS), dtype=np.float32)
    for u in range(4):
        tk = u * 128 + np.arange(128)
        tril[u] = (tk[:, None] <= tq[None, :]).astype(np.float32)
    if r == 0:
        return tril, np.ones_like(tril)
    return np.zeros_like(tril), tril


def shard_x(x):
    out = []
    for c in range(8):
        b, r = c // 2, c % 2
        xt = np.concatenate([x[b, g * BS:(g + 1) * BS, :] for g in core_blocks(r)], axis=0)
        out.append(np.ascontiguousarray(xt.T.astype(np.float32)))
    return out


def unshard_x(shards):
    out = np.zeros((4, 4096, D), dtype=np.float32)
    for c in range(8):
        b, r = c // 2, c % 2
        xt = np.asarray(shards[c]).T
        for p, g in enumerate(core_blocks(r)):
            out[b, g * BS:(g + 1) * BS, :] = xt[p * BS:(p + 1) * BS, :]
    return out



ENGS = ["tensor", "vector", "scalar", "gpsimd", "sync"]


class Ctx:
    def __init__(self, nc, block, sems):
        self.nc = nc
        self.block = block
        self.sem = sems
        self.cnt = {k: 0 for k in sems}
        self.dummy = None

    def barrier(self):
        bar = self.sem["bar"]
        tgt = self.cnt["bar"] + 5
        blk = self.block

        @blk.tensor
        def _(te):
            te.nop().then_inc(bar, 1)
            te.wait_ge(bar, tgt)

        @blk.vector
        def _(v):
            v.engine_nop().then_inc(bar, 1)
            v.wait_ge(bar, tgt)

        @blk.scalar
        def _(sc):
            sc.activation(self.dummy[0:1, 1:2], self.dummy[0:1, 0:1], AF.Copy,
                          bias=0.0, scale=1.0).then_inc(bar, 1)
            sc.wait_ge(bar, tgt)

        @blk.gpsimd
        def _(g):
            g.engine_nop().then_inc(bar, 1)
            g.wait_ge(bar, tgt)

        @blk.sync
        def _(s):
            s.sem_inc(bar, 1)
            s.wait_ge(bar, tgt)

        self.cnt["bar"] = tgt


def tiles4(ap, Mc):
    return ap.rearrange("(mc ts r) (dd t) -> mc ts (r dd) t", mc=Mc, ts=4, r=32, dd=4, t=512)


def emit_proj(ctx, W_ap, Mc, src_slice, evac, wbuf, pf, kc_n=KC, n_ts=4, ts_w=512, ev_name="ev"):
    block, sem = ctx.block, ctx.sem
    mm_base, ev_base = ctx.cnt["mm"], ctx.cnt[ev_name]
    wb = [ctx.cnt["wA"], ctx.cnt["wB"]]
    npf = len(pf)

    @block.gpsimd
    def _(g):
        for mc in range(Mc):
            if mc >= 2:
                g.wait_ge(sem["mm"], mm_base + (mc - 1) * n_ts)
            g.dma_start(out=wbuf[:, (mc % 2) * kc_n * 128:(mc % 2 + 1) * kc_n * 128],
                        in_=W_ap[mc]).then_inc(sem["wA" if mc % 2 == 0 else "wB"], 16)
    ctx.cnt["wA"] = wb[0] + 16 * ((Mc + 1) // 2)
    ctx.cnt["wB"] = wb[1] + 16 * (Mc // 2)

    @block.tensor
    def _(te):
        idx = 0
        for mc in range(Mc):
            te.wait_ge(sem["wA" if mc % 2 == 0 else "wB"],
                       wb[mc % 2] + 16 * (mc // 2 + 1))
            woff = (mc % 2) * kc_n * 128
            for ts in range(n_ts):
                if idx >= npf:
                    te.wait_ge(sem[ev_name], ev_base + idx - npf + 1)
                bank = pf[idx % npf]
                for kc in range(kc_n):
                    last = te.matmul(bank[:, 0:ts_w],
                                     wbuf[:, woff + kc * 128: woff + (kc + 1) * 128],
                                     src_slice(kc, ts),
                                     start=(kc == 0), stop=(kc == kc_n - 1))
                last.then_inc(sem["mm"], 1)
                idx += 1
    ctx.cnt["mm"] = mm_base + Mc * n_ts

    idx = 0
    for mc in range(Mc):
        for ts in range(n_ts):
            evac(mc, ts, pf[idx % npf], mm_base + idx + 1)
            idx += 1
    assert ctx.cnt[ev_name] == ev_base + Mc * n_ts



def alloc_shared(nc, stack, kind="l0a"):
    sems = {}
    for n in ["wA", "wB", "wC", "wD", "xs", "xsA", "xsB", "st", "sA", "sB", "kvA", "kvB",
              "vA", "vB", "qld", "gld", "mm", "ev", "dv", "bar", "cc", "aux", "tp", "ax2"]:
        sems[n] = stack.enter_context(nc.semaphore(n))
    R = {}

    def sb(nm, shp, dt):
        R[nm] = stack.enter_context(nc.sbuf_tensor(nm, shp, dt))

    sb("xbf", [128, KC * T], BF16)
    sb("buf2", [128, KC * T], BF16)
    sb("scrA", [128, 8192], BF16)
    sb("statf", [1, 6144], F32)
    sb("rowbf", [1, 4096], BF16)
    sb("gbuf", [128, 2048], BF16)
    if kind in ("l0a", "l1a", "fused"):
        sb("mska", [128, 4 * BS], BF16)
        sb("mskb", [128, 4 * BS], BF16)
    if kind in ("l1a", "fused"):
        sb("cosSB", [32, T], BF16)
        sb("sinSB", [32, T], BF16)
        sb("permSB", [128, 128], BF16)
    if kind in ("l0f", "fused"):
        sb("h128sb", [128, 128], BF16)
        sb("h64sb", [128, 128], BF16)
    if kind in ("l1f", "fused"):
        sb("gcol", [128, 32], F32)
        sb("srow", [1, 8], F32)
        sb("ccol", [128, 16], F32)
        sb("lcol", [128, 32], F32)
        sb("dcol", [128, 16], F32)
        sb("nsb", [128, 4], F32)
        sb("sstage", [128, 2560], BF16)
    sb("ocol", [128, 1024], F32)
    sb("identb", [128, 128], BF16)
    sb("onec", [128, 1], BF16)
    sb("oner", [1, 128], BF16)
    sb("constf", [128, 8], F32)

    def psm(nm, shp, dt):
        R[nm] = stack.enter_context(nc.psum_tensor(nm, shp, dt))

    for i in range(4):
        psm(f"pf{i}", [128, 512], F32)
    for i in range(4):
        psm(f"pt{i}", [128, 1024], BF16)
    return sems, R


def emit_const_loads(ctx, R, io, kind="l0a"):
    sem = ctx.sem

    @ctx.block.gpsimd
    def _(g):
        g.dma_start(out=R["identb"][:], in_=io["ident"]).then_inc(sem["aux"], 16)
        ctx.cnt["aux"] += 16
        if kind in ("l0a", "l1a", "fused"):
            g.dma_start(out=R["mska"][:].rearrange("p (u t) -> p u t", u=4), in_=io["maskA"].rearrange("u p t -> p u t")).then_inc(sem["aux"], 16)
            g.dma_start(out=R["mskb"][:].rearrange("p (u t) -> p u t", u=4), in_=io["maskB"].rearrange("u p t -> p u t")).then_inc(sem["aux"], 16)
            ctx.cnt["aux"] += 32
        if kind in ("l1a", "fused"):
            g.dma_start(out=R["cosSB"][:], in_=io["cosT"]).then_inc(sem["aux"], 16)
            g.dma_start(out=R["sinSB"][:], in_=io["sinT"]).then_inc(sem["aux"], 16)
            g.dma_start(out=R["permSB"][:], in_=io["perm32"]).then_inc(sem["aux"], 16)
            ctx.cnt["aux"] += 48
        if kind in ("l0f", "fused"):
            g.dma_start(out=R["h128sb"][:], in_=io["h128"]).then_inc(sem["aux"], 16)
            g.dma_start(out=R["h64sb"][:], in_=io["h64b"]).then_inc(sem["aux"], 16)
            ctx.cnt["aux"] += 32
        if kind == "fused":
            g.memset(R["constf"][:, 4:5], EXPSHIFT)
            ctx.cnt["aux"] += 0
        g.memset(R["onec"][:], 1.0)
        g.memset(R["oner"][:], 1.0)
        g.memset(R["constf"][:, 0:1], LNEPS)
        g.memset(R["constf"][:, 1:2], RMSEPS)
        g.memset(R["constf"][:, 2:4], 0.0).then_inc(sem["ax2"], 1)
    ctx.cnt["ax2"] += 1

    @ctx.block.scalar
    def _(sc):
        sc.wait_ge(sem["ax2"], ctx.cnt["ax2"])



def emit_ln_affine(ctx, R, mode, apply_affine=True):
    block, sem = ctx.block, ctx.sem
    xbf, statf, rowbf = R["xbf"], R["statf"], R["rowbf"]
    pf = [R[f"pf{i}"] for i in range(4)]
    sqb = R["buf2"]
    S = lambda i: statf[0:1, i * 2048:(i + 1) * 2048]

    mm0 = ctx.cnt["mm"]

    @block.tensor
    def _(te):
        for kc in range(KC):
            for ts in range(4):
                last = te.matmul(pf[ts][0:1, 0:512], R["onec"][:],
                                 xbf[:, kc * T + ts * 512: kc * T + ts * 512 + 512],
                                 start=(kc == 0), stop=(kc == KC - 1))
        last.then_inc(sem["mm"], 1)
    ctx.cnt["mm"] = mm0 + 1

    ev0 = ctx.cnt["ev"]

    @block.scalar
    def _(sc):
        sc.wait_ge(sem["mm"], mm0 + 1)
        for ts in range(4):
            last = sc.activation(S(0)[:, ts * 512:(ts + 1) * 512], pf[ts][0:1, 0:512],
                                 AF.Copy, bias=0.0, scale=1.0)
        last.then_inc(sem["ev"], 1)
    ctx.cnt["ev"] = ev0 + 1

    dv0 = ctx.cnt["dv"]
    mm1 = ctx.cnt["mm"]

    @block.scalar
    def _(sc):
        for kc in range(KC):
            if kc >= 2:
                sc.wait_ge(sem["mm"], mm1 + kc - 1)
            sc.activation(sqb[:, (kc % 2) * T:(kc % 2 + 1) * T], xbf[:, kc * T:(kc + 1) * T],
                          AF.Square, bias=0.0, scale=1.0).then_inc(sem["dv"], 1)
    ctx.cnt["dv"] = dv0 + KC

    @block.tensor
    def _(te):
        te.wait_ge(sem["ev"], ev0 + 1)
        for kc in range(KC):
            te.wait_ge(sem["dv"], dv0 + kc + 1)
            for ts in range(4):
                last = te.matmul(pf[ts][0:1, 0:512], R["onec"][:],
                                 sqb[:, (kc % 2) * T + ts * 512:(kc % 2) * T + ts * 512 + 512],
                                 start=(kc == 0), stop=(kc == KC - 1))
            last.then_inc(sem["mm"], 1)
    ctx.cnt["mm"] = mm1 + KC
    mm_sq = ctx.cnt["mm"]

    ev1 = ctx.cnt["ev"]

    @block.scalar
    def _(sc):
        sc.wait_ge(sem["mm"], mm_sq)
        for ts in range(4):
            last = sc.activation(S(1)[:, ts * 512:(ts + 1) * 512], pf[ts][0:1, 0:512],
                                 AF.Copy, bias=0.0, scale=1.0)
        last.then_inc(sem["ev"], 1)
    ctx.cnt["ev"] = ev1 + 1

    def vstep(op, *args, **kw):
        @block.vector
        def _(v):
            v.wait_ge(sem["dv"], ctx.cnt["dv"])
            getattr(v, op)(*args, **kw).then_inc(sem["dv"], 1)
        ctx.cnt["dv"] += 1

    def scstep(out, in_, func, bias, scale=1.0):
        @block.scalar
        def _(sc):
            sc.wait_ge(sem["dv"], ctx.cnt["dv"])
            sc.activation(out, in_, func, bias=bias, scale=scale).then_inc(sem["dv"], 1)
        ctx.cnt["dv"] += 1

    @block.vector
    def _(v):
        v.wait_ge(sem["ev"], ev1 + 1)
    vstep("tensor_scalar_mul", S(0), S(0), 1.0 / T)
    vstep("tensor_scalar_mul", S(1), S(1), 1.0 / T)
    vstep("tensor_mul", S(2), S(0), S(0))
    vstep("tensor_sub", S(1), S(1), S(2))
    scstep(S(2), S(1), AF.Sqrt, R["constf"][0:1, 0:1])
    vstep("reciprocal", S(2), S(2))
    if mode in ("ln_rms", "ln_ln"):
        vstep("tensor_mul", S(1), S(1), S(2))
        vstep("tensor_mul", S(1), S(1), S(2))
        scstep(S(1), S(1), AF.Sqrt,
               R["constf"][0:1, 1:2] if mode == "ln_rms" else R["constf"][0:1, 0:1])
        vstep("reciprocal", S(1), S(1))
        vstep("tensor_mul", S(2), S(2), S(1))
    vstep("tensor_copy", rowbf[0:1, 0:2048], S(2))
    vstep("tensor_mul", S(1), S(0), rowbf[0:1, 0:2048])
    vstep("tensor_scalar_mul", S(1), S(1), -1.0)
    vstep("tensor_copy", rowbf[0:1, 2048:4096], S(1))
    dva = ctx.cnt["dv"] - 1

    if not apply_affine:
        return

    mmb = ctx.cnt["mm"]
    evb = ctx.cnt["ev"]

    @block.tensor
    def _(te):
        te.wait_ge(sem["ev"], evb)
        te.wait_ge(sem["dv"], dva + 1)
        for i in range(8):
            if i >= 4:
                te.wait_ge(sem["ev"], evb + i - 3)
            te.matmul(pf[i % 4][:, 0:512], R["oner"][:],
                      rowbf[0:1, i * 512:(i + 1) * 512],
                      start=True, stop=True).then_inc(sem["mm"], 1)
    ctx.cnt["mm"] = mmb + 8

    @block.scalar
    def _(sc):
        for i in range(8):
            sc.wait_ge(sem["mm"], mmb + i + 1)
            sc.activation(R["scrA"][:, i * 512:(i + 1) * 512], pf[i % 4][:, 0:512],
                          AF.Copy, bias=0.0, scale=1.0).then_inc(sem["ev"], 1)
    ctx.cnt["ev"] = evb + 8

    @block.vector
    def _(v):
        v.wait_ge(sem["ev"], evb + 8)
        for kc in range(KC):
            v.tensor_tensor(xbf[:, kc * T:(kc + 1) * T], xbf[:, kc * T:(kc + 1) * T],
                            R["scrA"][:, 0:T], OP.mult).then_inc(sem["dv"], 1)
            ctx.cnt["dv"] += 1
            v.wait_ge(sem["dv"], ctx.cnt["dv"])
            v.tensor_tensor(xbf[:, kc * T:(kc + 1) * T], xbf[:, kc * T:(kc + 1) * T],
                            R["scrA"][:, T:2 * T], OP.add).then_inc(sem["dv"], 1)
            ctx.cnt["dv"] += 1



def emit_proj_spill(ctx, R, W_ap, act_fn, dst_view, src_sb):
    block, sem = ctx.block, ctx.sem
    pf = [R[f"pf{i}"] for i in range(4)]
    stage = R["buf2"]
    wbuf = R["scrA"][:, 0:4096]
    src = lambda kc, ts: src_sb[:, kc * T + ts * 512: kc * T + ts * 512 + 512]
    ev_base = ctx.cnt["ev"]
    sb_ = [ctx.cnt["sA"], ctx.cnt["sB"]]

    def evac(mc, ts, bank, mm_tgt):
        idx = mc * 4 + ts
        sp = "sA" if idx % 2 == 0 else "sB"

        @block.scalar
        def _(sc):
            if idx == 0:
                sc.wait_ge(sem["sA"], sb_[0])
                sc.wait_ge(sem["sB"], sb_[1])
            sc.wait_ge(sem["mm"], mm_tgt)
            if idx >= 2:
                sc.wait_ge(sem[sp], sb_[idx % 2] + 16 * (idx // 2))
            sc.activation(stage[:, (idx % 2) * 512:(idx % 2) * 512 + 512], bank[:, 0:512],
                          act_fn, bias=0.0, scale=1.0).then_inc(sem["ev"], 1)
        ctx.cnt["ev"] += 1

        @block.sync
        def _(s):
            s.wait_ge(sem["ev"], ev_base + idx + 1)
            s.dma_start(out=dst_view[mc, ts],
                        in_=stage[:, (idx % 2) * 512:(idx % 2) * 512 + 512]).then_inc(sem[sp], 16)
        ctx.cnt[sp] += 16

    emit_proj(ctx, W_ap, 16, src, evac, wbuf, pf)



def emit_l0_v_pass(ctx, R, Wv_ap, vin_ap, src_sb, act_fn=AF.Silu):
    block, sem = ctx.block, ctx.sem
    pf = [R[f"pf{i}"] for i in range(4)]
    pt = [R[f"pt{i}"] for i in range(4)]
    wbuf = R["scrA"][:, 0:4096]
    vstage = R["buf2"][:, 0:1024]
    stB = lambda u: R["buf2"][:, 1024 + u * T: 1024 + (u + 1) * T]
    vin_v = vin_ap.rearrange("(pu p) e -> pu p e", pu=16)
    src = lambda kc, ts: src_sb[:, kc * T + ts * 512: kc * T + ts * 512 + 512]

    for ts in range(4):
        mm_base, ev_base, tp_base, st_base = (ctx.cnt["mm"], ctx.cnt["ev"],
                                              ctx.cnt["tp"], ctx.cnt["st"])
        wb = [ctx.cnt["wA"], ctx.cnt["wB"]]

        @block.gpsimd
        def _(g):
            for mc in range(16):
                if mc >= 2:
                    g.wait_ge(sem["mm"], mm_base + (mc - 1) * 5)
                g.dma_start(out=wbuf[:, (mc % 2) * KC * 128:(mc % 2 + 1) * KC * 128],
                            in_=Wv_ap[mc]).then_inc(sem["wA" if mc % 2 == 0 else "wB"], 16)
        ctx.cnt["wA"] = wb[0] + 16 * 8
        ctx.cnt["wB"] = wb[1] + 16 * 8

        @block.tensor
        def _(te):
            for mc in range(16):
                te.wait_ge(sem["wA" if mc % 2 == 0 else "wB"],
                           wb[mc % 2] + 16 * (mc // 2 + 1))
                woff = (mc % 2) * KC * 128
                if mc >= 2:
                    te.wait_ge(sem["ev"], ev_base + mc - 1)
                for kc in range(KC):
                    last = te.matmul(pf[mc % 2][:, 0:512],
                                     wbuf[:, woff + kc * 128: woff + (kc + 1) * 128],
                                     src(kc, ts), start=(kc == 0), stop=(kc == KC - 1))
                last.then_inc(sem["mm"], 1)
                te.wait_ge(sem["ev"], ev_base + mc + 1)
                if mc >= 1:
                    te.wait_ge(sem["tp"], tp_base + mc * 4)
                for u in range(4):
                    te.transpose(pt[u][:, 0:128],
                                 vstage[:, (mc % 2) * 512 + u * 128:(mc % 2) * 512 + (u + 1) * 128],
                                 R["identb"][:]).then_inc(sem["mm"], 1)
        ctx.cnt["mm"] = mm_base + 16 * 5

        @block.scalar
        def _(sc):
            if ts == 0:
                sc.wait_ge(sem["sA"], ctx.cnt["sA"])
                sc.wait_ge(sem["sB"], ctx.cnt["sB"])
            for mc in range(16):
                sc.wait_ge(sem["mm"], mm_base + mc * 5 + 1)
                if mc >= 2:
                    sc.wait_ge(sem["tp"], tp_base + (mc - 1) * 4)
                sc.activation(vstage[:, (mc % 2) * 512:(mc % 2) * 512 + 512],
                              pf[mc % 2][:, 0:512], act_fn, bias=0.0, scale=1.0
                              ).then_inc(sem["ev"], 1)
        ctx.cnt["ev"] = ev_base + 16

        @block.vector
        def _(v):
            v.wait_ge(sem["st"], st_base)
            for mc in range(16):
                for u in range(4):
                    v.wait_ge(sem["mm"], mm_base + mc * 5 + 1 + u + 1)
                    v.tensor_copy(stB(u)[:, mc * 128:(mc + 1) * 128],
                                  pt[u][:, 0:128]).then_inc(sem["tp"], 1)
        ctx.cnt["tp"] = tp_base + 64

        @block.sync
        def _(s):
            s.wait_ge(sem["tp"], tp_base + 64)
            for u in range(4):
                s.dma_start(out=vin_v[ts * 4 + u], in_=stB(u)[:]).then_inc(sem["st"], 16)
        ctx.cnt["st"] += 64
        ctx.barrier()



def emit_l0_attention(ctx, R, dram):
    block, sem = ctx.block, ctx.sem
    pf = [R[f"pf{i}"] for i in range(4)]
    xbf, buf2, scrA = R["xbf"], R["buf2"], R["scrA"]

    kout_v = dram["k_out"].ap().rearrange("(rk mc ts r) (dd t) -> rk mc ts (r dd) t",
                                          rk=2, mc=16, ts=4, r=32, dd=4, t=512)
    vout_v = dram["v_out"].ap().rearrange("(rk pu p) e -> rk pu p e", rk=2, pu=16)
    qtile_v = tiles4(dram["q_t"].ap(), 16)
    gtile_v = tiles4(dram["g_t"].ap(), 16)

    qtile = buf2[:, 0:8192]
    sTb = buf2[:, 8192:8192 + 32 * 512]
    vcol = buf2[:, 8192 + 32 * 512: 8192 + 32 * 512 + 2 * D]
    kb = lambda hh: scrA[:, hh * 4096:(hh + 1) * 4096]

    for p in range(NBK):
        nblk = p + 1
        nslot = 8 * nblk

        q_base = ctx.cnt["qld"]
        kvb = [ctx.cnt["kvA"], ctx.cnt["kvB"]]

        @block.sync
        def _(s):
            if p == 0:
                s.wait_ge(sem["sA"], ctx.cnt["sA"])
                s.wait_ge(sem["sB"], ctx.cnt["sB"])
            s.dma_start(out=qtile.rearrange("p (mc t) -> p mc t", mc=16),
                        in_=qtile_v[:, p].rearrange("mc p t -> p mc t")).then_inc(sem["qld"], 16)
            for it in range(2 * nblk):
                rk, blk = divmod(it, nblk)
                for hh in range(2):
                    if it >= 1:
                        s.wait_ge(sem["mm"], ctx.cnt["mm"] + it * 4)
                    s.dma_start(out=kb(hh).rearrange("p (mc t) -> p mc t", mc=8),
                                in_=kout_v[rk, 8 * hh:8 * hh + 8, blk].rearrange("mc p t -> p mc t")
                                ).then_inc(sem["kvA" if hh == 0 else "kvB"], 16)
        ctx.cnt["qld"] = q_base + 16
        ctx.cnt["kvA"] = kvb[0] + 16 * 2 * nblk
        ctx.cnt["kvB"] = kvb[1] + 16 * 2 * nblk

        mm_base = ctx.cnt["mm"]
        ev_base = ctx.cnt["ev"]

        @block.tensor
        def _(te):
            te.wait_ge(sem["qld"], q_base + 16)
            for it in range(2 * nblk):
                for hh in range(2):
                    te.wait_ge(sem["kvA" if hh == 0 else "kvB"],
                               kvb[hh] + 16 * (it + 1))
                    for u in range(4):
                        if it * 4 + u >= 4:
                            te.wait_ge(sem["ev"], ev_base + it * 4 + u - 3)
                        for kc in range(8):
                            last = te.matmul(pf[u][:, 0:512],
                                             kb(hh)[:, kc * 512 + u * 128: kc * 512 + u * 128 + 128],
                                             qtile[:, (8 * hh + kc) * 512:(8 * hh + kc) * 512 + 512],
                                             start=(hh == 0 and kc == 0), stop=(hh == 1 and kc == 7))
                        if hh == 1:
                            last.then_inc(sem["mm"], 1)
        ctx.cnt["mm"] = mm_base + 2 * nblk * 4

        @block.vector
        def _(v):
            for it in range(2 * nblk):
                rk, blk = divmod(it, nblk)
                for u in range(4):
                    slot = rk * 4 * nblk + blk * 4 + u
                    v.wait_ge(sem["mm"], mm_base + it * 4 + u + 1)
                    if blk == nblk - 1:
                        msk = R["mska"] if rk == 0 else R["mskb"]
                        v.tensor_tensor(sTb[:, slot * 512:(slot + 1) * 512], pf[u][:, 0:512],
                                        msk[:, u * 512:(u + 1) * 512], OP.mult).then_inc(sem["ev"], 1)
                    else:
                        v.tensor_copy(sTb[:, slot * 512:(slot + 1) * 512],
                                      pf[u][:, 0:512]).then_inc(sem["ev"], 1)
        ctx.cnt["ev"] += 2 * nblk * 4
        ctx.barrier()

        for gi in range(4):
            ecs = [gi * 4 + j for j in range(4)]
            vb = [ctx.cnt["vA"], ctx.cnt["vB"]]
            mmb = ctx.cnt["mm"]
            gldb = ctx.cnt["gld"]
            dvb = ctx.cnt["dv"]

            @block.sync
            def _(s):
                for j, ec in enumerate(ecs):
                    s.dma_start(out=R["gbuf"][:, j * 512:(j + 1) * 512],
                                in_=gtile_v[ec, p]).then_inc(sem["gld"], 16)
                for sl in range(nslot):
                    if sl >= 2:
                        s.wait_ge(sem["mm"], mmb + sl - 1)
                    rk, pu = divmod(sl, 4 * nblk)
                    s.dma_start(out=vcol[:, (sl % 2) * D:(sl % 2) * D + D],
                                in_=vout_v[rk, pu]).then_inc(sem["vA" if sl % 2 == 0 else "vB"], 16)
            ctx.cnt["gld"] = gldb + 64
            ctx.cnt["vA"] = vb[0] + 16 * ((nslot + 1) // 2)
            ctx.cnt["vB"] = vb[1] + 16 * (nslot // 2)

            @block.tensor
            def _(te):
                for sl in range(nslot):
                    te.wait_ge(sem["vA" if sl % 2 == 0 else "vB"],
                               vb[sl % 2] + 16 * (sl // 2 + 1))
                    for j in range(4):
                        last = te.matmul(pf[j][:, 0:512],
                                         vcol[:, (sl % 2) * D + ecs[j] * 128:(sl % 2) * D + ecs[j] * 128 + 128],
                                         sTb[:, sl * 512:(sl + 1) * 512],
                                         start=(sl == 0), stop=(sl == nslot - 1))
                    last.then_inc(sem["mm"], 1)
            ctx.cnt["mm"] = mmb + nslot

            @block.vector
            def _(v):
                v.wait_ge(sem["mm"], mmb + nslot)
                v.wait_ge(sem["gld"], gldb + 64)
                for j, ec in enumerate(ecs):
                    v.tensor_tensor(xbf[:, ec * T + p * 512: ec * T + (p + 1) * 512],
                                    pf[j][:, 0:512], R["gbuf"][:, j * 512:(j + 1) * 512],
                                    OP.mult).then_inc(sem["dv"], 1)
            ctx.cnt["dv"] = dvb + 4
            ctx.barrier()



def emit_outproj_residual(ctx, R, W_ap, src_sb, x_in_ap, x_out_ap, Mc=16):
    block, sem = ctx.block, ctx.sem
    pf = [R[f"pf{i}"] for i in range(4)]
    wbuf = R["scrA"][:, 0:4096]
    ocol = R["ocol"]
    src = lambda kc, ts: src_sb[:, kc * T + ts * 512: kc * T + ts * 512 + 512]
    xin_t = x_in_ap.rearrange("(mc p) (ts t) -> mc ts p t", p=128, ts=4)
    xout_t = x_out_ap.rearrange("(mc p) (ts t) -> mc ts p t", p=128, ts=4)
    sb_ = [ctx.cnt["sA"], ctx.cnt["sB"]]
    xb_ = [ctx.cnt["xsA"], ctx.cnt["xsB"]]
    dv_base = ctx.cnt["dv"]

    @block.sync
    def _(s):
        for idx in range(2):
            mc, ts = divmod(idx, 4)
            s.dma_start(out=ocol[:, (idx % 2) * 512:(idx % 2) * 512 + 512],
                        in_=xin_t[mc, ts]).then_inc(sem["xsA" if idx % 2 == 0 else "xsB"], 16)
    ctx.cnt["xsA"] = xb_[0] + 16
    ctx.cnt["xsB"] = xb_[1] + 16

    def evac(mc, ts, bank, mm_tgt):
        idx = mc * 4 + ts
        xp = "xsA" if idx % 2 == 0 else "xsB"
        sp = "sA" if idx % 2 == 0 else "sB"

        @block.vector
        def _(v):
            v.wait_ge(sem["mm"], mm_tgt)
            v.wait_ge(sem[xp], xb_[idx % 2] + 16 * (idx // 2 + 1))
            v.tensor_tensor(ocol[:, (idx % 2) * 512:(idx % 2) * 512 + 512],
                            bank[:, 0:512], ocol[:, (idx % 2) * 512:(idx % 2) * 512 + 512],
                            OP.add).then_inc(sem["dv"], 1)
        ctx.cnt["dv"] += 1

        @block.sync
        def _(s):
            s.wait_ge(sem["dv"], dv_base + idx + 1)
            s.dma_start(out=xout_t[mc, ts],
                        in_=ocol[:, (idx % 2) * 512:(idx % 2) * 512 + 512]).then_inc(sem[sp], 16)
            if idx + 2 < Mc * 4:
                mc2, ts2 = divmod(idx + 2, 4)
                s.wait_ge(sem[sp], sb_[idx % 2] + 16 * (idx // 2 + 1))
                s.dma_start(out=ocol[:, (idx % 2) * 512:(idx % 2) * 512 + 512],
                            in_=xin_t[mc2, ts2]).then_inc(sem[xp], 16)
        ctx.cnt[sp] += 16
        if idx + 2 < Mc * 4:
            ctx.cnt[xp] += 16

    emit_proj(ctx, W_ap, Mc, src, evac, wbuf, pf, ev_name="dv")

    @block.sync
    def _(s):
        s.wait_ge(sem["sA"], ctx.cnt["sA"])
        s.wait_ge(sem["sB"], ctx.cnt["sB"])



def emit_l0_attn_block(ctx, R, io, dram, upto=99):
    sem = ctx.sem
    block = ctx.block
    xbf = R["xbf"]

    @block.gpsimd
    def _(g):
        g.dma_start(out=xbf[:].rearrange("p (kc t) -> p kc t", kc=KC),
                    in_=io["x_in"].rearrange("(kc p) t -> p kc t", p=128)).then_inc(sem["xs"], 16)
    ctx.cnt["xs"] += 16

    @block.tensor
    def _(te):
        te.wait_ge(sem["xs"], ctx.cnt["xs"])
        te.wait_ge(sem["aux"], ctx.cnt["aux"])
        te.wait_ge(sem["ax2"], ctx.cnt["ax2"])
    ctx.barrier()

    emit_ln_affine(ctx, R, "ln_rms")
    ctx.barrier()
    if upto <= 1:
        return

    ktile_v = tiles4(dram["k_in"].ap(), 16)
    qtile_v = tiles4(dram["q_t"].ap(), 16)
    gtile_v = tiles4(dram["g_t"].ap(), 16)

    emit_proj_spill(ctx, R, io["wk"], AF.Silu, ktile_v, xbf)
    ctx.barrier()
    if upto <= 2:
        return
    emit_l0_v_pass(ctx, R, io["wv"], dram["v_in"].ap(), xbf)
    if upto <= 3:
        return
    @block.gpsimd
    def _(g):
        g.wait_ge(sem["st"], ctx.cnt["st"])
        g.wait_ge(sem["sA"], ctx.cnt["sA"])
        g.wait_ge(sem["sB"], ctx.cnt["sB"])
        g.collective_compute("AllGather", OP.bypass, replica_groups=PAIRS,
                             ins=[dram["k_in"].ap().opt()], outs=[dram["k_out"].ap().opt()]
                             ).then_inc(sem["cc"])
        g.collective_compute("AllGather", OP.bypass, replica_groups=PAIRS,
                             ins=[dram["v_in"].ap().opt()], outs=[dram["v_out"].ap().opt()]
                             ).then_inc(sem["cc"])
    ctx.cnt["cc"] += 2
    if upto <= 4:
        @block.gpsimd
        def _(g):
            g.wait_ge(sem["cc"], ctx.cnt["cc"])
        return

    emit_proj_spill(ctx, R, io["wq"], AF.Silu, qtile_v, xbf)
    ctx.barrier()
    emit_proj_spill(ctx, R, io["wg"], AF.Sigmoid, gtile_v, xbf)
    ctx.barrier()
    if upto <= 6:
        return

    @block.gpsimd
    def _(g):
        g.wait_ge(sem["cc"], ctx.cnt["cc"])
    ctx.barrier()

    emit_l0_attention(ctx, R, dram)
    if upto <= 7:
        return

    emit_outproj_residual(ctx, R, io["wp"], xbf, io["x_in"], io["x_out"])
    ctx.barrier()



def build_l0_attn(upto=99):
    nc = bass.Bass("TRN2", target_bir_lowering=False)
    io = {}
    io["x_in"] = nc.dram_tensor("x_in", [D, T], F32, kind="ExternalInput").ap()
    io["x_out"] = nc.dram_tensor("x_out", [D, T], F32, kind="ExternalOutput").ap()
    for nm, shp in [("ident", [128, 128]), ("maskA", [4, 128, BS]), ("maskB", [4, 128, BS])]:
        io[nm] = nc.dram_tensor(nm, shp, F32, kind="ExternalInput").ap()
    for nm in ["wq", "wk", "wv", "wg", "wp"]:
        io[nm] = nc.dram_tensor(nm, [16, 128, KC * 128], F32, kind="ExternalInput").ap()
    dram = {
        "k_in": nc.dram_tensor("k_cc_in", [D, T], BF16),
        "k_out": nc.dram_tensor("k_cc_out", [2 * D, T], BF16),
        "v_in": nc.dram_tensor("v_cc_in", [T, D], BF16),
        "v_out": nc.dram_tensor("v_cc_out", [2 * T, D], BF16),
        "q_t": nc.dram_tensor("q_t", [D, T], BF16),
        "g_t": nc.dram_tensor("g_t", [D, T], BF16),
    }
    with contextlib.ExitStack() as stack:
        sems, R = alloc_shared(nc, stack)
        with nc.Block() as block:
            ctx = Ctx(nc, block, sems)
            ctx.dummy = R["constf"][0:1, 2:4]
            emit_const_loads(ctx, R, io)
            emit_l0_attn_block(ctx, R, io, dram, upto=upto)

            @block.sync
            def _(s):
                s.wait_ge(sems["st"], ctx.cnt["st"])
                s.wait_ge(sems["sA"], ctx.cnt["sA"])
                s.wait_ge(sems["sB"], ctx.cnt["sB"])
    return nc


_L0A_CACHE = {}


def run_l0_attn(x_shards, weights):
    if "nc" not in _L0A_CACHE:
        _L0A_CACHE["nc"] = build_l0_attn()
    nc = _L0A_CACHE["nc"]
    wt = {f"w{k}": lhsT_tiles(weights[k]) for k in ["q", "k", "v", "g", "p"]}
    ident = np.eye(128, dtype=np.float32)
    in_maps = []
    for c in range(8):
        r = c % 2
        mA, mB = make_masks(r)
        m = dict(x_in=x_shards[c], ident=ident, maskA=mA, maskB=mB)
        m.update(wt)
        in_maps.append(m)
    res = run_bass_kernel_spmd(nc, in_maps, core_ids=list(range(8)))
    return [res.results[c]["x_out"] for c in range(8)]



def emit_l0_ffn_block(ctx, R, io, dram):
    block, sem = ctx.block, ctx.sem
    xbf, buf2 = R["xbf"], R["buf2"]
    pf = [R[f"pf{i}"] for i in range(4)]
    sg = R["gbuf"]

    @block.gpsimd
    def _(g):
        g.dma_start(out=xbf[:].rearrange("p (kc t) -> p kc t", kc=KC),
                    in_=io["x_in"].rearrange("(kc p) t -> p kc t", p=128)).then_inc(sem["xs"], 16)
    ctx.cnt["xs"] += 16

    @block.tensor
    def _(te):
        te.wait_ge(sem["xs"], ctx.cnt["xs"])
        te.wait_ge(sem["aux"], ctx.cnt["aux"])
        te.wait_ge(sem["ax2"], ctx.cnt["ax2"])
    ctx.barrier()

    emit_ln_affine(ctx, R, "ln")
    ctx.barrier()

    src = lambda kc, ts: xbf[:, kc * T + ts * 512: kc * T + ts * 512 + 512]
    w1buf = lambda h: R["scrA"][:, h * 2048:(h + 1) * 2048]
    w3buf = lambda h: R["scrA"][:, 4096 + h * 2048: 4096 + (h + 1) * 2048]
    hbuf = lambda h: buf2[:, h * 2048:(h + 1) * 2048]
    habuf = lambda h: buf2[:, 4096 + h * 2048: 4096 + (h + 1) * 2048]
    h2 = dram["h2"].ap()

    NB = 64
    mm0 = ctx.cnt["mm"]
    ev0 = ctx.cnt["ev"]
    dv0 = ctx.cnt["dv"]
    wb = [ctx.cnt["wA"], ctx.cnt["wB"]]
    kb_ = [ctx.cnt["wC"], ctx.cnt["wD"]]
    sb_ = [ctx.cnt["sA"], ctx.cnt["sB"]]

    @block.gpsimd
    def _(g):
        for b in range(NB):
            if b >= 2:
                g.wait_ge(sem["mm"], mm0 + (b - 1) * 12)
            g.dma_start(out=w1buf(b % 2), in_=io["w1"][b]).then_inc(
                sem["wA" if b % 2 == 0 else "wB"], 16)
            g.dma_start(out=w3buf(b % 2), in_=io["w3"][b]).then_inc(
                sem["wC" if b % 2 == 0 else "wD"], 16)
    ctx.cnt["wA"] = wb[0] + 16 * 32
    ctx.cnt["wB"] = wb[1] + 16 * 32
    ctx.cnt["wC"] = kb_[0] + 16 * 32
    ctx.cnt["wD"] = kb_[1] + 16 * 32

    @block.tensor
    def _(te):
        for b in range(NB):
            par = "A" if b % 2 == 0 else "B"
            te.wait_ge(sem["w" + par], wb[b % 2] + 16 * (b // 2 + 1))
            for ts in range(4):
                if b >= 1:
                    te.wait_ge(sem["ev"], ev0 + (b - 1) * 8 + 4 + ts + 1)
                for kc in range(KC):
                    last = te.matmul(pf[ts][:, 0:512], w1buf(b % 2)[:, kc * 128:(kc + 1) * 128],
                                     src(kc, ts), start=(kc == 0), stop=(kc == KC - 1))
                last.then_inc(sem["mm"], 1)
            te.wait_ge(sem["wC" if b % 2 == 0 else "wD"], kb_[b % 2] + 16 * (b // 2 + 1))
            for ts in range(4):
                te.wait_ge(sem["ev"], ev0 + b * 8 + ts + 1)
                for kc in range(KC):
                    last = te.matmul(pf[ts][:, 0:512], w3buf(b % 2)[:, kc * 128:(kc + 1) * 128],
                                     src(kc, ts), start=(kc == 0), stop=(kc == KC - 1))
                last.then_inc(sem["mm"], 1)
            for ts in range(4):
                te.wait_ge(sem["dv"], dv0 + b * 4 + ts + 1)
                te.matmul(pf[ts][:, 0:512], R["h128sb"][:],
                          hbuf(b % 2)[:, ts * 512:(ts + 1) * 512],
                          start=True, stop=True).then_inc(sem["mm"], 1)
    ctx.cnt["mm"] = mm0 + NB * 12

    @block.scalar
    def _(sc):
        for b in range(NB):
            for ts in range(4):
                sc.wait_ge(sem["mm"], mm0 + b * 12 + ts + 1)
                if b >= 1:
                    sc.wait_ge(sem["dv"], dv0 + (b - 1) * 4 + ts + 1)
                sc.activation(sg[:, ts * 512:(ts + 1) * 512], pf[ts][:, 0:512],
                              AF.Silu, bias=0.0, scale=1.0).then_inc(sem["ev"], 1)
            for ts in range(4):
                sc.wait_ge(sem["mm"], mm0 + b * 12 + 8 + ts + 1)
                if b >= 2 and ts == 0:
                    sc.wait_ge(sem["s" + ("A" if b % 2 == 0 else "B")],
                               sb_[b % 2] + 16 * (b // 2))
                sc.activation(habuf(b % 2)[:, ts * 512:(ts + 1) * 512], pf[ts][:, 0:512],
                              AF.Copy, bias=0.0, scale=1.0).then_inc(sem["ev"], 1)
    ctx.cnt["ev"] = ev0 + NB * 8

    @block.vector
    def _(v):
        for b in range(NB):
            for ts in range(4):
                v.wait_ge(sem["mm"], mm0 + b * 12 + 4 + ts + 1)
                v.wait_ge(sem["ev"], ev0 + b * 8 + ts + 1)
                v.tensor_tensor(hbuf(b % 2)[:, ts * 512:(ts + 1) * 512], pf[ts][:, 0:512],
                                sg[:, ts * 512:(ts + 1) * 512], OP.mult).then_inc(sem["dv"], 1)
    ctx.cnt["dv"] = dv0 + NB * 4

    @block.sync
    def _(s):
        for b in range(NB):
            s.wait_ge(sem["ev"], ev0 + b * 8 + 8)
            s.dma_start(out=h2[b * 128:(b + 1) * 128, :], in_=habuf(b % 2)[:]
                        ).then_inc(sem["s" + ("A" if b % 2 == 0 else "B")], 16)
    ctx.cnt["sA"] = sb_[0] + 16 * 32
    ctx.cnt["sB"] = sb_[1] + 16 * 32
    ctx.barrier()

    h2v = h2.rearrange("(b i) t -> b i t", b=64)
    xin_t = io["x_in"].rearrange("(mc p) (tg t) -> mc tg p t", p=128, tg=4)
    xout_t = io["x_out"].rearrange("(mc p) (tg t) -> mc tg p t", p=128, tg=4)

    for tg in range(4):
        q0 = ctx.cnt["qld"]
        mmb = ctx.cnt["mm"]
        evb = ctx.cnt["ev"]

        @block.sync
        def _(s):
            if tg == 0:
                s.wait_ge(sem["sA"], ctx.cnt["sA"])
                s.wait_ge(sem["sB"], ctx.cnt["sB"])
            for gp in range(8):
                for half in range(2):
                    s.dma_start(
                        out=buf2[64 * half:64 * half + 64,
                                 gp * 4096:(gp + 1) * 4096].rearrange("p (i t) -> p i t", i=8),
                        in_=h2v[:, gp * 8 + 64 * half: gp * 8 + 64 * half + 8,
                                tg * 512:(tg + 1) * 512],
                    ).then_inc(sem["qld"], 16)
        ctx.cnt["qld"] = q0 + 16 * 16

        @block.tensor
        def _(te):
            te.wait_ge(sem["qld"], q0 + 16 * 16)
            for c in range(64):
                if c >= 4:
                    te.wait_ge(sem["ev"], evb + c - 3)
                te.matmul(pf[c % 4][:, 0:512], R["h64sb"][:],
                          buf2[:, c * 512:(c + 1) * 512],
                          start=True, stop=True).then_inc(sem["mm"], 1)
        ctx.cnt["mm"] = mmb + 64

        @block.scalar
        def _(sc):
            for c in range(64):
                sc.wait_ge(sem["mm"], mmb + c + 1)
                sc.activation(buf2[:, c * 512:(c + 1) * 512], pf[c % 4][:, 0:512],
                              AF.Copy, bias=0.0, scale=FWHT_SCALE).then_inc(sem["ev"], 1)
        ctx.cnt["ev"] = evb + 64
        ctx.barrier()

        wb2 = [ctx.cnt["wA"], ctx.cnt["wB"]]
        mm2 = ctx.cnt["mm"]
        dv2 = ctx.cnt["dv"]
        xb2 = [ctx.cnt["xsA"], ctx.cnt["xsB"]]
        sb2 = [ctx.cnt["sA"], ctx.cnt["sB"]]

        @block.gpsimd
        def _(g):
            for i2 in range(32):
                mc, half = divmod(i2, 2)
                if i2 >= 2:
                    g.wait_ge(sem["mm"], mm2 + i2 - 1)
                g.dma_start(out=R["scrA"][:, (i2 % 2) * 4096:(i2 % 2 + 1) * 4096],
                            in_=io["w2"][mc, half]).then_inc(
                    sem["wA" if i2 % 2 == 0 else "wB"], 16)
        ctx.cnt["wA"] = wb2[0] + 16 * 16
        ctx.cnt["wB"] = wb2[1] + 16 * 16

        @block.sync
        def _(s):
            s.wait_ge(sem["sA"], sb2[0])
            s.wait_ge(sem["sB"], sb2[1])
            for idx in range(2):
                s.dma_start(out=R["ocol"][:, (idx % 2) * 512:(idx % 2) * 512 + 512],
                            in_=xin_t[idx, tg]).then_inc(sem["xsA" if idx % 2 == 0 else "xsB"], 16)
        ctx.cnt["xsA"] = xb2[0] + 16
        ctx.cnt["xsB"] = xb2[1] + 16

        @block.tensor
        def _(te):
            for i2 in range(32):
                mc, half = divmod(i2, 2)
                te.wait_ge(sem["wA" if i2 % 2 == 0 else "wB"], wb2[i2 % 2] + 16 * (i2 // 2 + 1))
                if half == 0 and mc >= 4:
                    te.wait_ge(sem["dv"], dv2 + mc - 3)
                for c2 in range(32):
                    last = te.matmul(pf[mc % 4][:, 0:512],
                                     R["scrA"][:, (i2 % 2) * 4096 + c2 * 128:(i2 % 2) * 4096 + (c2 + 1) * 128],
                                     buf2[:, (half * 32 + c2) * 512:(half * 32 + c2) * 512 + 512],
                                     start=(half == 0 and c2 == 0), stop=(half == 1 and c2 == 31))
                last.then_inc(sem["mm"], 1)
        ctx.cnt["mm"] = mm2 + 32

        for mc in range(16):
            xp = "xsA" if mc % 2 == 0 else "xsB"
            sp = "sA" if mc % 2 == 0 else "sB"

            @block.vector
            def _(v, mc=mc, xp=xp):
                v.wait_ge(sem["mm"], mm2 + 2 * mc + 2)
                v.wait_ge(sem[xp], xb2[mc % 2] + 16 * (mc // 2 + 1))
                v.tensor_tensor(R["ocol"][:, (mc % 2) * 512:(mc % 2) * 512 + 512],
                                pf[mc % 4][:, 0:512],
                                R["ocol"][:, (mc % 2) * 512:(mc % 2) * 512 + 512],
                                OP.add).then_inc(sem["dv"], 1)
            ctx.cnt["dv"] += 1

            @block.sync
            def _(s, mc=mc, xp=xp, sp=sp):
                s.wait_ge(sem["dv"], dv2 + mc + 1)
                s.dma_start(out=xout_t[mc, tg],
                            in_=R["ocol"][:, (mc % 2) * 512:(mc % 2) * 512 + 512]
                            ).then_inc(sem[sp], 16)
                if mc + 2 < 16:
                    s.wait_ge(sem[sp], sb2[mc % 2] + 16 * (mc // 2 + 1))
                    s.dma_start(out=R["ocol"][:, (mc % 2) * 512:(mc % 2) * 512 + 512],
                                in_=xin_t[mc + 2, tg]).then_inc(sem[xp], 16)
            ctx.cnt[sp] += 16
            if mc + 2 < 16:
                ctx.cnt[xp] += 16
        ctx.barrier()


def build_l0_ffn():
    nc = bass.Bass("TRN2", target_bir_lowering=False)
    io = {}
    io["x_in"] = nc.dram_tensor("x_in", [D, T], F32, kind="ExternalInput").ap()
    io["x_out"] = nc.dram_tensor("x_out", [D, T], F32, kind="ExternalOutput").ap()
    io["ident"] = nc.dram_tensor("ident", [128, 128], F32, kind="ExternalInput").ap()
    io["h128"] = nc.dram_tensor("h128", [128, 128], F32, kind="ExternalInput").ap()
    io["h64b"] = nc.dram_tensor("h64b", [128, 128], F32, kind="ExternalInput").ap()
    io["w1"] = nc.dram_tensor("w1", [64, 128, KC * 128], F32, kind="ExternalInput").ap()
    io["w3"] = nc.dram_tensor("w3", [64, 128, KC * 128], F32, kind="ExternalInput").ap()
    io["w2"] = nc.dram_tensor("w2", [16, 2, 128, 32 * 128], F32, kind="ExternalInput").ap()
    dram = {"h2": nc.dram_tensor("h2", [64 * 128, T], BF16)}
    with contextlib.ExitStack() as stack:
        sems, R = alloc_shared(nc, stack, kind="l0f")
        with nc.Block() as block:
            ctx = Ctx(nc, block, sems)
            ctx.dummy = R["constf"][0:1, 2:4]
            emit_const_loads(ctx, R, io, kind="l0f")
            emit_l0_ffn_block(ctx, R, io, dram)

            @block.sync
            def _(s):
                s.wait_ge(sems["sA"], ctx.cnt["sA"])
                s.wait_ge(sems["sB"], ctx.cnt["sB"])
    return nc


def w2_tiles(W2):
    W2 = np.asarray(W2)
    hidx = np.zeros((64, 128), dtype=np.int64)
    for c in range(64):
        p = np.arange(128)
        b = np.where(p < 64, p, p - 64)
        i = np.where(p < 64, c, c + 64)
        hidx[c] = b * 128 + i
    out = np.zeros((16, 2, 128, 32 * 128), np.float32)
    for mc in range(16):
        blk = W2[mc * 128:(mc + 1) * 128]
        for c in range(64):
            t = blk[:, hidx[c]].T
            half, c2 = divmod(c, 32)
            out[mc, half, :, c2 * 128:(c2 + 1) * 128] = t
    return np.ascontiguousarray(out)


def hadamard(n):
    h = np.array([[1.0]], dtype=np.float32)
    while h.shape[0] < n:
        h = np.block([[h, h], [h, -h]])
    return h


_BUILD_CACHE = {}


def run_l0_ffn(x_shards, W1, W2, W3):
    if "l0f" not in _BUILD_CACHE:
        _BUILD_CACHE["l0f"] = build_l0_ffn()
    nc = _BUILD_CACHE["l0f"]
    wt = dict(w1=lhsT_tiles(np.asarray(W1)),
              w3=lhsT_tiles(np.asarray(W3)),
              w2=w2_tiles(W2),
              ident=np.eye(128, dtype=np.float32),
              h128=hadamard(128),
              h64b=np.kron(np.eye(2, dtype=np.float32), hadamard(64)))
    in_maps = []
    for c in range(8):
        m = dict(x_in=x_shards[c])
        m.update(wt)
        in_maps.append(m)
    res = run_bass_kernel_spmd(nc, in_maps, core_ids=list(range(8)))
    return [res.results[c]["x_out"] for c in range(8)]



EXPSHIFT = -20.0
HSCALE = float(1.0 / np.sqrt(128.0))


def emit_l1_qk_pass(ctx, R, Wqk_ap, q_view, k_view):
    block, sem = ctx.block, ctx.sem
    pf = [R[f"pf{i}"] for i in range(4)]
    pt = [R[f"pt{i}"] for i in range(4)]
    xbf, buf2 = R["xbf"], R["buf2"]
    wbuf = R["scrA"][:, 0:4096]
    stage = lambda h: buf2[:, h * 2048:(h + 1) * 2048]
    src = lambda kc, ts: xbf[:, kc * T + ts * 512: kc * T + ts * 512 + 512]
    rsc = R["ocol"]

    mm0 = ctx.cnt["mm"]
    ev0 = ctx.cnt["ev"]
    dv0 = ctx.cnt["dv"]
    wb = [ctx.cnt["wA"], ctx.cnt["wB"]]
    sb_ = [ctx.cnt["sA"], ctx.cnt["sB"]]

    @block.gpsimd
    def _(g):
        for mc in range(32):
            if mc >= 2:
                g.wait_ge(sem["mm"], mm0 + (mc - 1) * 8)
            g.dma_start(out=wbuf[:, (mc % 2) * 2048:(mc % 2 + 1) * 2048],
                        in_=Wqk_ap[mc]).then_inc(sem["wA" if mc % 2 == 0 else "wB"], 16)
    ctx.cnt["wA"] = wb[0] + 16 * 16
    ctx.cnt["wB"] = wb[1] + 16 * 16

    @block.tensor
    def _(te):
        for mc in range(32):
            te.wait_ge(sem["wA" if mc % 2 == 0 else "wB"], wb[mc % 2] + 16 * (mc // 2 + 1))
            for ts in range(4):
                if mc >= 1:
                    te.wait_ge(sem["dv"], dv0 + (mc - 1) * 12 + ts * 3 + 2)
                for kc in range(KC):
                    last = te.matmul(pf[ts][:, 0:512],
                                     wbuf[:, (mc % 2) * 2048 + kc * 128:(mc % 2) * 2048 + (kc + 1) * 128],
                                     src(kc, ts), start=(kc == 0), stop=(kc == KC - 1))
                last.then_inc(sem["mm"], 1)
            for ts in range(4):
                te.wait_ge(sem["ev"], ev0 + mc * 4 + ts + 1)
                te.matmul(pf[ts][0:32, 0:512], R["permSB"][0:32, 0:32],
                          stage(mc % 2)[0:32, ts * 512:(ts + 1) * 512],
                          start=True, stop=True).then_inc(sem["mm"], 1)
    ctx.cnt["mm"] = mm0 + 32 * 8

    @block.scalar
    def _(sc):
        for mc in range(32):
            for ts in range(4):
                sc.wait_ge(sem["mm"], mm0 + mc * 8 + ts + 1)
                if mc >= 2 and ts == 0:
                    sc.wait_ge(sem["s" + ("A" if mc % 2 == 0 else "B")],
                               sb_[mc % 2] + 16 * (mc // 2))
                sc.activation(stage(mc % 2)[:, ts * 512:(ts + 1) * 512], pf[ts][:, 0:512],
                              AF.Copy, bias=0.0, scale=1.0).then_inc(sem["ev"], 1)
    ctx.cnt["ev"] = ev0 + 32 * 4

    @block.vector
    def _(v):
        for mc in range(32):
            for ts in range(4):
                v.wait_ge(sem["mm"], mm0 + mc * 8 + 4 + ts + 1)
                v.wait_ge(sem["dv"], ctx.cnt["dv"])
                v.tensor_tensor(rsc[0:32, 0:512], stage(mc % 2)[0:32, ts * 512:(ts + 1) * 512],
                                R["cosSB"][:, ts * 512:(ts + 1) * 512], OP.mult).then_inc(sem["dv"], 1)
                ctx.cnt["dv"] += 1
                v.tensor_tensor(rsc[0:32, 512:1024], pf[ts][0:32, 0:512],
                                R["sinSB"][:, ts * 512:(ts + 1) * 512], OP.mult).then_inc(sem["dv"], 1)
                ctx.cnt["dv"] += 1
                v.wait_ge(sem["dv"], ctx.cnt["dv"])
                v.tensor_tensor(stage(mc % 2)[0:32, ts * 512:(ts + 1) * 512],
                                rsc[0:32, 0:512], rsc[0:32, 512:1024], OP.add).then_inc(sem["dv"], 1)
                ctx.cnt["dv"] += 1
    assert ctx.cnt["dv"] == dv0 + 32 * 12

    @block.sync
    def _(s):
        for mc in range(32):
            s.wait_ge(sem["dv"], dv0 + mc * 12 + 12)
            dst = q_view[mc] if mc < 16 else k_view[mc - 16]
            s.dma_start(out=dst.rearrange("ts p t -> p ts t"),
                        in_=stage(mc % 2)[:].rearrange("p (ts t) -> p ts t", ts=4)
                        ).then_inc(sem["s" + ("A" if mc % 2 == 0 else "B")], 16)
    ctx.cnt["sA"] = sb_[0] + 16 * 16
    ctx.cnt["sB"] = sb_[1] + 16 * 16


def emit_l1_attention(ctx, R, dram):
    block, sem = ctx.block, ctx.sem
    pf = [R[f"pf{i}"] for i in range(4)]
    xbf, buf2, scrA = R["xbf"], R["buf2"], R["scrA"]

    kout_v = dram["k_out"].ap().rearrange("(rk mc ts r) (dd t) -> rk mc ts (r dd) t",
                                          rk=2, mc=16, ts=4, r=32, dd=4, t=512)
    vout_v = dram["v_out"].ap().rearrange("(rk pu p) e -> rk pu p e", rk=2, pu=16)
    qtile_v = tiles4(dram["q_t"].ap(), 16)

    qtile = buf2[:, 0:8192]
    pT = buf2[:, 8192:8192 + 32 * 512]
    vcol = lambda h: buf2[:, 24576 + (h % 2) * 4096: 24576 + (h % 2) * 4096 + 4096]
    kb = lambda h: scrA[:, (h % 2) * 4096:((h % 2) + 1) * 4096]
    rlbc = R["gbuf"][:, 0:512]
    rlrow = R["rowbf"][0:1, 0:512]

    for p in range(NBK):
        nblk = p + 1
        nslot = 8 * nblk
        q0 = ctx.cnt["qld"]

        @block.sync
        def _(s):
            if p == 0:
                s.wait_ge(sem["sA"], ctx.cnt["sA"])
                s.wait_ge(sem["sB"], ctx.cnt["sB"])
                s.wait_ge(sem["st"], ctx.cnt["st"])
            s.dma_start(out=qtile.rearrange("p (mc t) -> p mc t", mc=16),
                        in_=qtile_v[:, p].rearrange("mc p t -> p mc t")).then_inc(sem["qld"], 16)
        ctx.cnt["qld"] = q0 + 16

        for h in range(NH):
            kvp = "kvA" if h % 2 == 0 else "kvB"
            vp = "vA" if h % 2 == 0 else "vB"
            kvb = ctx.cnt[kvp]
            vb = ctx.cnt[vp]
            mmb = ctx.cnt["mm"]
            evb = ctx.cnt["ev"]
            dvb = ctx.cnt["dv"]

            @block.sync
            def _(s, h=h, p=p, kvp=kvp, vp=vp, mmb=mmb):
                if h >= 2 or p > 0:
                    s.wait_ge(sem["mm"], mmb)
                for rk in range(2):
                    s.dma_start(out=kb(h)[:, rk * nblk * 512:(rk + 1) * nblk * 512
                                          ].rearrange("p (b t) -> p b t", b=nblk),
                                in_=kout_v[rk, h, 0:nblk].rearrange("b p t -> p b t")
                                ).then_inc(sem[kvp], 16)
                for rk in range(2):
                    for bh in range(2):
                        n1 = (nblk * 4) // 2 if nblk * 4 >= 2 else nblk * 4
                        lo = bh * n1
                        hi = nblk * 4 if bh == 1 else n1
                        if lo >= hi:
                            continue
                        s.dma_start(
                            out=vcol(h)[:, (rk * 4 * nblk + lo) * 128:(rk * 4 * nblk + hi) * 128
                                        ].rearrange("p (sl e) -> p sl e", e=128),
                            in_=vout_v[rk, lo:hi, :, h * 128:(h + 1) * 128
                                       ].rearrange("sl p e -> p sl e"),
                        ).then_inc(sem[vp], 16)

            nvd = sum(1 for rk in range(2) for bh in range(2)
                      if (bh * ((nblk * 4) // 2 if nblk * 4 >= 2 else nblk * 4)) <
                      (nblk * 4 if bh == 1 else ((nblk * 4) // 2 if nblk * 4 >= 2 else nblk * 4)))
            ctx.cnt[kvp] = kvb + 32
            ctx.cnt[vp] = vb + 16 * nvd

            @block.tensor
            def _(te, h=h, p=p, nslot=nslot, kvb=kvb, evb=evb, mmb=mmb, q0=q0):
                te.wait_ge(sem["qld"], q0 + 16)
                te.wait_ge(sem[kvp], kvb + 32)
                for sl in range(nslot):
                    if sl >= 2:
                        te.wait_ge(sem["ev"], evb + sl - 1)
                    te.matmul(pf[sl % 2][:, 0:512], kb(h)[:, sl * 128:(sl + 1) * 128],
                              qtile[:, h * 512:(h + 1) * 512],
                              start=True, stop=True).then_inc(sem["mm"], 1)
            ctx.cnt["mm"] = mmb + nslot

            @block.scalar
            def _(sc, nslot=nslot, mmb=mmb, evb=evb):
                for sl in range(nslot):
                    sc.wait_ge(sem["mm"], mmb + sl + 1)
                    sc.activation(pT[:, sl * 512:(sl + 1) * 512], pf[sl % 2][:, 0:512],
                                  AF.Exp, bias=R["constf"][0:128, 4:5], scale=HSCALE
                                  ).then_inc(sem["ev"], 1)
            ctx.cnt["ev"] = evb + nslot

            @block.vector
            def _(v, nblk=nblk, evb=evb, nslot=nslot):
                v.wait_ge(sem["ev"], evb + nslot)
                for rk in range(2):
                    msk = R["mska"] if rk == 0 else R["mskb"]
                    for u in range(4):
                        slot = rk * 4 * nblk + (nblk - 1) * 4 + u
                        v.tensor_tensor(pT[:, slot * 512:(slot + 1) * 512],
                                        pT[:, slot * 512:(slot + 1) * 512],
                                        msk[:, u * 512:(u + 1) * 512], OP.mult
                                        ).then_inc(sem["dv"], 1)
            ctx.cnt["dv"] = dvb + 8

            mml = ctx.cnt["mm"]

            @block.tensor
            def _(te, nslot=nslot, dvb=dvb, mml=mml):
                te.wait_ge(sem["dv"], dvb + 8)
                te.wait_ge(sem["ev"], ctx.cnt["ev"])
                for sl in range(nslot):
                    last = te.matmul(pf[2][0:1, 0:512], R["onec"][:],
                                     pT[:, sl * 512:(sl + 1) * 512],
                                     start=(sl == 0), stop=(sl == nslot - 1))
                last.then_inc(sem["mm"], 1)
            ctx.cnt["mm"] = mml + 1

            dvl = ctx.cnt["dv"]

            @block.vector
            def _(v, mml=mml, dvl=dvl):
                v.wait_ge(sem["mm"], mml + 1)
                v.reciprocal(R["statf"][0:1, 0:512], pf[2][0:1, 0:512]).then_inc(sem["dv"], 1)
                v.wait_ge(sem["dv"], dvl + 1)
                v.tensor_copy(rlrow, R["statf"][0:1, 0:512]).then_inc(sem["dv"], 1)
            ctx.cnt["dv"] = dvl + 2

            mmr = ctx.cnt["mm"]

            @block.tensor
            def _(te, dvl=dvl, mmr=mmr):
                te.wait_ge(sem["dv"], dvl + 2)
                te.matmul(pf[2][:, 0:512], R["oner"][:], rlrow,
                          start=True, stop=True).then_inc(sem["mm"], 1)
            ctx.cnt["mm"] = mmr + 1

            evr = ctx.cnt["ev"]

            @block.scalar
            def _(sc, mmr=mmr):
                sc.wait_ge(sem["mm"], mmr + 1)
                sc.activation(rlbc, pf[2][:, 0:512], AF.Copy, bias=0.0, scale=1.0
                              ).then_inc(sem["ev"], 1)
            ctx.cnt["ev"] = evr + 1

            mma = ctx.cnt["mm"]

            @block.tensor
            def _(te, nslot=nslot, vb=vb, mma=mma, h=h, nvd=nvd):
                te.wait_ge(sem[vp], vb + 16 * nvd)
                for sl in range(nslot):
                    last = te.matmul(pf[3][:, 0:512], vcol(h)[:, sl * 128:(sl + 1) * 128],
                                     pT[:, sl * 512:(sl + 1) * 512],
                                     start=(sl == 0), stop=(sl == nslot - 1))
                last.then_inc(sem["mm"], 1)
            ctx.cnt["mm"] = mma + 1

            @block.vector
            def _(v, mma=mma, evr=evr, h=h, p=p):
                v.wait_ge(sem["mm"], mma + 1)
                v.wait_ge(sem["ev"], evr + 1)
                v.tensor_tensor(xbf[:, h * T + p * 512: h * T + (p + 1) * 512],
                                pf[3][:, 0:512], rlbc, OP.mult).then_inc(sem["dv"], 1)
            ctx.cnt["dv"] += 1
        ctx.barrier()


def emit_l1_attn_block(ctx, R, io, dram):
    block, sem = ctx.block, ctx.sem
    xbf = R["xbf"]

    @block.gpsimd
    def _(g):
        g.dma_start(out=xbf[:].rearrange("p (kc t) -> p kc t", kc=KC),
                    in_=io["x_in"].rearrange("(kc p) t -> p kc t", p=128)).then_inc(sem["xs"], 16)
    ctx.cnt["xs"] += 16

    @block.tensor
    def _(te):
        te.wait_ge(sem["xs"], ctx.cnt["xs"])
        te.wait_ge(sem["aux"], ctx.cnt["aux"])
        te.wait_ge(sem["ax2"], ctx.cnt["ax2"])
    ctx.barrier()

    emit_ln_affine(ctx, R, "ln")
    ctx.barrier()

    qt4 = tiles4(dram["q_t"].ap(), 16)
    kt4 = tiles4(dram["k_in"].ap(), 16)
    emit_l1_qk_pass(ctx, R, io["wqk"], qt4, kt4)
    ctx.barrier()
    emit_l0_v_pass(ctx, R, io["wv"], dram["v_in"].ap(), xbf, act_fn=AF.Copy)

    @block.gpsimd
    def _(g):
        g.wait_ge(sem["st"], ctx.cnt["st"])
        g.wait_ge(sem["sA"], ctx.cnt["sA"])
        g.wait_ge(sem["sB"], ctx.cnt["sB"])
        g.collective_compute("AllGather", OP.bypass, replica_groups=PAIRS,
                             ins=[dram["k_in"].ap().opt()], outs=[dram["k_out"].ap().opt()]
                             ).then_inc(sem["cc"])
        g.collective_compute("AllGather", OP.bypass, replica_groups=PAIRS,
                             ins=[dram["v_in"].ap().opt()], outs=[dram["v_out"].ap().opt()]
                             ).then_inc(sem["cc"])
        g.wait_ge(sem["cc"], ctx.cnt["cc"] + 2)
    ctx.cnt["cc"] += 2
    ctx.barrier()

    emit_l1_attention(ctx, R, dram)
    emit_outproj_residual(ctx, R, io["wout"], xbf, io["x_in"], io["x_out"])
    ctx.barrier()


def build_l1_attn():
    nc = bass.Bass("TRN2", target_bir_lowering=False)
    io = {}
    io["x_in"] = nc.dram_tensor("x_in", [D, T], F32, kind="ExternalInput").ap()
    io["x_out"] = nc.dram_tensor("x_out", [D, T], F32, kind="ExternalOutput").ap()
    for nm, shp in [("ident", [128, 128]), ("maskA", [4, 128, BS]), ("maskB", [4, 128, BS]),
                    ("cosT", [32, T]), ("sinT", [32, T]), ("perm32", [128, 128])]:
        io[nm] = nc.dram_tensor(nm, shp, F32, kind="ExternalInput").ap()
    io["wqk"] = nc.dram_tensor("wqk", [32, 128, KC * 128], F32, kind="ExternalInput").ap()
    io["wv"] = nc.dram_tensor("wv", [16, 128, KC * 128], F32, kind="ExternalInput").ap()
    io["wout"] = nc.dram_tensor("wout", [16, 128, KC * 128], F32, kind="ExternalInput").ap()
    dram = {
        "k_in": nc.dram_tensor("k_cc_in", [D, T], BF16),
        "k_out": nc.dram_tensor("k_cc_out", [2 * D, T], BF16),
        "v_in": nc.dram_tensor("v_cc_in", [T, D], BF16),
        "v_out": nc.dram_tensor("v_cc_out", [2 * T, D], BF16),
        "q_t": nc.dram_tensor("q_t", [D, T], BF16),
    }
    with contextlib.ExitStack() as stack:
        sems, R = alloc_shared(nc, stack, kind="l1a")
        with nc.Block() as block:
            ctx = Ctx(nc, block, sems)
            ctx.dummy = R["constf"][0:1, 2:4]

            @block.gpsimd
            def _(g):
                g.memset(R["constf"][:, 4:5], EXPSHIFT).then_inc(sems["ax2"], 1)
            ctx.cnt["ax2"] += 1

            emit_const_loads(ctx, R, io, kind="l1a")
            emit_l1_attn_block(ctx, R, io, dram)

            @block.sync
            def _(s):
                s.wait_ge(sems["sA"], ctx.cnt["sA"])
                s.wait_ge(sems["sB"], ctx.cnt["sB"])
    return nc


def rot_tables(r):
    ROT = 32
    inv_freq = 1.0 / (10000.0 ** (np.arange(0, ROT, 2, dtype=np.float64) / ROT))
    pos_all = np.arange(4096, dtype=np.float64)[:, None] * inv_freq[None, :]
    pos_all = np.concatenate([pos_all, pos_all], axis=1)
    loc = np.concatenate([np.arange(g * BS, (g + 1) * BS) for g in core_blocks(r)])
    pos = pos_all[loc]
    return (np.ascontiguousarray(np.cos(pos).T.astype(np.float32)),
            np.ascontiguousarray(np.sin(pos).T.astype(np.float32)))


def perm32_mat():
    P = np.zeros((32, 32), dtype=np.float32)
    for j in range(16):
        P[j, j + 16] = -1.0
        P[j + 16, j] = 1.0
    L = np.zeros((128, 128), dtype=np.float32)
    L[0:32, 0:32] = P.T
    return L


def run_l1_attn(x_shards, w_qkv, w_out):
    if "l1a" not in _BUILD_CACHE:
        _BUILD_CACHE["l1a"] = build_l1_attn()
    nc = _BUILD_CACHE["l1a"]
    qkv_t = lhsT_tiles(np.asarray(w_qkv))
    wt = dict(wqk=np.ascontiguousarray(qkv_t[0:32]),
              wv=np.ascontiguousarray(qkv_t[32:48]),
              wout=lhsT_tiles(np.asarray(w_out)),
              ident=np.eye(128, dtype=np.float32),
              perm32=perm32_mat())
    in_maps = []
    for c in range(8):
        r = c % 2
        mA, mB = make_masks(r)
        cosT, sinT = rot_tables(r)
        m = dict(x_in=x_shards[c], maskA=mA, maskB=mB, cosT=cosT, sinT=sinT)
        m.update(wt)
        in_maps.append(m)
    res = run_bass_kernel_spmd(nc, in_maps, core_ids=list(range(8)))
    return [res.results[c]["x_out"] for c in range(8)]



def emit_l1_ffn_block(ctx, R, io, dram):
    block, sem = ctx.block, ctx.sem
    xbf, buf2, scrA = R["xbf"], R["buf2"], R["scrA"]
    pf = [R[f"pf{i}"] for i in range(4)]
    pt = [R[f"pt{i}"] for i in range(4)]
    statf = R["statf"]
    S = lambda i: statf[0:1, i * 2048:(i + 1) * 2048]

    @block.gpsimd
    def _(g):
        g.dma_start(out=xbf[:].rearrange("p (kc t) -> p kc t", kc=KC),
                    in_=io["x_in"].rearrange("(kc p) t -> p kc t", p=128)).then_inc(sem["xs"], 16)
        g.dma_start(out=R["gcol"][:, 0:16], in_=io["gamma"].rearrange("(kc p) -> p kc", p=128),
                    allow_slow_non_contiguous=True).then_inc(sem["xs"], 16)
        g.dma_start(out=R["gcol"][:, 16:32], in_=io["beta"].rearrange("(kc p) -> p kc", p=128),
                    allow_slow_non_contiguous=True).then_inc(sem["xs"], 16)
        g.dma_start(out=R["srow"][0:1, 0:1], in_=io["scale"].rearrange("(one a) -> one a", one=1)
                    ).then_inc(sem["xs"], 16)
    ctx.cnt["xs"] += 64

    @block.tensor
    def _(te):
        te.wait_ge(sem["xs"], ctx.cnt["xs"])
        te.wait_ge(sem["aux"], ctx.cnt["aux"])
        te.wait_ge(sem["ax2"], ctx.cnt["ax2"])
    ctx.barrier()

    emit_ln_affine(ctx, R, "ln_ln")

    @block.vector
    def _(v):
        v.wait_ge(sem["dv"], ctx.cnt["dv"])
        for kc in range(KC):
            v.tensor_scalar(xbf[:, kc * T:(kc + 1) * T], xbf[:, kc * T:(kc + 1) * T],
                            R["gcol"][:, kc:kc + 1], R["gcol"][:, 16 + kc:17 + kc],
                            OP.mult, OP.add).then_inc(sem["dv"], 1)
            ctx.cnt["dv"] += 1
    ctx.barrier()

    mm0 = ctx.cnt["mm"]
    dv0 = ctx.cnt["dv"]
    sqb = buf2

    @block.scalar
    def _(sc):
        for kc in range(KC):
            if kc >= 2:
                sc.wait_ge(sem["mm"], mm0 + kc - 1)
            sc.activation(sqb[:, (kc % 2) * T:(kc % 2 + 1) * T], xbf[:, kc * T:(kc + 1) * T],
                          AF.Square, bias=0.0, scale=1.0).then_inc(sem["dv"], 1)
    ctx.cnt["dv"] = dv0 + KC

    @block.tensor
    def _(te):
        for kc in range(KC):
            te.wait_ge(sem["dv"], dv0 + kc + 1)
            for ts in range(4):
                last = te.matmul(pf[ts][0:1, 0:512], R["onec"][:],
                                 sqb[:, (kc % 2) * T + ts * 512:(kc % 2) * T + ts * 512 + 512],
                                 start=(kc == 0), stop=(kc == KC - 1))
            last.then_inc(sem["mm"], 1)
    ctx.cnt["mm"] = mm0 + KC

    ev0 = ctx.cnt["ev"]

    @block.scalar
    def _(sc):
        sc.wait_ge(sem["mm"], mm0 + KC)
        for ts in range(4):
            last = sc.activation(S(0)[:, ts * 512:(ts + 1) * 512], pf[ts][0:1, 0:512],
                                 AF.Copy, bias=0.0, scale=1.0)
        last.then_inc(sem["ev"], 1)
    ctx.cnt["ev"] = ev0 + 1

    def vstep(op, *args):
        @block.vector
        def _(v):
            v.wait_ge(sem["dv"], ctx.cnt["dv"])
            getattr(v, op)(*args).then_inc(sem["dv"], 1)
        ctx.cnt["dv"] += 1

    @block.scalar
    def _(sc):
        sc.wait_ge(sem["ev"], ev0 + 1)
        sc.activation(S(0), S(0), AF.Sqrt, bias=0.0, scale=1.0).then_inc(sem["dv"], 1)
        ctx.cnt["dv"] += 1
        sc.wait_ge(sem["dv"], ctx.cnt["dv"])
        sc.activation(R["srow"][0:1, 1:2], R["srow"][0:1, 0:1], AF.Abs,
                      bias=0.0, scale=1.0).then_inc(sem["dv"], 1)
        ctx.cnt["dv"] += 1
        sc.wait_ge(sem["dv"], ctx.cnt["dv"])
        sc.activation(R["srow"][0:1, 2:3], R["srow"][0:1, 1:2], AF.Copy,
                      bias=0.0, scale=-1.0).then_inc(sem["dv"], 1)
        ctx.cnt["dv"] += 1
    vstep("tensor_scalar_max", S(0), S(0), 1e-12)
    vstep("reciprocal", S(0), S(0))
    @block.vector
    def _(v):
        v.wait_ge(sem["dv"], ctx.cnt["dv"])
        v.tensor_scalar(S(0), S(0), R["srow"][0:1, 0:1], None, OP.mult).then_inc(sem["dv"], 1)
        ctx.cnt["dv"] += 1

    mmc = ctx.cnt["mm"]
    dvc = ctx.cnt["dv"]
    vstep("tensor_copy", R["rowbf"][0:1, 0:2048], S(0))

    evc = ctx.cnt["ev"]
    mmc = ctx.cnt["mm"]

    @block.tensor
    def _(te):
        te.wait_ge(sem["dv"], ctx.cnt["dv"])
        for i in range(16):
            if i >= 4:
                te.wait_ge(sem["ev"], evc + i - 3)
            te.transpose(pt[i % 4][0:128, 0:1], R["rowbf"][0:1, i * 128:(i + 1) * 128],
                         R["identb"][0:1, 0:1]).then_inc(sem["mm"], 1)
    ctx.cnt["mm"] = mmc + 16

    @block.scalar
    def _(sc):
        for i in range(16):
            sc.wait_ge(sem["mm"], mmc + i + 1)
            sc.activation(R["ccol"][:, i:i + 1], pt[i % 4][0:128, 0:1], AF.Copy,
                          bias=0.0, scale=1.0).then_inc(sem["ev"], 1)
    ctx.cnt["ev"] = evc + 16

    @block.vector
    def _(v):
        v.wait_ge(sem["dv"], ctx.cnt["dv"])
        v.tensor_copy(R["rowbf"][0:1, 2048:2049], R["srow"][0:1, 2:3]).then_inc(sem["dv"], 1)
        ctx.cnt["dv"] += 1
    mmq = ctx.cnt["mm"]

    @block.tensor
    def _(te):
        te.wait_ge(sem["dv"], ctx.cnt["dv"])
        te.matmul(pf[0][:, 0:1], R["oner"][:], R["rowbf"][0:1, 2048:2049],
                  start=True, stop=True).then_inc(sem["mm"], 1)
    ctx.cnt["mm"] = mmq + 1

    @block.scalar
    def _(sc):
        sc.wait_ge(sem["mm"], mmq + 1)
        sc.activation(R["nsb"][:, 0:1], pf[0][:, 0:1], AF.Copy, bias=0.0, scale=1.0
                      ).then_inc(sem["ev"], 1)
    ctx.cnt["ev"] += 1
    ctx.barrier()

    muv = io["mu_t"]
    mmn = ctx.cnt["mm"]
    dvn = ctx.cnt["dv"]
    evn = ctx.cnt["ev"]
    wbn = [ctx.cnt["wA"], ctx.cnt["wB"]]

    @block.gpsimd
    def _(g):
        for esc in range(16):
            if esc >= 2:
                g.wait_ge(sem["dv"], dvn + (esc - 1))
            g.dma_start(out=scrA[:, (esc % 2) * 2048:(esc % 2 + 1) * 2048],
                        in_=muv[esc]).then_inc(sem["wA" if esc % 2 == 0 else "wB"], 16)
    ctx.cnt["wA"] = wbn[0] + 16 * 8
    ctx.cnt["wB"] = wbn[1] + 16 * 8

    @block.vector
    def _(v):
        for esc in range(16):
            v.wait_ge(sem["wA" if esc % 2 == 0 else "wB"], wbn[esc % 2] + 16 * (esc // 2 + 1))
            if esc >= 2:
                v.wait_ge(sem["mm"], mmn + esc - 1)
            v.tensor_tensor(sqb[:, (esc % 2) * T:(esc % 2) * T + 2048],
                            scrA[:, (esc % 2) * 2048:(esc % 2 + 1) * 2048],
                            scrA[:, (esc % 2) * 2048:(esc % 2 + 1) * 2048],
                            OP.mult).then_inc(sem["dv"], 1)
    ctx.cnt["dv"] = dvn + 16

    @block.tensor
    def _(te):
        for esc in range(16):
            te.wait_ge(sem["dv"], dvn + esc + 1)
            if esc >= 2:
                te.wait_ge(sem["ev"], evn + esc - 1)
            for kc in range(KC):
                last = te.matmul(pf[esc % 2][0:1, 0:128], R["onec"][:],
                                 sqb[:, (esc % 2) * T + kc * 128:(esc % 2) * T + (kc + 1) * 128],
                                 start=(kc == 0), stop=(kc == KC - 1))
            last.then_inc(sem["mm"], 1)
    ctx.cnt["mm"] = mmn + 16

    @block.scalar
    def _(sc):
        for esc in range(16):
            sc.wait_ge(sem["mm"], mmn + esc + 1)
            sc.activation(S(1)[:, esc * 128:(esc + 1) * 128], pf[esc % 2][0:1, 0:128],
                          AF.Copy, bias=0.0, scale=1.0).then_inc(sem["ev"], 1)
    ctx.cnt["ev"] = evn + 16

    @block.scalar
    def _(sc):
        sc.wait_ge(sem["ev"], evn + 16)
        sc.wait_ge(sem["dv"], ctx.cnt["dv"])
        sc.activation(S(1), S(1), AF.Sqrt, bias=0.0, scale=1.0).then_inc(sem["dv"], 1)
        ctx.cnt["dv"] += 1
    vstep("tensor_scalar_max", S(1), S(1), 1e-12)
    vstep("reciprocal", S(1), S(1))
    vstep("tensor_copy", R["rowbf"][0:1, 2048:4096], S(1))

    mmg = ctx.cnt["mm"]

    @block.tensor
    def _(te):
        te.wait_ge(sem["dv"], ctx.cnt["dv"])
        for i in range(4):
            te.matmul(pf[i][:, 0:512], R["oner"][:], R["rowbf"][0:1, 2048 + i * 512:2048 + (i + 1) * 512],
                      start=True, stop=True).then_inc(sem["mm"], 1)
    ctx.cnt["mm"] = mmg + 4

    evg = ctx.cnt["ev"]

    @block.scalar
    def _(sc):
        for i in range(4):
            sc.wait_ge(sem["mm"], mmg + i + 1)
            sc.activation(R["gbuf"][:, i * 512:(i + 1) * 512], pf[i][:, 0:512], AF.Copy,
                          bias=0.0, scale=1.0).then_inc(sem["ev"], 1)
    ctx.cnt["ev"] = evg + 4
    ctx.barrier()

    mur = io["mu_r"]
    for ess in range(4):
        wbl = [ctx.cnt["wA"], ctx.cnt["wB"]]
        mml = ctx.cnt["mm"]
        evl = ctx.cnt["ev"]
        dvl = ctx.cnt["dv"]

        @block.gpsimd
        def _(g, ess=ess):
            if ess >= 1:
                g.wait_ge(sem["mm"], ctx.cnt["mm"])
            g.dma_start(out=scrA[:, 0:8192].rearrange("p (kc e) -> p kc e", kc=KC),
                        in_=mur[ess]).then_inc(sem["wA"], 16)
        ctx.cnt["wA"] = wbl[0] + 16

        @block.tensor
        def _(te, ess=ess, mml=mml, evl=evl, wbl=wbl):
            te.wait_ge(sem["wA"], wbl[0] + 16)
            for tc in range(16):
                if ess >= 1 or tc >= 4:
                    te.wait_ge(sem["ev"], evl + tc - 3)
                for kc in range(KC):
                    last = te.matmul(pf[tc % 4][:, 0:512],
                                     xbf[:, kc * T + tc * 128: kc * T + (tc + 1) * 128],
                                     scrA[:, kc * 512:(kc + 1) * 512],
                                     start=(kc == 0), stop=(kc == KC - 1))
                last.then_inc(sem["mm"], 1)
        ctx.cnt["mm"] = mml + 16

        @block.scalar
        def _(sc, ess=ess, mml=mml):
            for tc in range(16):
                sc.wait_ge(sem["mm"], mml + tc + 1)
                if tc >= 14:
                    pass
                sc.activation(buf2[:, tc * T + ess * 512: tc * T + ess * 512 + 512],
                              pf[tc % 4][:, 0:512], AF.Copy, bias=0.0, scale=1.0
                              ).then_inc(sem["ev"], 1)
        ctx.cnt["ev"] = evl + 16

        @block.vector
        def _(v, ess=ess, evl=evl, dvl=dvl):
            for tc in range(16):
                v.wait_ge(sem["ev"], evl + tc + 1)
                v.tensor_tensor(buf2[:, tc * T + ess * 512: tc * T + ess * 512 + 512],
                                buf2[:, tc * T + ess * 512: tc * T + ess * 512 + 512],
                                R["gbuf"][:, ess * 512:(ess + 1) * 512], OP.mult
                                ).then_inc(sem["dv"], 1)
        ctx.cnt["dv"] = dvl + 16
    ctx.barrier()

    dve = ctx.cnt["dv"]
    eve = ctx.cnt["ev"]

    @block.scalar
    def _(sc):
        sc.wait_ge(sem["dv"], dve)
        for tc in range(16):
            sc.activation(buf2[:, tc * T:(tc + 1) * T], buf2[:, tc * T:(tc + 1) * T],
                          AF.Exp, bias=R["nsb"][:, 0:1], scale=R["ccol"][:, tc:tc + 1]
                          ).then_inc(sem["ev"], 1)
    ctx.cnt["ev"] = eve + 16

    @block.vector
    def _(v):
        for tc in range(16):
            v.wait_ge(sem["ev"], eve + tc + 1)
            v.tensor_reduce(R["lcol"][:, tc:tc + 1], buf2[:, tc * T:(tc + 1) * T],
                            mybir.AxisListType.X, OP.add).then_inc(sem["dv"], 1)
            ctx.cnt["dv"] += 1
            v.wait_ge(sem["dv"], ctx.cnt["dv"])
            v.reciprocal(R["lcol"][:, 16 + tc:17 + tc], R["lcol"][:, tc:tc + 1]).then_inc(sem["dv"], 1)
            ctx.cnt["dv"] += 1

    mmD = ctx.cnt["mm"]

    @block.tensor
    def _(te):
        te.wait_ge(sem["ev"], eve + 16)
        for tc in range(16):
            for ts in range(4):
                last = te.matmul(pf[ts][0:1, 0:512], R["onec"][:],
                                 buf2[:, tc * T + ts * 512: tc * T + ts * 512 + 512],
                                 start=(tc == 0), stop=(tc == 15))
        last.then_inc(sem["mm"], 1)
    ctx.cnt["mm"] = mmD + 1

    evD = ctx.cnt["ev"]

    @block.scalar
    def _(sc):
        sc.wait_ge(sem["mm"], mmD + 1)
        for ts in range(4):
            last = sc.activation(S(2)[:, ts * 512:(ts + 1) * 512], pf[ts][0:1, 0:512],
                                 AF.Copy, bias=0.0, scale=1.0)
        last.then_inc(sem["ev"], 1)
    ctx.cnt["ev"] = evD + 1

    @block.sync
    def _(s):
        s.wait_ge(sem["ev"], evD + 1)
        s.dma_start(out=dram["d_in"].ap().rearrange("(one a) e -> one (a e)", one=1, a=16),
                    in_=S(2)).then_inc(sem["st"], 16)
    ctx.cnt["st"] += 16
    ctx.barrier()

    mmT = ctx.cnt["mm"]
    evT = ctx.cnt["ev"]
    pairs = []
    for a in range(KC):
        for b in range(a, KC):
            pairs.append((a, b))

    @block.tensor
    def _(te):
        n = 0
        for (a, b) in pairs:
            if n >= 1:
                te.wait_ge(sem["ev"], evT + n)
            te.transpose(pt[0][:, 0:128], xbf[:, a * T + b * 128: a * T + b * 128 + 128],
                         R["identb"][:]).then_inc(sem["mm"], 1)
            if a != b:
                te.transpose(pt[1][:, 0:128], xbf[:, b * T + a * 128: b * T + a * 128 + 128],
                             R["identb"][:]).then_inc(sem["mm"], 1)
            n += 1
    nsw = len(pairs)
    ntr = sum(2 if a != b else 1 for (a, b) in pairs)
    ctx.cnt["mm"] = mmT + ntr

    @block.scalar
    def _(sc):
        k = 0
        n = 0
        for (a, b) in pairs:
            if a != b:
                sc.wait_ge(sem["mm"], mmT + k + 2)
                sc.activation(xbf[:, a * T + b * 128: a * T + b * 128 + 128], pt[1][:, 0:128],
                              AF.Copy, bias=0.0, scale=1.0)
                sc.activation(xbf[:, b * T + a * 128: b * T + a * 128 + 128], pt[0][:, 0:128],
                              AF.Copy, bias=0.0, scale=1.0).then_inc(sem["ev"], 1)
                k += 2
            else:
                sc.wait_ge(sem["mm"], mmT + k + 1)
                sc.activation(xbf[:, a * T + b * 128: a * T + b * 128 + 128], pt[0][:, 0:128],
                              AF.Copy, bias=0.0, scale=1.0).then_inc(sem["ev"], 1)
                k += 1
            n += 1
    ctx.cnt["ev"] = evT + nsw
    ctx.barrier()

    sdt = dram["s_in"].ap().rearrange("(esc kc r) (dd m) -> esc kc (r dd) m",
                                      esc=16, kc=16, r=8, dd=16, m=128)
    mmS = ctx.cnt["mm"]
    evS = ctx.cnt["ev"]
    stS = [ctx.cnt["sA"], ctx.cnt["sB"]]

    @block.tensor
    def _(te):
        idx = 0
        for kcd in range(16):
            for ds in range(4):
                if idx >= 4:
                    te.wait_ge(sem["ev"], evS + idx - 3)
                for tc in range(16):
                    last = te.matmul(pf[idx % 4][:, 0:512],
                                     xbf[:, tc * T + kcd * 128: tc * T + (kcd + 1) * 128],
                                     buf2[:, tc * T + ds * 512: tc * T + ds * 512 + 512],
                                     start=(tc == 0), stop=(tc == 15))
                last.then_inc(sem["mm"], 1)
                idx += 1
    ctx.cnt["mm"] = mmS + 64

    for idx in range(64):
        kcd, ds = divmod(idx, 4)
        sp = "sA" if idx % 2 == 0 else "sB"

        @block.scalar
        def _(sc, idx=idx):
            sc.wait_ge(sem["mm"], mmS + idx + 1)
            if idx >= 2:
                sc.wait_ge(sem["s" + ("A" if idx % 2 == 0 else "B")],
                           stS[idx % 2] + 16 * (idx // 2))
            sc.activation(R["sstage"][:, (idx % 2) * 512:(idx % 2) * 512 + 512],
                          pf[idx % 4][:, 0:512], AF.Copy, bias=0.0, scale=1.0
                          ).then_inc(sem["ev"], 1)
        ctx.cnt["ev"] += 1

        @block.sync
        def _(s, idx=idx, kcd=kcd, ds=ds, sp=sp):
            s.wait_ge(sem["ev"], evS + idx + 1)
            s.dma_start(out=sdt[ds * 4:(ds + 1) * 4, kcd].rearrange("e p m -> p e m"),
                        in_=R["sstage"][:, (idx % 2) * 512:(idx % 2) * 512 + 512
                                        ].rearrange("p (e m) -> p e m", e=4)
                        ).then_inc(sem[sp], 16)
        ctx.cnt[sp] += 16

    @block.gpsimd
    def _(g):
        g.wait_ge(sem["sA"], ctx.cnt["sA"])
        g.wait_ge(sem["sB"], ctx.cnt["sB"])
        g.wait_ge(sem["st"], ctx.cnt["st"])
        g.collective_compute("AllGather", OP.bypass, replica_groups=PAIRS,
                             ins=[dram["s_in"].ap().opt()], outs=[dram["s_out"].ap().opt()]
                             ).then_inc(sem["cc"])
        g.collective_compute("AllGather", OP.bypass, replica_groups=PAIRS,
                             ins=[dram["d_in"].ap().opt()], outs=[dram["d_out"].ap().opt()]
                             ).then_inc(sem["cc"])
    ctx.cnt["cc"] += 2

    dvC = ctx.cnt["dv"]

    @block.vector
    def _(v):
        v.wait_ge(sem["mm"], ctx.cnt["mm"])
        for tc in range(16):
            v.tensor_scalar(buf2[:, tc * T:(tc + 1) * T], buf2[:, tc * T:(tc + 1) * T],
                            R["lcol"][:, 16 + tc:17 + tc], None, OP.mult).then_inc(sem["dv"], 1)
    ctx.cnt["dv"] = dvC + 16
    ctx.barrier()

    mmC = ctx.cnt["mm"]
    evC = ctx.cnt["ev"]

    @block.tensor
    def _(te):
        n = 0
        for esc in range(16):
            for tc in range(16):
                if n >= 4:
                    te.wait_ge(sem["ev"], evC + n - 3)
                te.transpose(pt[n % 4][:, 0:128],
                             buf2[:, tc * T + esc * 128: tc * T + esc * 128 + 128],
                             R["identb"][:]).then_inc(sem["mm"], 1)
                n += 1
    ctx.cnt["mm"] = mmC + 256

    @block.scalar
    def _(sc):
        n = 0
        for esc in range(16):
            for tc in range(16):
                sc.wait_ge(sem["mm"], mmC + n + 1)
                sc.activation(xbf[:, esc * T + tc * 128: esc * T + tc * 128 + 128],
                              pt[n % 4][:, 0:128], AF.Copy, bias=0.0, scale=1.0
                              ).then_inc(sem["ev"], 1)
                n += 1
    ctx.cnt["ev"] = evC + 256
    ctx.barrier()

    @block.gpsimd
    def _(g):
        g.wait_ge(sem["cc"], ctx.cnt["cc"])
        g.dma_start(out=R["sstage"][0:1, 0:2048],
                    in_=dram["d_out"].ap().rearrange("(rk a) e -> rk (a e)", rk=2)[0:1]
                    ).then_inc(sem["xs"], 16)
        g.dma_start(out=R["gbuf"][0:1, 0:2048],
                    in_=dram["d_out"].ap().rearrange("(rk a) e -> rk (a e)", rk=2)[1:2]
                    ).then_inc(sem["xs"], 16)
    ctx.cnt["xs"] += 32

    @block.vector
    def _(v):
        v.wait_ge(sem["xs"], ctx.cnt["xs"])
        v.tensor_tensor(S(1), R["sstage"][0:1, 0:2048], R["gbuf"][0:1, 0:2048],
                        OP.add).then_inc(sem["dv"], 1)
        ctx.cnt["dv"] += 1
        v.wait_ge(sem["dv"], ctx.cnt["dv"])
        v.tensor_scalar_max(S(1), S(1), 1e-30).then_inc(sem["dv"], 1)
        ctx.cnt["dv"] += 1
        v.wait_ge(sem["dv"], ctx.cnt["dv"])
        v.reciprocal(S(1), S(1)).then_inc(sem["dv"], 1)
        ctx.cnt["dv"] += 1
        v.wait_ge(sem["dv"], ctx.cnt["dv"])
        v.tensor_copy(R["rowbf"][0:1, 0:2048], S(1)).then_inc(sem["dv"], 1)
        ctx.cnt["dv"] += 1

    mmd = ctx.cnt["mm"]

    evd = ctx.cnt["ev"]

    @block.tensor
    def _(te):
        te.wait_ge(sem["dv"], ctx.cnt["dv"])
        for i in range(16):
            if i >= 4:
                te.wait_ge(sem["ev"], evd + i - 3)
            te.transpose(pt[i % 4][0:128, 0:1], R["rowbf"][0:1, i * 128:(i + 1) * 128],
                         R["identb"][0:1, 0:1]).then_inc(sem["mm"], 1)
    ctx.cnt["mm"] = mmd + 16

    @block.scalar
    def _(sc):
        for i in range(16):
            sc.wait_ge(sem["mm"], mmd + i + 1)
            sc.activation(R["dcol"][:, i:i + 1], pt[i % 4][0:128, 0:1], AF.Copy,
                          bias=0.0, scale=1.0).then_inc(sem["ev"], 1)
    ctx.cnt["ev"] = evd + 16
    ctx.barrier()

    sot = dram["so_t"].ap().rearrange("(dc esc r) (dd m) -> dc esc (r dd) m",
                                      dc=16, esc=16, r=8, dd=16, m=128)
    sov = dram["s_out"].ap().rearrange("(rk esc kc r) (dd m) -> rk esc kc (r dd) m",
                                       rk=2, esc=16, kc=16, r=8, dd=16, m=128)
    SA = buf2[:, 0:2048]
    SB_ = buf2[:, 2048:4096]
    SE = buf2[:, 4096:6144]
    W1b = lambda h: buf2[:, 6144:14336]
    W2b = lambda h: buf2[:, 14336:22528]
    h1sb = buf2[:, 22528:23040]
    h1t = buf2[:, 23040:23552]
    sostage = buf2[:, 23552:25600]

    for e in range(NE):
        kvb = [ctx.cnt["kvA"], ctx.cnt["kvB"]]
        wbe = [ctx.cnt["wA"], ctx.cnt["wB"]]
        mme = ctx.cnt["mm"]
        eve2 = ctx.cnt["ev"]
        dve2 = ctx.cnt["dv"]
        vbe = [ctx.cnt["wC"], ctx.cnt["wD"]]
        ste = [ctx.cnt["sA"], ctx.cnt["sB"]]

        @block.sync
        def _(s, e=e, mme=mme):
            if e >= 1:
                s.wait_ge(sem["mm"], mme)
            for rk in range(2):
                s.dma_start(out=(SA if rk == 0 else SB_).rearrange("p (kc m) -> p kc m", kc=16),
                            in_=sov[rk, e].rearrange("kc p m -> p kc m")
                            ).then_inc(sem["kvA" if rk == 0 else "kvB"], 16)

        ctx.cnt["kvA"] = kvb[0] + 16
        ctx.cnt["kvB"] = kvb[1] + 16
        ctx.cnt["wC"] = vbe[0] + 16
        ctx.cnt["wD"] = vbe[1] + 16

        @block.gpsimd
        def _(g, e=e, mme=mme):
            if e >= 1:
                g.wait_ge(sem["mm"], mme)
            g.dma_start(out=R["sstage"][0:1, 2048:2560],
                        in_=io["b1"][e].rearrange("(one h) -> one h", one=1)).then_inc(sem["wC"], 16)
            g.dma_start(out=R["sstage"][0:1, 0:2048],
                        in_=io["b2"][e].rearrange("(one d) -> one d", one=1)).then_inc(sem["wD"], 16)
            g.dma_start(out=W1b(e % 2).rearrange("p (kc h) -> p kc h", kc=16),
                        in_=io["w1"][e].rearrange("(kc p) h -> p kc h", p=128)
                        ).then_inc(sem["wA"], 16)
            g.dma_start(out=W2b(e % 2).rearrange("p (hc d) -> p hc d", hc=4),
                        in_=io["w2e"][e].rearrange("(hc p) d -> p hc d", p=128)
                        ).then_inc(sem["wB"], 16)
        ctx.cnt["wA"] = wbe[0] + 16
        ctx.cnt["wB"] = wbe[1] + 16

        @block.vector
        def _(v, e=e, kvb=kvb, dve2=dve2):
            v.wait_ge(sem["kvA"], kvb[0] + 16)
            v.wait_ge(sem["kvB"], kvb[1] + 16)
            v.tensor_tensor(SE, SA, SB_, OP.add).then_inc(sem["dv"], 1)
            v.wait_ge(sem["dv"], dve2 + 1)
            v.tensor_scalar(SE, SE, R["dcol"][:, e:e + 1], None, OP.mult).then_inc(sem["dv"], 1)
        ctx.cnt["dv"] = dve2 + 2

        @block.tensor
        def _(te, e=e, mme=mme, dve2=dve2, wbe=wbe, vbe=vbe, eve2=eve2):
            te.wait_ge(sem["dv"], dve2 + 2)
            te.wait_ge(sem["ev"], eve2)
            te.wait_ge(sem["wA"], wbe[0] + 16)
            te.wait_ge(sem["wC"], vbe[0] + 16)
            for kc in range(16):
                te.matmul(pf[0][:, 0:512], SE[:, kc * 128:(kc + 1) * 128],
                          W1b(e % 2)[:, kc * 512:(kc + 1) * 512],
                          start=(kc == 0), stop=False)
            te.matmul(pf[0][:, 0:512], R["oner"][:],
                      R["sstage"][0:1, 2048:2560],
                      start=False, stop=True).then_inc(sem["mm"], 1)
            te.wait_ge(sem["ev"], eve2 + 1)
            for hc in range(4):
                te.transpose(pt[hc][:, 0:128], h1sb[:, hc * 128:(hc + 1) * 128],
                             R["identb"][:]).then_inc(sem["mm"], 1)
            te.wait_ge(sem["ev"], eve2 + 1 + 4)
            te.wait_ge(sem["wB"], wbe[1] + 16)
            te.wait_ge(sem["wD"], vbe[1] + 16)
            for ds in range(4):
                for hc in range(4):
                    te.matmul(pf[ds][:, 0:512], h1t[:, hc * 128:(hc + 1) * 128],
                              W2b(e % 2)[:, hc * 2048 + ds * 512: hc * 2048 + ds * 512 + 512],
                              start=(hc == 0), stop=False)
                te.matmul(pf[ds][:, 0:512], R["oner"][:],
                          R["sstage"][0:1, ds * 512:(ds + 1) * 512],
                          start=False, stop=True).then_inc(sem["mm"], 1)
        ctx.cnt["mm"] = mme + 9

        @block.scalar
        def _(sc, e=e, mme=mme, eve2=eve2, ste=ste):
            sc.wait_ge(sem["mm"], mme + 1)
            sc.activation(h1sb, pf[0][:, 0:512], AF.Gelu, bias=0.0, scale=1.0
                          ).then_inc(sem["ev"], 1)
            for hc in range(4):
                sc.wait_ge(sem["mm"], mme + 1 + hc + 1)
                sc.activation(h1t[:, hc * 128:(hc + 1) * 128], pt[hc][:, 0:128],
                              AF.Copy, bias=0.0, scale=1.0).then_inc(sem["ev"], 1)
            if e >= 1:
                sc.wait_ge(sem["sA"], ste[0])
                sc.wait_ge(sem["sB"], ste[1])
            for ds in range(4):
                sc.wait_ge(sem["mm"], mme + 5 + ds + 1)
                sc.activation(sostage[:, ds * 512:(ds + 1) * 512], pf[ds][:, 0:512],
                              AF.Copy, bias=0.0, scale=1.0).then_inc(sem["ev"], 1)
        ctx.cnt["ev"] = eve2 + 9

        sp = "sA" if e % 2 == 0 else "sB"

        @block.sync
        def _(s, e=e, eve2=eve2, sp=sp):
            s.wait_ge(sem["ev"], eve2 + 9)
            s.dma_start(out=sot[:, e].rearrange("dc p m -> p dc m"),
                        in_=sostage.rearrange("p (dc m) -> p dc m", dc=16)
                        ).then_inc(sem[sp], 16)
        ctx.cnt[sp] += 16
    ctx.barrier()

    @block.sync
    def _(s):
        s.wait_ge(sem["sA"], ctx.cnt["sA"])
        s.wait_ge(sem["sB"], ctx.cnt["sB"])
    ctx.barrier()

    so_w = sot.rearrange("dc esc p m -> dc p esc m")
    emit_outproj_residual(ctx, R, so_w, xbf, io["x_in"], io["x_out"])
    ctx.barrier()


def build_l1_ffn():
    nc = bass.Bass("TRN2", target_bir_lowering=False)
    io = {}
    io["x_in"] = nc.dram_tensor("x_in", [D, T], F32, kind="ExternalInput").ap()
    io["x_out"] = nc.dram_tensor("x_out", [D, T], F32, kind="ExternalOutput").ap()
    io["ident"] = nc.dram_tensor("ident", [128, 128], F32, kind="ExternalInput").ap()
    io["gamma"] = nc.dram_tensor("gamma", [D], F32, kind="ExternalInput").ap()
    io["beta"] = nc.dram_tensor("beta", [D], F32, kind="ExternalInput").ap()
    io["scale"] = nc.dram_tensor("scale", [1], F32, kind="ExternalInput").ap()
    io["mu_t"] = nc.dram_tensor("mu_t", [16, 128, KC * 128], F32, kind="ExternalInput").ap()
    io["mu_r"] = nc.dram_tensor("mu_r", [4, 128, KC * 512], F32, kind="ExternalInput").ap()
    io["w1"] = nc.dram_tensor("w1", [16, 2048, 512], F32, kind="ExternalInput").ap()
    io["b1"] = nc.dram_tensor("b1", [16, 512], F32, kind="ExternalInput").ap()
    io["w2e"] = nc.dram_tensor("w2e", [16, 512, 2048], F32, kind="ExternalInput").ap()
    io["b2"] = nc.dram_tensor("b2", [16, 2048], F32, kind="ExternalInput").ap()
    dram = {
        "s_in": nc.dram_tensor("s_cc_in", [D, T], BF16),
        "s_out": nc.dram_tensor("s_cc_out", [2 * D, T], BF16),
        "d_in": nc.dram_tensor("d_cc_in", [16, 128], F32),
        "d_out": nc.dram_tensor("d_cc_out", [32, 128], F32),
        "so_t": nc.dram_tensor("so_t", [D, T], BF16),
    }
    with contextlib.ExitStack() as stack:
        sems, R = alloc_shared(nc, stack, kind="l1f")
        with nc.Block() as block:
            ctx = Ctx(nc, block, sems)
            ctx.dummy = R["constf"][0:1, 2:4]
            emit_const_loads(ctx, R, io, kind="l1f")
            emit_l1_ffn_block(ctx, R, io, dram)

            @block.sync
            def _(s):
                s.wait_ge(sems["sA"], ctx.cnt["sA"])
                s.wait_ge(sems["sB"], ctx.cnt["sB"])
    return nc


def run_l1_ffn(x_shards, gamma, beta, mu, scale, W1, b1, W2, b2):
    if "l1f" not in _BUILD_CACHE:
        _BUILD_CACHE["l1f"] = build_l1_ffn()
    nc = _BUILD_CACHE["l1f"]
    mu2d = np.ascontiguousarray(np.asarray(mu).reshape(D, 2048).astype(np.float32))
    mu_t = lhsT_tiles(mu2d.T)
    mu_r = np.ascontiguousarray(
        mu2d.reshape(KC, 128, 4, 512).transpose(2, 1, 0, 3).reshape(4, 128, KC * 512))
    wt = dict(ident=np.eye(128, dtype=np.float32),
              gamma=np.asarray(gamma, np.float32), beta=np.asarray(beta, np.float32),
              scale=np.asarray(scale, np.float32),
              mu_t=mu_t, mu_r=mu_r,
              w1=np.asarray(W1, np.float32), b1=np.asarray(b1, np.float32),
              w2e=np.asarray(W2, np.float32), b2=np.asarray(b2, np.float32))
    in_maps = []
    for c in range(8):
        m = dict(x_in=x_shards[c])
        m.update(wt)
        in_maps.append(m)
    res = run_bass_kernel_spmd(nc, in_maps, core_ids=list(range(8)))
    return [res.results[c]["x_out"] for c in range(8)]



def build_fused():
    nc = bass.Bass("TRN2", target_bir_lowering=False)
    io = {}
    io["x_in"] = nc.dram_tensor("x_in", [D, T], F32, kind="ExternalInput").ap()
    io["x_out"] = nc.dram_tensor("x_out", [D, T], F32, kind="ExternalOutput").ap()
    for nm, shp in [("ident", [128, 128]), ("maskA", [4, 128, BS]), ("maskB", [4, 128, BS]),
                    ("cosT", [32, T]), ("sinT", [32, T]), ("perm32", [128, 128]),
                    ("h128", [128, 128]), ("h64b", [128, 128])]:
        io[nm] = nc.dram_tensor(nm, shp, F32, kind="ExternalInput").ap()
    for nm in ["wq", "wk", "wv0", "wg", "wp"]:
        io[nm] = nc.dram_tensor(nm, [16, 128, KC * 128], F32, kind="ExternalInput").ap()
    io["f_w1"] = nc.dram_tensor("f_w1", [64, 128, KC * 128], F32, kind="ExternalInput").ap()
    io["f_w3"] = nc.dram_tensor("f_w3", [64, 128, KC * 128], F32, kind="ExternalInput").ap()
    io["f_w2"] = nc.dram_tensor("f_w2", [16, 2, 128, 32 * 128], F32, kind="ExternalInput").ap()
    io["wqk"] = nc.dram_tensor("wqk", [32, 128, KC * 128], F32, kind="ExternalInput").ap()
    io["wv1"] = nc.dram_tensor("wv1", [16, 128, KC * 128], F32, kind="ExternalInput").ap()
    io["wout"] = nc.dram_tensor("wout", [16, 128, KC * 128], F32, kind="ExternalInput").ap()
    io["gamma"] = nc.dram_tensor("gamma", [D], F32, kind="ExternalInput").ap()
    io["beta"] = nc.dram_tensor("beta", [D], F32, kind="ExternalInput").ap()
    io["scale"] = nc.dram_tensor("scale", [1], F32, kind="ExternalInput").ap()
    io["mu_t"] = nc.dram_tensor("mu_t", [16, 128, KC * 128], F32, kind="ExternalInput").ap()
    io["mu_r"] = nc.dram_tensor("mu_r", [4, 128, KC * 512], F32, kind="ExternalInput").ap()
    io["w1"] = nc.dram_tensor("w1", [16, 2048, 512], F32, kind="ExternalInput").ap()
    io["b1"] = nc.dram_tensor("b1", [16, 512], F32, kind="ExternalInput").ap()
    io["w2e"] = nc.dram_tensor("w2e", [16, 512, 2048], F32, kind="ExternalInput").ap()
    io["b2"] = nc.dram_tensor("b2", [16, 2048], F32, kind="ExternalInput").ap()

    xb1 = nc.dram_tensor("xb1", [D, T], F32)
    xb2 = nc.dram_tensor("xb2", [D, T], F32)
    xb3 = nc.dram_tensor("xb3", [D, T], F32)
    dram = {
        "k_in": nc.dram_tensor("k_cc_in", [D, T], BF16),
        "k_out": nc.dram_tensor("k_cc_out", [2 * D, T], BF16),
        "v_in": nc.dram_tensor("v_cc_in", [T, D], BF16),
        "v_out": nc.dram_tensor("v_cc_out", [2 * T, D], BF16),
        "q_t": nc.dram_tensor("q_t", [D, T], BF16),
        "g_t": nc.dram_tensor("g_t", [D, T], BF16),
        "h2": nc.dram_tensor("h2", [64 * 128, T], BF16),
        "s_in": nc.dram_tensor("s_cc_in", [D, T], BF16),
        "s_out": nc.dram_tensor("s_cc_out", [2 * D, T], BF16),
        "d_in": nc.dram_tensor("d_cc_in", [16, 128], F32),
        "d_out": nc.dram_tensor("d_cc_out", [32, 128], F32),
        "so_t": nc.dram_tensor("so_t", [D, T], BF16),
    }

    with contextlib.ExitStack() as stack:
        sems, R = alloc_shared(nc, stack, kind="fused")
        with nc.Block() as block:
            ctx = Ctx(nc, block, sems)
            ctx.dummy = R["constf"][0:1, 2:4]
            emit_const_loads(ctx, R, io, kind="fused")

            def glue():
                @block.sync
                def _(s):
                    s.wait_ge(sems["sA"], ctx.cnt["sA"])
                    s.wait_ge(sems["sB"], ctx.cnt["sB"])
                    s.wait_ge(sems["st"], ctx.cnt["st"])
                ctx.barrier()

            io0 = dict(io)
            io0["x_out"] = xb1.ap()
            io0["wv"] = io["wv0"]
            emit_l0_attn_block(ctx, R, io0, dram)
            glue()

            io1 = dict(io)
            io1["x_in"] = xb1.ap()
            io1["x_out"] = xb2.ap()
            io1["w1"] = io["f_w1"]
            io1["w3"] = io["f_w3"]
            io1["w2"] = io["f_w2"]
            emit_l0_ffn_block(ctx, R, io1, dram)
            glue()

            io2 = dict(io)
            io2["x_in"] = xb2.ap()
            io2["x_out"] = xb3.ap()
            io2["wv"] = io["wv1"]
            emit_l1_attn_block(ctx, R, io2, dram)
            glue()

            io3 = dict(io)
            io3["x_in"] = xb3.ap()
            emit_l1_ffn_block(ctx, R, io3, dram)
            glue()
    return nc


def prep_fused_inputs(inputs):
    qkv_t = lhsT_tiles(np.asarray(inputs["L1_attn_w_qkv"]))
    shared = dict(
        ident=np.eye(128, dtype=np.float32),
        h128=hadamard(128),
        h64b=np.kron(np.eye(2, dtype=np.float32), hadamard(64)),
        perm32=perm32_mat(),
        wq=lhsT_tiles(np.asarray(inputs["L0_attn_Wq"])),
        wk=lhsT_tiles(np.asarray(inputs["L0_attn_Wk"])),
        wv0=lhsT_tiles(np.asarray(inputs["L0_attn_Wv"])),
        wg=lhsT_tiles(np.asarray(inputs["L0_attn_Wg"])),
        wp=lhsT_tiles(np.asarray(inputs["L0_attn_Wp"])),
        f_w1=lhsT_tiles(np.asarray(inputs["L0_ffn_W1"])),
        f_w3=lhsT_tiles(np.asarray(inputs["L0_ffn_W3"])),
        f_w2=w2_tiles(np.asarray(inputs["L0_ffn_W2"])),
        wqk=np.ascontiguousarray(qkv_t[0:32]),
        wv1=np.ascontiguousarray(qkv_t[32:48]),
        wout=lhsT_tiles(np.asarray(inputs["L1_attn_w_out"])),
        gamma=np.asarray(inputs["L1_ffn_gamma"], np.float32),
        beta=np.asarray(inputs["L1_ffn_beta"], np.float32),
        scale=np.asarray(inputs["L1_ffn_scale"], np.float32),
        w1=np.asarray(inputs["L1_ffn_W1"], np.float32),
        b1=np.asarray(inputs["L1_ffn_b1"], np.float32),
        w2e=np.asarray(inputs["L1_ffn_W2"], np.float32),
        b2=np.asarray(inputs["L1_ffn_b2"], np.float32),
    )
    mu2d = np.ascontiguousarray(np.asarray(inputs["L1_ffn_mu"]).reshape(D, 2048).astype(np.float32))
    shared["mu_t"] = lhsT_tiles(mu2d.T)
    shared["mu_r"] = np.ascontiguousarray(
        mu2d.reshape(KC, 128, 4, 512).transpose(2, 1, 0, 3).reshape(4, 128, KC * 512))
    percore = []
    for c in range(8):
        r = c % 2
        mA, mB = make_masks(r)
        cosT, sinT = rot_tables(r)
        percore.append(dict(maskA=mA, maskB=mB, cosT=cosT, sinT=sinT))
    return shared, percore


def kernel(**inputs):
    x = np.asarray(inputs["L0_attn_x"], np.float32)
    shards = shard_x(x)
    if "fused" not in _BUILD_CACHE:
        _BUILD_CACHE["fused"] = build_fused()
    nc = _BUILD_CACHE["fused"]
    shared, percore = prep_fused_inputs(inputs)
    in_maps = []
    for c in range(8):
        m = dict(x_in=shards[c])
        m.update(shared)
        m.update(percore[c])
        in_maps.append(m)
    res = run_bass_kernel_spmd(nc, in_maps, core_ids=list(range(8)))
    return unshard_x([res.results[c]["x_out"] for c in range(8)]).astype(np.float32)


def kernel_unfused(**inputs):
    x = np.asarray(inputs["L0_attn_x"], np.float32)
    shards = shard_x(x)
    Ws0 = {k: np.asarray(inputs[f"L0_attn_W{k}"]) for k in ["q", "k", "v", "g", "p"]}
    shards = run_l0_attn(shards, Ws0)
    shards = run_l0_ffn(shards, inputs["L0_ffn_W1"], inputs["L0_ffn_W2"], inputs["L0_ffn_W3"])
    shards = run_l1_attn(shards, inputs["L1_attn_w_qkv"], inputs["L1_attn_w_out"])
    shards = run_l1_ffn(shards, inputs["L1_ffn_gamma"], inputs["L1_ffn_beta"],
                        inputs["L1_ffn_mu"], inputs["L1_ffn_scale"],
                        inputs["L1_ffn_W1"], inputs["L1_ffn_b1"],
                        inputs["L1_ffn_W2"], inputs["L1_ffn_b2"])
    return unshard_x(shards).astype(np.float32)


# version 03
# speedup vs baseline: 1.0846x; 1.0846x over previous
import contextlib
import numpy as np
import ml_dtypes
import concourse.bass as bass
import concourse.mybir as mybir
from concourse.bass_utils import run_bass_kernel_spmd

F32 = mybir.dt.float32
BF16 = mybir.dt.bfloat16
AF = mybir.ActivationFunctionType
OP = mybir.AluOpType
BF = ml_dtypes.bfloat16

D = 2048
T = 2048
NBK = 4
BS = 512
KC = 16
LNEPS = 1e-5
RMSEPS = float(np.finfo(np.float32).eps)
FWHT_SCALE = float(1.0 / np.sqrt(8192.0))
NH = 16
NE = 16
PAIRS = [[0, 1], [2, 3], [4, 5], [6, 7]]


def lhsT_tiles(w_out_in):
    M, K = w_out_in.shape
    Mc, Kcc = M // 128, K // 128
    t = np.asarray(w_out_in).reshape(Mc, 128, Kcc, 128).transpose(0, 3, 2, 1)
    return np.ascontiguousarray(t.reshape(Mc, 128, Kcc * 128).astype(np.float32))


def core_blocks(r):
    return [2 * p + 1 - r for p in range(NBK)]


def make_masks(r):
    tq = np.arange(BS)
    tril = np.zeros((4, 128, BS), dtype=np.float32)
    for u in range(4):
        tk = u * 128 + np.arange(128)
        tril[u] = (tk[:, None] <= tq[None, :]).astype(np.float32)
    if r == 0:
        return tril, np.ones_like(tril)
    return np.zeros_like(tril), tril


def shard_x(x):
    out = []
    for c in range(8):
        b, r = c // 2, c % 2
        xt = np.concatenate([x[b, g * BS:(g + 1) * BS, :] for g in core_blocks(r)], axis=0)
        out.append(np.ascontiguousarray(xt.T.astype(np.float32)))
    return out


def unshard_x(shards):
    out = np.zeros((4, 4096, D), dtype=np.float32)
    for c in range(8):
        b, r = c // 2, c % 2
        xt = np.asarray(shards[c]).T
        for p, g in enumerate(core_blocks(r)):
            out[b, g * BS:(g + 1) * BS, :] = xt[p * BS:(p + 1) * BS, :]
    return out



ENGS = ["tensor", "vector", "scalar", "gpsimd", "sync"]


class Ctx:
    def __init__(self, nc, block, sems):
        self.nc = nc
        self.block = block
        self.sem = sems
        self.cnt = {k: 0 for k in sems}
        self.dummy = None

    def barrier(self):
        bar = self.sem["bar"]
        tgt = self.cnt["bar"] + 5
        blk = self.block

        @blk.tensor
        def _(te):
            te.nop().then_inc(bar, 1)
            te.wait_ge(bar, tgt)

        @blk.vector
        def _(v):
            v.engine_nop().then_inc(bar, 1)
            v.wait_ge(bar, tgt)

        @blk.scalar
        def _(sc):
            sc.activation(self.dummy[0:1, 1:2], self.dummy[0:1, 0:1], AF.Copy,
                          bias=0.0, scale=1.0).then_inc(bar, 1)
            sc.wait_ge(bar, tgt)

        @blk.gpsimd
        def _(g):
            g.engine_nop().then_inc(bar, 1)
            g.wait_ge(bar, tgt)

        @blk.sync
        def _(s):
            s.sem_inc(bar, 1)
            s.wait_ge(bar, tgt)

        self.cnt["bar"] = tgt


def tiles4(ap, Mc):
    return ap.rearrange("(mc ts r) (dd t) -> mc ts (r dd) t", mc=Mc, ts=4, r=32, dd=4, t=512)


def emit_proj(ctx, W_ap, Mc, src_slice, evac, wbuf, pf, kc_n=KC, n_ts=4, ts_w=512, ev_name="ev"):
    block, sem = ctx.block, ctx.sem
    mm_base, ev_base = ctx.cnt["mm"], ctx.cnt[ev_name]
    wb = [ctx.cnt["wA"], ctx.cnt["wB"]]
    npf = len(pf)

    @block.gpsimd
    def _(g):
        for mc in range(Mc):
            if mc >= 2:
                g.wait_ge(sem["mm"], mm_base + (mc - 1) * n_ts)
            g.dma_start(out=wbuf[:, (mc % 2) * kc_n * 128:(mc % 2 + 1) * kc_n * 128],
                        in_=W_ap[mc]).then_inc(sem["wA" if mc % 2 == 0 else "wB"], 16)
    ctx.cnt["wA"] = wb[0] + 16 * ((Mc + 1) // 2)
    ctx.cnt["wB"] = wb[1] + 16 * (Mc // 2)

    @block.tensor
    def _(te):
        idx = 0
        for mc in range(Mc):
            te.wait_ge(sem["wA" if mc % 2 == 0 else "wB"],
                       wb[mc % 2] + 16 * (mc // 2 + 1))
            woff = (mc % 2) * kc_n * 128
            for ts in range(n_ts):
                if idx >= npf:
                    te.wait_ge(sem[ev_name], ev_base + idx - npf + 1)
                bank = pf[idx % npf]
                for kc in range(kc_n):
                    last = te.matmul(bank[:, 0:ts_w],
                                     wbuf[:, woff + kc * 128: woff + (kc + 1) * 128],
                                     src_slice(kc, ts),
                                     start=(kc == 0), stop=(kc == kc_n - 1))
                last.then_inc(sem["mm"], 1)
                idx += 1
    ctx.cnt["mm"] = mm_base + Mc * n_ts

    idx = 0
    for mc in range(Mc):
        for ts in range(n_ts):
            evac(mc, ts, pf[idx % npf], mm_base + idx + 1)
            idx += 1
    assert ctx.cnt[ev_name] == ev_base + Mc * n_ts



def alloc_shared(nc, stack, kind="l0a"):
    sems = {}
    for n in ["wA", "wB", "wC", "wD", "xs", "xsA", "xsB", "st", "sA", "sB", "kvA", "kvB",
              "vA", "vB", "qld", "gld", "mm", "ev", "dv", "bar", "cc", "aux", "tp", "ax2"]:
        sems[n] = stack.enter_context(nc.semaphore(n))
    R = {}

    def sb(nm, shp, dt):
        R[nm] = stack.enter_context(nc.sbuf_tensor(nm, shp, dt))

    sb("xbf", [128, KC * T], BF16)
    sb("buf2", [128, KC * T], BF16)
    sb("scrA", [128, 8192], BF16)
    sb("statf", [1, 6144], F32)
    sb("rowbf", [1, 4096], BF16)
    sb("gbuf", [128, 2048], BF16)
    if kind in ("l0a", "l1a", "fused"):
        sb("mska", [128, 4 * BS], BF16)
        sb("mskb", [128, 4 * BS], BF16)
    if kind in ("l1a", "fused"):
        sb("cosSB", [32, T], BF16)
        sb("sinSB", [32, T], BF16)
        sb("permSB", [128, 128], BF16)
    if kind in ("l0f", "fused"):
        sb("h128sb", [128, 128], BF16)
        sb("h64sb", [128, 128], BF16)
    if kind in ("l1f", "fused"):
        sb("gcol", [128, 32], F32)
        sb("srow", [1, 8], F32)
        sb("ccol", [128, 16], F32)
        sb("lcol", [128, 32], F32)
        sb("dcol", [128, 16], F32)
        sb("nsb", [128, 4], F32)
        sb("sstage", [128, 2560], BF16)
    sb("ocol", [128, 1024], F32)
    sb("identb", [128, 128], BF16)
    sb("onec", [128, 1], BF16)
    sb("oner", [1, 128], BF16)
    sb("constf", [128, 8], F32)

    def psm(nm, shp, dt):
        R[nm] = stack.enter_context(nc.psum_tensor(nm, shp, dt))

    for i in range(4):
        psm(f"pf{i}", [128, 512], F32)
    for i in range(4):
        psm(f"pt{i}", [128, 1024], BF16)
    return sems, R


def emit_const_loads(ctx, R, io, kind="l0a"):
    sem = ctx.sem

    @ctx.block.gpsimd
    def _(g):
        g.dma_start(out=R["identb"][:], in_=io["ident"]).then_inc(sem["aux"], 16)
        ctx.cnt["aux"] += 16
        if kind in ("l0a", "l1a", "fused"):
            g.dma_start(out=R["mska"][:].rearrange("p (u t) -> p u t", u=4), in_=io["maskA"].rearrange("u p t -> p u t")).then_inc(sem["aux"], 16)
            g.dma_start(out=R["mskb"][:].rearrange("p (u t) -> p u t", u=4), in_=io["maskB"].rearrange("u p t -> p u t")).then_inc(sem["aux"], 16)
            ctx.cnt["aux"] += 32
        if kind in ("l1a", "fused"):
            g.dma_start(out=R["cosSB"][:], in_=io["cosT"]).then_inc(sem["aux"], 16)
            g.dma_start(out=R["sinSB"][:], in_=io["sinT"]).then_inc(sem["aux"], 16)
            g.dma_start(out=R["permSB"][:], in_=io["perm32"]).then_inc(sem["aux"], 16)
            ctx.cnt["aux"] += 48
        if kind in ("l0f", "fused"):
            g.dma_start(out=R["h128sb"][:], in_=io["h128"]).then_inc(sem["aux"], 16)
            g.dma_start(out=R["h64sb"][:], in_=io["h64b"]).then_inc(sem["aux"], 16)
            ctx.cnt["aux"] += 32
        if kind == "fused":
            g.memset(R["constf"][:, 4:5], EXPSHIFT)
            ctx.cnt["aux"] += 0
        g.memset(R["onec"][:], 1.0)
        g.memset(R["oner"][:], 1.0)
        g.memset(R["constf"][:, 0:1], LNEPS)
        g.memset(R["constf"][:, 1:2], RMSEPS)
        g.memset(R["constf"][:, 2:4], 0.0).then_inc(sem["ax2"], 1)
    ctx.cnt["ax2"] += 1

    @ctx.block.scalar
    def _(sc):
        sc.wait_ge(sem["ax2"], ctx.cnt["ax2"])



def emit_ln_affine(ctx, R, mode, apply_affine=True):
    block, sem = ctx.block, ctx.sem
    xbf, statf, rowbf = R["xbf"], R["statf"], R["rowbf"]
    pf = [R[f"pf{i}"] for i in range(4)]
    sqb = R["buf2"]
    S = lambda i: statf[0:1, i * 2048:(i + 1) * 2048]

    mm0 = ctx.cnt["mm"]

    @block.tensor
    def _(te):
        for kc in range(KC):
            for ts in range(4):
                last = te.matmul(pf[ts][0:1, 0:512], R["onec"][:],
                                 xbf[:, kc * T + ts * 512: kc * T + ts * 512 + 512],
                                 start=(kc == 0), stop=(kc == KC - 1))
        last.then_inc(sem["mm"], 1)
    ctx.cnt["mm"] = mm0 + 1

    ev0 = ctx.cnt["ev"]

    @block.scalar
    def _(sc):
        sc.wait_ge(sem["mm"], mm0 + 1)
        for ts in range(4):
            last = sc.activation(S(0)[:, ts * 512:(ts + 1) * 512], pf[ts][0:1, 0:512],
                                 AF.Copy, bias=0.0, scale=1.0)
        last.then_inc(sem["ev"], 1)
    ctx.cnt["ev"] = ev0 + 1

    dv0 = ctx.cnt["dv"]
    mm1 = ctx.cnt["mm"]

    @block.scalar
    def _(sc):
        for kc in range(KC):
            if kc >= 2:
                sc.wait_ge(sem["mm"], mm1 + kc - 1)
            sc.activation(sqb[:, (kc % 2) * T:(kc % 2 + 1) * T], xbf[:, kc * T:(kc + 1) * T],
                          AF.Square, bias=0.0, scale=1.0).then_inc(sem["dv"], 1)
    ctx.cnt["dv"] = dv0 + KC

    @block.tensor
    def _(te):
        te.wait_ge(sem["ev"], ev0 + 1)
        for kc in range(KC):
            te.wait_ge(sem["dv"], dv0 + kc + 1)
            for ts in range(4):
                last = te.matmul(pf[ts][0:1, 0:512], R["onec"][:],
                                 sqb[:, (kc % 2) * T + ts * 512:(kc % 2) * T + ts * 512 + 512],
                                 start=(kc == 0), stop=(kc == KC - 1))
            last.then_inc(sem["mm"], 1)
    ctx.cnt["mm"] = mm1 + KC
    mm_sq = ctx.cnt["mm"]

    ev1 = ctx.cnt["ev"]

    @block.scalar
    def _(sc):
        sc.wait_ge(sem["mm"], mm_sq)
        for ts in range(4):
            last = sc.activation(S(1)[:, ts * 512:(ts + 1) * 512], pf[ts][0:1, 0:512],
                                 AF.Copy, bias=0.0, scale=1.0)
        last.then_inc(sem["ev"], 1)
    ctx.cnt["ev"] = ev1 + 1

    def vstep(op, *args, **kw):
        @block.vector
        def _(v):
            v.wait_ge(sem["dv"], ctx.cnt["dv"])
            getattr(v, op)(*args, **kw).then_inc(sem["dv"], 1)
        ctx.cnt["dv"] += 1

    def scstep(out, in_, func, bias, scale=1.0):
        @block.scalar
        def _(sc):
            sc.wait_ge(sem["dv"], ctx.cnt["dv"])
            sc.activation(out, in_, func, bias=bias, scale=scale).then_inc(sem["dv"], 1)
        ctx.cnt["dv"] += 1

    @block.vector
    def _(v):
        v.wait_ge(sem["ev"], ev1 + 1)
    vstep("tensor_scalar_mul", S(0), S(0), 1.0 / T)
    vstep("tensor_scalar_mul", S(1), S(1), 1.0 / T)
    vstep("tensor_mul", S(2), S(0), S(0))
    vstep("tensor_sub", S(1), S(1), S(2))
    scstep(S(2), S(1), AF.Sqrt, R["constf"][0:1, 0:1])
    vstep("reciprocal", S(2), S(2))
    if mode in ("ln_rms", "ln_ln"):
        vstep("tensor_mul", S(1), S(1), S(2))
        vstep("tensor_mul", S(1), S(1), S(2))
        scstep(S(1), S(1), AF.Sqrt,
               R["constf"][0:1, 1:2] if mode == "ln_rms" else R["constf"][0:1, 0:1])
        vstep("reciprocal", S(1), S(1))
        vstep("tensor_mul", S(2), S(2), S(1))
    vstep("tensor_copy", rowbf[0:1, 0:2048], S(2))
    vstep("tensor_mul", S(1), S(0), rowbf[0:1, 0:2048])
    vstep("tensor_scalar_mul", S(1), S(1), -1.0)
    vstep("tensor_copy", rowbf[0:1, 2048:4096], S(1))
    dva = ctx.cnt["dv"] - 1

    if not apply_affine:
        return

    mmb = ctx.cnt["mm"]
    evb = ctx.cnt["ev"]

    @block.tensor
    def _(te):
        te.wait_ge(sem["ev"], evb)
        te.wait_ge(sem["dv"], dva + 1)
        for i in range(8):
            if i >= 4:
                te.wait_ge(sem["ev"], evb + i - 3)
            te.matmul(pf[i % 4][:, 0:512], R["oner"][:],
                      rowbf[0:1, i * 512:(i + 1) * 512],
                      start=True, stop=True).then_inc(sem["mm"], 1)
    ctx.cnt["mm"] = mmb + 8

    @block.scalar
    def _(sc):
        for i in range(8):
            sc.wait_ge(sem["mm"], mmb + i + 1)
            sc.activation(R["scrA"][:, i * 512:(i + 1) * 512], pf[i % 4][:, 0:512],
                          AF.Copy, bias=0.0, scale=1.0).then_inc(sem["ev"], 1)
    ctx.cnt["ev"] = evb + 8

    @block.vector
    def _(v):
        v.wait_ge(sem["ev"], evb + 8)
        for kc in range(KC):
            v.tensor_tensor(xbf[:, kc * T:(kc + 1) * T], xbf[:, kc * T:(kc + 1) * T],
                            R["scrA"][:, 0:T], OP.mult).then_inc(sem["dv"], 1)
            ctx.cnt["dv"] += 1
            v.wait_ge(sem["dv"], ctx.cnt["dv"])
            v.tensor_tensor(xbf[:, kc * T:(kc + 1) * T], xbf[:, kc * T:(kc + 1) * T],
                            R["scrA"][:, T:2 * T], OP.add).then_inc(sem["dv"], 1)
            ctx.cnt["dv"] += 1



def emit_proj_spill(ctx, R, W_ap, act_fn, dst_view, src_sb):
    block, sem = ctx.block, ctx.sem
    pf = [R[f"pf{i}"] for i in range(4)]
    stage = R["buf2"]
    wbuf = R["scrA"][:, 0:4096]
    src = lambda kc, ts: src_sb[:, kc * T + ts * 512: kc * T + ts * 512 + 512]
    ev_base = ctx.cnt["ev"]
    sb_ = [ctx.cnt["sA"], ctx.cnt["sB"]]

    def evac(mc, ts, bank, mm_tgt):
        idx = mc * 4 + ts
        sp = "sA" if idx % 2 == 0 else "sB"

        @block.scalar
        def _(sc):
            if idx == 0:
                sc.wait_ge(sem["sA"], sb_[0])
                sc.wait_ge(sem["sB"], sb_[1])
            sc.wait_ge(sem["mm"], mm_tgt)
            if idx >= 2:
                sc.wait_ge(sem[sp], sb_[idx % 2] + 16 * (idx // 2))
            sc.activation(stage[:, (idx % 2) * 512:(idx % 2) * 512 + 512], bank[:, 0:512],
                          act_fn, bias=0.0, scale=1.0).then_inc(sem["ev"], 1)
        ctx.cnt["ev"] += 1

        @block.sync
        def _(s):
            s.wait_ge(sem["ev"], ev_base + idx + 1)
            s.dma_start(out=dst_view[mc, ts],
                        in_=stage[:, (idx % 2) * 512:(idx % 2) * 512 + 512]).then_inc(sem[sp], 16)
        ctx.cnt[sp] += 16

    emit_proj(ctx, W_ap, 16, src, evac, wbuf, pf)



def emit_l0_v_pass(ctx, R, Wv_ap, vin_ap, src_sb, act_fn=AF.Silu):
    block, sem = ctx.block, ctx.sem
    pf = [R[f"pf{i}"] for i in range(4)]
    pt = [R[f"pt{i}"] for i in range(4)]
    wbuf = R["scrA"][:, 0:4096]
    vstage = R["buf2"][:, 0:1024]
    stB = lambda u: R["buf2"][:, 1024 + u * T: 1024 + (u + 1) * T]
    vin_v = vin_ap.rearrange("(pu p) e -> pu p e", pu=16)
    src = lambda kc, ts: src_sb[:, kc * T + ts * 512: kc * T + ts * 512 + 512]

    for ts in range(4):
        mm_base, ev_base, tp_base, st_base = (ctx.cnt["mm"], ctx.cnt["ev"],
                                              ctx.cnt["tp"], ctx.cnt["st"])
        wb = [ctx.cnt["wA"], ctx.cnt["wB"]]

        @block.gpsimd
        def _(g):
            for mc in range(16):
                if mc >= 2:
                    g.wait_ge(sem["mm"], mm_base + (mc - 1) * 5)
                g.dma_start(out=wbuf[:, (mc % 2) * KC * 128:(mc % 2 + 1) * KC * 128],
                            in_=Wv_ap[mc]).then_inc(sem["wA" if mc % 2 == 0 else "wB"], 16)
        ctx.cnt["wA"] = wb[0] + 16 * 8
        ctx.cnt["wB"] = wb[1] + 16 * 8

        @block.tensor
        def _(te):
            for mc in range(16):
                te.wait_ge(sem["wA" if mc % 2 == 0 else "wB"],
                           wb[mc % 2] + 16 * (mc // 2 + 1))
                woff = (mc % 2) * KC * 128
                if mc >= 2:
                    te.wait_ge(sem["ev"], ev_base + mc - 1)
                for kc in range(KC):
                    last = te.matmul(pf[mc % 2][:, 0:512],
                                     wbuf[:, woff + kc * 128: woff + (kc + 1) * 128],
                                     src(kc, ts), start=(kc == 0), stop=(kc == KC - 1))
                last.then_inc(sem["mm"], 1)
                te.wait_ge(sem["ev"], ev_base + mc + 1)
                if mc >= 1:
                    te.wait_ge(sem["tp"], tp_base + mc * 4)
                for u in range(4):
                    te.transpose(pt[u][:, 0:128],
                                 vstage[:, (mc % 2) * 512 + u * 128:(mc % 2) * 512 + (u + 1) * 128],
                                 R["identb"][:]).then_inc(sem["mm"], 1)
        ctx.cnt["mm"] = mm_base + 16 * 5

        @block.scalar
        def _(sc):
            if ts == 0:
                sc.wait_ge(sem["sA"], ctx.cnt["sA"])
                sc.wait_ge(sem["sB"], ctx.cnt["sB"])
            for mc in range(16):
                sc.wait_ge(sem["mm"], mm_base + mc * 5 + 1)
                if mc >= 2:
                    sc.wait_ge(sem["tp"], tp_base + (mc - 1) * 4)
                sc.activation(vstage[:, (mc % 2) * 512:(mc % 2) * 512 + 512],
                              pf[mc % 2][:, 0:512], act_fn, bias=0.0, scale=1.0
                              ).then_inc(sem["ev"], 1)
        ctx.cnt["ev"] = ev_base + 16

        @block.vector
        def _(v):
            v.wait_ge(sem["st"], st_base)
            for mc in range(16):
                for u in range(4):
                    v.wait_ge(sem["mm"], mm_base + mc * 5 + 1 + u + 1)
                    v.tensor_copy(stB(u)[:, mc * 128:(mc + 1) * 128],
                                  pt[u][:, 0:128]).then_inc(sem["tp"], 1)
        ctx.cnt["tp"] = tp_base + 64

        @block.sync
        def _(s):
            s.wait_ge(sem["tp"], tp_base + 64)
            for u in range(4):
                s.dma_start(out=vin_v[ts * 4 + u], in_=stB(u)[:]).then_inc(sem["st"], 16)
        ctx.cnt["st"] += 64
        ctx.barrier()



def emit_l0_attention(ctx, R, dram):
    block, sem = ctx.block, ctx.sem
    pf = [R[f"pf{i}"] for i in range(4)]
    xbf, buf2, scrA = R["xbf"], R["buf2"], R["scrA"]

    kout_v = dram["k_out"].ap().rearrange("(rk mc ts r) (dd t) -> rk mc ts (r dd) t",
                                          rk=2, mc=16, ts=4, r=32, dd=4, t=512)
    vout_v = dram["v_out"].ap().rearrange("(rk pu p) e -> rk pu p e", rk=2, pu=16)
    qtile_v = tiles4(dram["q_t"].ap(), 16)
    gtile_v = tiles4(dram["g_t"].ap(), 16)

    qtile = buf2[:, 0:8192]
    sTb = buf2[:, 8192:8192 + 32 * 512]
    vcol = buf2[:, 8192 + 32 * 512: 8192 + 32 * 512 + 2 * D]
    kb = lambda hh: scrA[:, hh * 4096:(hh + 1) * 4096]

    for p in range(NBK):
        nblk = p + 1
        nslot = 8 * nblk

        q_base = ctx.cnt["qld"]
        kvb = [ctx.cnt["kvA"], ctx.cnt["kvB"]]

        @block.sync
        def _(s):
            if p == 0:
                s.wait_ge(sem["sA"], ctx.cnt["sA"])
                s.wait_ge(sem["sB"], ctx.cnt["sB"])
            else:
                s.wait_ge(sem["mm"], ctx.cnt["mm"])
            s.dma_start(out=qtile.rearrange("p (mc t) -> p mc t", mc=16),
                        in_=qtile_v[:, p].rearrange("mc p t -> p mc t")).then_inc(sem["qld"], 16)
            for it in range(2 * nblk):
                rk, blk = divmod(it, nblk)
                for hh in range(2):
                    if it >= 1:
                        s.wait_ge(sem["mm"], ctx.cnt["mm"] + it * 4)
                    s.dma_start(out=kb(hh).rearrange("p (mc t) -> p mc t", mc=8),
                                in_=kout_v[rk, 8 * hh:8 * hh + 8, blk].rearrange("mc p t -> p mc t")
                                ).then_inc(sem["kvA" if hh == 0 else "kvB"], 16)
        ctx.cnt["qld"] = q_base + 16
        ctx.cnt["kvA"] = kvb[0] + 16 * 2 * nblk
        ctx.cnt["kvB"] = kvb[1] + 16 * 2 * nblk

        mm_base = ctx.cnt["mm"]
        ev_base = ctx.cnt["ev"]

        @block.tensor
        def _(te):
            te.wait_ge(sem["qld"], q_base + 16)
            te.wait_ge(sem["dv"], ctx.cnt["dv"])
            for it in range(2 * nblk):
                for hh in range(2):
                    te.wait_ge(sem["kvA" if hh == 0 else "kvB"],
                               kvb[hh] + 16 * (it + 1))
                    for u in range(4):
                        if it * 4 + u >= 4:
                            te.wait_ge(sem["ev"], ev_base + it * 4 + u - 3)
                        for kc in range(8):
                            last = te.matmul(pf[u][:, 0:512],
                                             kb(hh)[:, kc * 512 + u * 128: kc * 512 + u * 128 + 128],
                                             qtile[:, (8 * hh + kc) * 512:(8 * hh + kc) * 512 + 512],
                                             start=(hh == 0 and kc == 0), stop=(hh == 1 and kc == 7))
                        if hh == 1:
                            last.then_inc(sem["mm"], 1)
        ctx.cnt["mm"] = mm_base + 2 * nblk * 4

        @block.vector
        def _(v):
            for it in range(2 * nblk):
                rk, blk = divmod(it, nblk)
                for u in range(4):
                    slot = rk * 4 * nblk + blk * 4 + u
                    v.wait_ge(sem["mm"], mm_base + it * 4 + u + 1)
                    if blk == nblk - 1:
                        msk = R["mska"] if rk == 0 else R["mskb"]
                        v.tensor_tensor(sTb[:, slot * 512:(slot + 1) * 512], pf[u][:, 0:512],
                                        msk[:, u * 512:(u + 1) * 512], OP.mult).then_inc(sem["ev"], 1)
                    else:
                        v.tensor_copy(sTb[:, slot * 512:(slot + 1) * 512],
                                      pf[u][:, 0:512]).then_inc(sem["ev"], 1)
        ctx.cnt["ev"] += 2 * nblk * 4
        ev_sim = ctx.cnt["ev"]

        for gi in range(4):
            ecs = [gi * 4 + j for j in range(4)]
            vb = [ctx.cnt["vA"], ctx.cnt["vB"]]
            mmb = ctx.cnt["mm"]
            gldb = ctx.cnt["gld"]
            dvb = ctx.cnt["dv"]

            @block.sync
            def _(s):
                s.wait_ge(sem["dv"], dvb)
                for j, ec in enumerate(ecs):
                    s.dma_start(out=R["gbuf"][:, j * 512:(j + 1) * 512],
                                in_=gtile_v[ec, p]).then_inc(sem["gld"], 16)
                for sl in range(nslot):
                    if sl >= 2:
                        s.wait_ge(sem["mm"], mmb + sl - 1)
                    rk, pu = divmod(sl, 4 * nblk)
                    s.dma_start(out=vcol[:, (sl % 2) * D:(sl % 2) * D + D],
                                in_=vout_v[rk, pu]).then_inc(sem["vA" if sl % 2 == 0 else "vB"], 16)
            ctx.cnt["gld"] = gldb + 64
            ctx.cnt["vA"] = vb[0] + 16 * ((nslot + 1) // 2)
            ctx.cnt["vB"] = vb[1] + 16 * (nslot // 2)

            @block.tensor
            def _(te):
                te.wait_ge(sem["ev"], ev_sim)
                te.wait_ge(sem["dv"], dvb)
                for sl in range(nslot):
                    te.wait_ge(sem["vA" if sl % 2 == 0 else "vB"],
                               vb[sl % 2] + 16 * (sl // 2 + 1))
                    for j in range(4):
                        last = te.matmul(pf[j][:, 0:512],
                                         vcol[:, (sl % 2) * D + ecs[j] * 128:(sl % 2) * D + ecs[j] * 128 + 128],
                                         sTb[:, sl * 512:(sl + 1) * 512],
                                         start=(sl == 0), stop=(sl == nslot - 1))
                    last.then_inc(sem["mm"], 1)
            ctx.cnt["mm"] = mmb + nslot

            @block.vector
            def _(v):
                v.wait_ge(sem["mm"], mmb + nslot)
                v.wait_ge(sem["gld"], gldb + 64)
                for j, ec in enumerate(ecs):
                    v.tensor_tensor(xbf[:, ec * T + p * 512: ec * T + (p + 1) * 512],
                                    pf[j][:, 0:512], R["gbuf"][:, j * 512:(j + 1) * 512],
                                    OP.mult).then_inc(sem["dv"], 1)
            ctx.cnt["dv"] = dvb + 4
    ctx.barrier()



def emit_outproj_residual(ctx, R, W_ap, src_sb, x_in_ap, x_out_ap, Mc=16):
    block, sem = ctx.block, ctx.sem
    pf = [R[f"pf{i}"] for i in range(4)]
    wbuf = R["scrA"][:, 0:4096]
    ocol = R["ocol"]
    src = lambda kc, ts: src_sb[:, kc * T + ts * 512: kc * T + ts * 512 + 512]
    xin_t = x_in_ap.rearrange("(mc p) (ts t) -> mc ts p t", p=128, ts=4)
    xout_t = x_out_ap.rearrange("(mc p) (ts t) -> mc ts p t", p=128, ts=4)
    sb_ = [ctx.cnt["sA"], ctx.cnt["sB"]]
    xb_ = [ctx.cnt["xsA"], ctx.cnt["xsB"]]
    dv_base = ctx.cnt["dv"]

    @block.sync
    def _(s):
        for idx in range(2):
            mc, ts = divmod(idx, 4)
            s.dma_start(out=ocol[:, (idx % 2) * 512:(idx % 2) * 512 + 512],
                        in_=xin_t[mc, ts]).then_inc(sem["xsA" if idx % 2 == 0 else "xsB"], 16)
    ctx.cnt["xsA"] = xb_[0] + 16
    ctx.cnt["xsB"] = xb_[1] + 16

    def evac(mc, ts, bank, mm_tgt):
        idx = mc * 4 + ts
        xp = "xsA" if idx % 2 == 0 else "xsB"
        sp = "sA" if idx % 2 == 0 else "sB"

        @block.vector
        def _(v):
            v.wait_ge(sem["mm"], mm_tgt)
            v.wait_ge(sem[xp], xb_[idx % 2] + 16 * (idx // 2 + 1))
            v.tensor_tensor(ocol[:, (idx % 2) * 512:(idx % 2) * 512 + 512],
                            bank[:, 0:512], ocol[:, (idx % 2) * 512:(idx % 2) * 512 + 512],
                            OP.add).then_inc(sem["dv"], 1)
        ctx.cnt["dv"] += 1

        @block.sync
        def _(s):
            s.wait_ge(sem["dv"], dv_base + idx + 1)
            s.dma_start(out=xout_t[mc, ts],
                        in_=ocol[:, (idx % 2) * 512:(idx % 2) * 512 + 512]).then_inc(sem[sp], 16)
            if idx + 2 < Mc * 4:
                mc2, ts2 = divmod(idx + 2, 4)
                s.wait_ge(sem[sp], sb_[idx % 2] + 16 * (idx // 2 + 1))
                s.dma_start(out=ocol[:, (idx % 2) * 512:(idx % 2) * 512 + 512],
                            in_=xin_t[mc2, ts2]).then_inc(sem[xp], 16)
        ctx.cnt[sp] += 16
        if idx + 2 < Mc * 4:
            ctx.cnt[xp] += 16

    emit_proj(ctx, W_ap, Mc, src, evac, wbuf, pf, ev_name="dv")

    @block.sync
    def _(s):
        s.wait_ge(sem["sA"], ctx.cnt["sA"])
        s.wait_ge(sem["sB"], ctx.cnt["sB"])



def emit_l0_attn_block(ctx, R, io, dram, upto=99):
    sem = ctx.sem
    block = ctx.block
    xbf = R["xbf"]

    @block.gpsimd
    def _(g):
        g.dma_start(out=xbf[:].rearrange("p (kc t) -> p kc t", kc=KC),
                    in_=io["x_in"].rearrange("(kc p) t -> p kc t", p=128)).then_inc(sem["xs"], 16)
    ctx.cnt["xs"] += 16

    @block.tensor
    def _(te):
        te.wait_ge(sem["xs"], ctx.cnt["xs"])
        te.wait_ge(sem["aux"], ctx.cnt["aux"])
        te.wait_ge(sem["ax2"], ctx.cnt["ax2"])
    ctx.barrier()

    emit_ln_affine(ctx, R, "ln_rms")
    ctx.barrier()
    if upto <= 1:
        return

    ktile_v = tiles4(dram["k_in"].ap(), 16)
    qtile_v = tiles4(dram["q_t"].ap(), 16)
    gtile_v = tiles4(dram["g_t"].ap(), 16)

    emit_proj_spill(ctx, R, io["wk"], AF.Silu, ktile_v, xbf)
    ctx.barrier()
    if upto <= 2:
        return
    emit_l0_v_pass(ctx, R, io["wv"], dram["v_in"].ap(), xbf)
    if upto <= 3:
        return
    @block.gpsimd
    def _(g):
        g.wait_ge(sem["st"], ctx.cnt["st"])
        g.wait_ge(sem["sA"], ctx.cnt["sA"])
        g.wait_ge(sem["sB"], ctx.cnt["sB"])
        g.collective_compute("AllGather", OP.bypass, replica_groups=PAIRS,
                             ins=[dram["k_in"].ap().opt()], outs=[dram["k_out"].ap().opt()]
                             ).then_inc(sem["cc"])
        g.collective_compute("AllGather", OP.bypass, replica_groups=PAIRS,
                             ins=[dram["v_in"].ap().opt()], outs=[dram["v_out"].ap().opt()]
                             ).then_inc(sem["cc"])
    ctx.cnt["cc"] += 2
    if upto <= 4:
        @block.gpsimd
        def _(g):
            g.wait_ge(sem["cc"], ctx.cnt["cc"])
        return

    emit_proj_spill(ctx, R, io["wq"], AF.Silu, qtile_v, xbf)
    ctx.barrier()
    emit_proj_spill(ctx, R, io["wg"], AF.Sigmoid, gtile_v, xbf)
    ctx.barrier()
    if upto <= 6:
        return

    @block.gpsimd
    def _(g):
        g.wait_ge(sem["cc"], ctx.cnt["cc"])
    ctx.barrier()

    emit_l0_attention(ctx, R, dram)
    if upto <= 7:
        return

    emit_outproj_residual(ctx, R, io["wp"], xbf, io["x_in"], io["x_out"])
    ctx.barrier()



def build_l0_attn(upto=99):
    nc = bass.Bass("TRN2", target_bir_lowering=False)
    io = {}
    io["x_in"] = nc.dram_tensor("x_in", [D, T], F32, kind="ExternalInput").ap()
    io["x_out"] = nc.dram_tensor("x_out", [D, T], F32, kind="ExternalOutput").ap()
    for nm, shp in [("ident", [128, 128]), ("maskA", [4, 128, BS]), ("maskB", [4, 128, BS])]:
        io[nm] = nc.dram_tensor(nm, shp, F32, kind="ExternalInput").ap()
    for nm in ["wq", "wk", "wv", "wg", "wp"]:
        io[nm] = nc.dram_tensor(nm, [16, 128, KC * 128], F32, kind="ExternalInput").ap()
    dram = {
        "k_in": nc.dram_tensor("k_cc_in", [D, T], BF16),
        "k_out": nc.dram_tensor("k_cc_out", [2 * D, T], BF16),
        "v_in": nc.dram_tensor("v_cc_in", [T, D], BF16),
        "v_out": nc.dram_tensor("v_cc_out", [2 * T, D], BF16),
        "q_t": nc.dram_tensor("q_t", [D, T], BF16),
        "g_t": nc.dram_tensor("g_t", [D, T], BF16),
    }
    with contextlib.ExitStack() as stack:
        sems, R = alloc_shared(nc, stack)
        with nc.Block() as block:
            ctx = Ctx(nc, block, sems)
            ctx.dummy = R["constf"][0:1, 2:4]
            emit_const_loads(ctx, R, io)
            emit_l0_attn_block(ctx, R, io, dram, upto=upto)

            @block.sync
            def _(s):
                s.wait_ge(sems["st"], ctx.cnt["st"])
                s.wait_ge(sems["sA"], ctx.cnt["sA"])
                s.wait_ge(sems["sB"], ctx.cnt["sB"])
    return nc


_L0A_CACHE = {}


def run_l0_attn(x_shards, weights):
    if "nc" not in _L0A_CACHE:
        _L0A_CACHE["nc"] = build_l0_attn()
    nc = _L0A_CACHE["nc"]
    wt = {f"w{k}": lhsT_tiles(weights[k]) for k in ["q", "k", "v", "g", "p"]}
    ident = np.eye(128, dtype=np.float32)
    in_maps = []
    for c in range(8):
        r = c % 2
        mA, mB = make_masks(r)
        m = dict(x_in=x_shards[c], ident=ident, maskA=mA, maskB=mB)
        m.update(wt)
        in_maps.append(m)
    res = run_bass_kernel_spmd(nc, in_maps, core_ids=list(range(8)))
    return [res.results[c]["x_out"] for c in range(8)]



def emit_l0_ffn_block(ctx, R, io, dram):
    block, sem = ctx.block, ctx.sem
    xbf, buf2 = R["xbf"], R["buf2"]
    pf = [R[f"pf{i}"] for i in range(4)]
    sg = R["gbuf"]

    @block.gpsimd
    def _(g):
        g.dma_start(out=xbf[:].rearrange("p (kc t) -> p kc t", kc=KC),
                    in_=io["x_in"].rearrange("(kc p) t -> p kc t", p=128)).then_inc(sem["xs"], 16)
    ctx.cnt["xs"] += 16

    @block.tensor
    def _(te):
        te.wait_ge(sem["xs"], ctx.cnt["xs"])
        te.wait_ge(sem["aux"], ctx.cnt["aux"])
        te.wait_ge(sem["ax2"], ctx.cnt["ax2"])
    ctx.barrier()

    emit_ln_affine(ctx, R, "ln")
    ctx.barrier()

    src = lambda kc, ts: xbf[:, kc * T + ts * 512: kc * T + ts * 512 + 512]
    w1buf = lambda h: R["scrA"][:, h * 2048:(h + 1) * 2048]
    w3buf = lambda h: R["scrA"][:, 4096 + h * 2048: 4096 + (h + 1) * 2048]
    hbuf = lambda h: buf2[:, h * 2048:(h + 1) * 2048]
    habuf = lambda h: buf2[:, 4096 + h * 2048: 4096 + (h + 1) * 2048]
    h2 = dram["h2"].ap()

    NB = 64
    mm0 = ctx.cnt["mm"]
    ev0 = ctx.cnt["ev"]
    dv0 = ctx.cnt["dv"]
    wb = [ctx.cnt["wA"], ctx.cnt["wB"]]
    kb_ = [ctx.cnt["wC"], ctx.cnt["wD"]]
    sb_ = [ctx.cnt["sA"], ctx.cnt["sB"]]

    @block.gpsimd
    def _(g):
        for b in range(NB):
            if b >= 2:
                g.wait_ge(sem["mm"], mm0 + (b - 1) * 12)
            g.dma_start(out=w1buf(b % 2), in_=io["w1"][b]).then_inc(
                sem["wA" if b % 2 == 0 else "wB"], 16)
            g.dma_start(out=w3buf(b % 2), in_=io["w3"][b]).then_inc(
                sem["wC" if b % 2 == 0 else "wD"], 16)
    ctx.cnt["wA"] = wb[0] + 16 * 32
    ctx.cnt["wB"] = wb[1] + 16 * 32
    ctx.cnt["wC"] = kb_[0] + 16 * 32
    ctx.cnt["wD"] = kb_[1] + 16 * 32

    @block.tensor
    def _(te):
        for b in range(NB):
            par = "A" if b % 2 == 0 else "B"
            te.wait_ge(sem["w" + par], wb[b % 2] + 16 * (b // 2 + 1))
            for ts in range(4):
                if b >= 1:
                    te.wait_ge(sem["ev"], ev0 + (b - 1) * 8 + 4 + ts + 1)
                for kc in range(KC):
                    last = te.matmul(pf[ts][:, 0:512], w1buf(b % 2)[:, kc * 128:(kc + 1) * 128],
                                     src(kc, ts), start=(kc == 0), stop=(kc == KC - 1))
                last.then_inc(sem["mm"], 1)
            te.wait_ge(sem["wC" if b % 2 == 0 else "wD"], kb_[b % 2] + 16 * (b // 2 + 1))
            for ts in range(4):
                te.wait_ge(sem["ev"], ev0 + b * 8 + ts + 1)
                for kc in range(KC):
                    last = te.matmul(pf[ts][:, 0:512], w3buf(b % 2)[:, kc * 128:(kc + 1) * 128],
                                     src(kc, ts), start=(kc == 0), stop=(kc == KC - 1))
                last.then_inc(sem["mm"], 1)
            for ts in range(4):
                te.wait_ge(sem["dv"], dv0 + b * 4 + ts + 1)
                te.matmul(pf[ts][:, 0:512], R["h128sb"][:],
                          hbuf(b % 2)[:, ts * 512:(ts + 1) * 512],
                          start=True, stop=True).then_inc(sem["mm"], 1)
    ctx.cnt["mm"] = mm0 + NB * 12

    @block.scalar
    def _(sc):
        for b in range(NB):
            for ts in range(4):
                sc.wait_ge(sem["mm"], mm0 + b * 12 + ts + 1)
                if b >= 1:
                    sc.wait_ge(sem["dv"], dv0 + (b - 1) * 4 + ts + 1)
                sc.activation(sg[:, ts * 512:(ts + 1) * 512], pf[ts][:, 0:512],
                              AF.Silu, bias=0.0, scale=1.0).then_inc(sem["ev"], 1)
            for ts in range(4):
                sc.wait_ge(sem["mm"], mm0 + b * 12 + 8 + ts + 1)
                if b >= 2 and ts == 0:
                    sc.wait_ge(sem["s" + ("A" if b % 2 == 0 else "B")],
                               sb_[b % 2] + 16 * (b // 2))
                sc.activation(habuf(b % 2)[:, ts * 512:(ts + 1) * 512], pf[ts][:, 0:512],
                              AF.Copy, bias=0.0, scale=1.0).then_inc(sem["ev"], 1)
    ctx.cnt["ev"] = ev0 + NB * 8

    @block.vector
    def _(v):
        for b in range(NB):
            for ts in range(4):
                v.wait_ge(sem["mm"], mm0 + b * 12 + 4 + ts + 1)
                v.wait_ge(sem["ev"], ev0 + b * 8 + ts + 1)
                v.tensor_tensor(hbuf(b % 2)[:, ts * 512:(ts + 1) * 512], pf[ts][:, 0:512],
                                sg[:, ts * 512:(ts + 1) * 512], OP.mult).then_inc(sem["dv"], 1)
    ctx.cnt["dv"] = dv0 + NB * 4

    @block.sync
    def _(s):
        for b in range(NB):
            s.wait_ge(sem["ev"], ev0 + b * 8 + 8)
            s.dma_start(out=h2[b * 128:(b + 1) * 128, :], in_=habuf(b % 2)[:]
                        ).then_inc(sem["s" + ("A" if b % 2 == 0 else "B")], 16)
    ctx.cnt["sA"] = sb_[0] + 16 * 32
    ctx.cnt["sB"] = sb_[1] + 16 * 32
    ctx.barrier()

    h2v = h2.rearrange("(b i) t -> b i t", b=64)
    xin_t = io["x_in"].rearrange("(mc p) (tg t) -> mc tg p t", p=128, tg=4)
    xout_t = io["x_out"].rearrange("(mc p) (tg t) -> mc tg p t", p=128, tg=4)

    for tg in range(4):
        q0 = [ctx.cnt["qld"], ctx.cnt["gld"]]
        mmb = ctx.cnt["mm"]
        evb = ctx.cnt["ev"]

        @block.sync
        def _(s):
            if tg == 0:
                s.wait_ge(sem["sA"], ctx.cnt["sA"])
                s.wait_ge(sem["sB"], ctx.cnt["sB"])
            for gp in range(8):
                if gp >= 2:
                    s.wait_ge(sem["mm"], mmb + 8 * (gp - 1))
                sm = "qld" if gp % 2 == 0 else "gld"
                for half in range(2):
                    s.dma_start(
                        out=buf2[64 * half:64 * half + 64,
                                 gp * 4096:(gp + 1) * 4096].rearrange("p (i t) -> p i t", i=8),
                        in_=h2v[:, gp * 8 + 64 * half: gp * 8 + 64 * half + 8,
                                tg * 512:(tg + 1) * 512],
                    ).then_inc(sem[sm], 16)
        ctx.cnt["qld"] = q0[0] + 16 * 8
        ctx.cnt["gld"] = q0[1] + 16 * 8

        @block.tensor
        def _(te):
            for c in range(64):
                gp = c // 8
                if c % 8 == 0:
                    te.wait_ge(sem["qld" if gp % 2 == 0 else "gld"],
                               q0[gp % 2] + 32 * (gp // 2 + 1))
                if c >= 4:
                    te.wait_ge(sem["ev"], evb + c - 3)
                te.matmul(pf[c % 4][:, 0:512], R["h64sb"][:],
                          buf2[:, c * 512:(c + 1) * 512],
                          start=True, stop=True).then_inc(sem["mm"], 1)
        ctx.cnt["mm"] = mmb + 64

        @block.scalar
        def _(sc):
            for c in range(64):
                sc.wait_ge(sem["mm"], mmb + c + 1)
                sc.activation(buf2[:, c * 512:(c + 1) * 512], pf[c % 4][:, 0:512],
                              AF.Copy, bias=0.0, scale=FWHT_SCALE).then_inc(sem["ev"], 1)
        ctx.cnt["ev"] = evb + 64
        ctx.barrier()

        wb2 = [ctx.cnt["wA"], ctx.cnt["wB"]]
        mm2 = ctx.cnt["mm"]
        dv2 = ctx.cnt["dv"]
        xb2 = [ctx.cnt["xsA"], ctx.cnt["xsB"]]
        sb2 = [ctx.cnt["sA"], ctx.cnt["sB"]]

        @block.gpsimd
        def _(g):
            for i2 in range(32):
                mc, half = divmod(i2, 2)
                if i2 >= 2:
                    g.wait_ge(sem["mm"], mm2 + i2 - 1)
                g.dma_start(out=R["scrA"][:, (i2 % 2) * 4096:(i2 % 2 + 1) * 4096],
                            in_=io["w2"][mc, half]).then_inc(
                    sem["wA" if i2 % 2 == 0 else "wB"], 16)
        ctx.cnt["wA"] = wb2[0] + 16 * 16
        ctx.cnt["wB"] = wb2[1] + 16 * 16

        @block.sync
        def _(s):
            s.wait_ge(sem["sA"], sb2[0])
            s.wait_ge(sem["sB"], sb2[1])
            for idx in range(2):
                s.dma_start(out=R["ocol"][:, (idx % 2) * 512:(idx % 2) * 512 + 512],
                            in_=xin_t[idx, tg]).then_inc(sem["xsA" if idx % 2 == 0 else "xsB"], 16)
        ctx.cnt["xsA"] = xb2[0] + 16
        ctx.cnt["xsB"] = xb2[1] + 16

        @block.tensor
        def _(te):
            for i2 in range(32):
                mc, half = divmod(i2, 2)
                te.wait_ge(sem["wA" if i2 % 2 == 0 else "wB"], wb2[i2 % 2] + 16 * (i2 // 2 + 1))
                if half == 0 and mc >= 4:
                    te.wait_ge(sem["dv"], dv2 + mc - 3)
                for c2 in range(32):
                    last = te.matmul(pf[mc % 4][:, 0:512],
                                     R["scrA"][:, (i2 % 2) * 4096 + c2 * 128:(i2 % 2) * 4096 + (c2 + 1) * 128],
                                     buf2[:, (half * 32 + c2) * 512:(half * 32 + c2) * 512 + 512],
                                     start=(half == 0 and c2 == 0), stop=(half == 1 and c2 == 31))
                last.then_inc(sem["mm"], 1)
        ctx.cnt["mm"] = mm2 + 32

        for mc in range(16):
            xp = "xsA" if mc % 2 == 0 else "xsB"
            sp = "sA" if mc % 2 == 0 else "sB"

            @block.vector
            def _(v, mc=mc, xp=xp):
                v.wait_ge(sem["mm"], mm2 + 2 * mc + 2)
                v.wait_ge(sem[xp], xb2[mc % 2] + 16 * (mc // 2 + 1))
                v.tensor_tensor(R["ocol"][:, (mc % 2) * 512:(mc % 2) * 512 + 512],
                                pf[mc % 4][:, 0:512],
                                R["ocol"][:, (mc % 2) * 512:(mc % 2) * 512 + 512],
                                OP.add).then_inc(sem["dv"], 1)
            ctx.cnt["dv"] += 1

            @block.sync
            def _(s, mc=mc, xp=xp, sp=sp):
                s.wait_ge(sem["dv"], dv2 + mc + 1)
                s.dma_start(out=xout_t[mc, tg],
                            in_=R["ocol"][:, (mc % 2) * 512:(mc % 2) * 512 + 512]
                            ).then_inc(sem[sp], 16)
                if mc + 2 < 16:
                    s.wait_ge(sem[sp], sb2[mc % 2] + 16 * (mc // 2 + 1))
                    s.dma_start(out=R["ocol"][:, (mc % 2) * 512:(mc % 2) * 512 + 512],
                                in_=xin_t[mc + 2, tg]).then_inc(sem[xp], 16)
            ctx.cnt[sp] += 16
            if mc + 2 < 16:
                ctx.cnt[xp] += 16
        ctx.barrier()


def build_l0_ffn():
    nc = bass.Bass("TRN2", target_bir_lowering=False)
    io = {}
    io["x_in"] = nc.dram_tensor("x_in", [D, T], F32, kind="ExternalInput").ap()
    io["x_out"] = nc.dram_tensor("x_out", [D, T], F32, kind="ExternalOutput").ap()
    io["ident"] = nc.dram_tensor("ident", [128, 128], F32, kind="ExternalInput").ap()
    io["h128"] = nc.dram_tensor("h128", [128, 128], F32, kind="ExternalInput").ap()
    io["h64b"] = nc.dram_tensor("h64b", [128, 128], F32, kind="ExternalInput").ap()
    io["w1"] = nc.dram_tensor("w1", [64, 128, KC * 128], F32, kind="ExternalInput").ap()
    io["w3"] = nc.dram_tensor("w3", [64, 128, KC * 128], F32, kind="ExternalInput").ap()
    io["w2"] = nc.dram_tensor("w2", [16, 2, 128, 32 * 128], F32, kind="ExternalInput").ap()
    dram = {"h2": nc.dram_tensor("h2", [64 * 128, T], BF16)}
    with contextlib.ExitStack() as stack:
        sems, R = alloc_shared(nc, stack, kind="l0f")
        with nc.Block() as block:
            ctx = Ctx(nc, block, sems)
            ctx.dummy = R["constf"][0:1, 2:4]
            emit_const_loads(ctx, R, io, kind="l0f")
            emit_l0_ffn_block(ctx, R, io, dram)

            @block.sync
            def _(s):
                s.wait_ge(sems["sA"], ctx.cnt["sA"])
                s.wait_ge(sems["sB"], ctx.cnt["sB"])
    return nc


def w2_tiles(W2):
    W2 = np.asarray(W2)
    hidx = np.zeros((64, 128), dtype=np.int64)
    for c in range(64):
        p = np.arange(128)
        b = np.where(p < 64, p, p - 64)
        i = np.where(p < 64, c, c + 64)
        hidx[c] = b * 128 + i
    out = np.zeros((16, 2, 128, 32 * 128), np.float32)
    for mc in range(16):
        blk = W2[mc * 128:(mc + 1) * 128]
        for c in range(64):
            t = blk[:, hidx[c]].T
            half, c2 = divmod(c, 32)
            out[mc, half, :, c2 * 128:(c2 + 1) * 128] = t
    return np.ascontiguousarray(out)


def hadamard(n):
    h = np.array([[1.0]], dtype=np.float32)
    while h.shape[0] < n:
        h = np.block([[h, h], [h, -h]])
    return h


_BUILD_CACHE = {}


def run_l0_ffn(x_shards, W1, W2, W3):
    if "l0f" not in _BUILD_CACHE:
        _BUILD_CACHE["l0f"] = build_l0_ffn()
    nc = _BUILD_CACHE["l0f"]
    wt = dict(w1=lhsT_tiles(np.asarray(W1)),
              w3=lhsT_tiles(np.asarray(W3)),
              w2=w2_tiles(W2),
              ident=np.eye(128, dtype=np.float32),
              h128=hadamard(128),
              h64b=np.kron(np.eye(2, dtype=np.float32), hadamard(64)))
    in_maps = []
    for c in range(8):
        m = dict(x_in=x_shards[c])
        m.update(wt)
        in_maps.append(m)
    res = run_bass_kernel_spmd(nc, in_maps, core_ids=list(range(8)))
    return [res.results[c]["x_out"] for c in range(8)]



EXPSHIFT = -20.0
HSCALE = float(1.0 / np.sqrt(128.0))


def emit_l1_qk_pass(ctx, R, Wqk_ap, q_view, k_view):
    block, sem = ctx.block, ctx.sem
    pf = [R[f"pf{i}"] for i in range(4)]
    pt = [R[f"pt{i}"] for i in range(4)]
    xbf, buf2 = R["xbf"], R["buf2"]
    wbuf = R["scrA"][:, 0:4096]
    stage = lambda h: buf2[:, h * 2048:(h + 1) * 2048]
    src = lambda kc, ts: xbf[:, kc * T + ts * 512: kc * T + ts * 512 + 512]
    rsc = R["ocol"]

    mm0 = ctx.cnt["mm"]
    ev0 = ctx.cnt["ev"]
    dv0 = ctx.cnt["dv"]
    wb = [ctx.cnt["wA"], ctx.cnt["wB"]]
    sb_ = [ctx.cnt["sA"], ctx.cnt["sB"]]

    @block.gpsimd
    def _(g):
        for mc in range(32):
            if mc >= 2:
                g.wait_ge(sem["mm"], mm0 + (mc - 1) * 8)
            g.dma_start(out=wbuf[:, (mc % 2) * 2048:(mc % 2 + 1) * 2048],
                        in_=Wqk_ap[mc]).then_inc(sem["wA" if mc % 2 == 0 else "wB"], 16)
    ctx.cnt["wA"] = wb[0] + 16 * 16
    ctx.cnt["wB"] = wb[1] + 16 * 16

    @block.tensor
    def _(te):
        for mc in range(32):
            te.wait_ge(sem["wA" if mc % 2 == 0 else "wB"], wb[mc % 2] + 16 * (mc // 2 + 1))
            for ts in range(4):
                if mc >= 1:
                    te.wait_ge(sem["dv"], dv0 + (mc - 1) * 12 + ts * 3 + 2)
                for kc in range(KC):
                    last = te.matmul(pf[ts][:, 0:512],
                                     wbuf[:, (mc % 2) * 2048 + kc * 128:(mc % 2) * 2048 + (kc + 1) * 128],
                                     src(kc, ts), start=(kc == 0), stop=(kc == KC - 1))
                last.then_inc(sem["mm"], 1)
            for ts in range(4):
                te.wait_ge(sem["ev"], ev0 + mc * 4 + ts + 1)
                te.matmul(pf[ts][0:32, 0:512], R["permSB"][0:32, 0:32],
                          stage(mc % 2)[0:32, ts * 512:(ts + 1) * 512],
                          start=True, stop=True).then_inc(sem["mm"], 1)
    ctx.cnt["mm"] = mm0 + 32 * 8

    @block.scalar
    def _(sc):
        for mc in range(32):
            for ts in range(4):
                sc.wait_ge(sem["mm"], mm0 + mc * 8 + ts + 1)
                if mc >= 2 and ts == 0:
                    sc.wait_ge(sem["s" + ("A" if mc % 2 == 0 else "B")],
                               sb_[mc % 2] + 16 * (mc // 2))
                sc.activation(stage(mc % 2)[:, ts * 512:(ts + 1) * 512], pf[ts][:, 0:512],
                              AF.Copy, bias=0.0, scale=1.0).then_inc(sem["ev"], 1)
    ctx.cnt["ev"] = ev0 + 32 * 4

    @block.vector
    def _(v):
        for mc in range(32):
            for ts in range(4):
                v.wait_ge(sem["mm"], mm0 + mc * 8 + 4 + ts + 1)
                v.wait_ge(sem["dv"], ctx.cnt["dv"])
                v.tensor_tensor(rsc[0:32, 0:512], stage(mc % 2)[0:32, ts * 512:(ts + 1) * 512],
                                R["cosSB"][:, ts * 512:(ts + 1) * 512], OP.mult).then_inc(sem["dv"], 1)
                ctx.cnt["dv"] += 1
                v.tensor_tensor(rsc[0:32, 512:1024], pf[ts][0:32, 0:512],
                                R["sinSB"][:, ts * 512:(ts + 1) * 512], OP.mult).then_inc(sem["dv"], 1)
                ctx.cnt["dv"] += 1
                v.wait_ge(sem["dv"], ctx.cnt["dv"])
                v.tensor_tensor(stage(mc % 2)[0:32, ts * 512:(ts + 1) * 512],
                                rsc[0:32, 0:512], rsc[0:32, 512:1024], OP.add).then_inc(sem["dv"], 1)
                ctx.cnt["dv"] += 1
    assert ctx.cnt["dv"] == dv0 + 32 * 12

    @block.sync
    def _(s):
        for mc in range(32):
            s.wait_ge(sem["dv"], dv0 + mc * 12 + 12)
            dst = q_view[mc] if mc < 16 else k_view[mc - 16]
            s.dma_start(out=dst.rearrange("ts p t -> p ts t"),
                        in_=stage(mc % 2)[:].rearrange("p (ts t) -> p ts t", ts=4)
                        ).then_inc(sem["s" + ("A" if mc % 2 == 0 else "B")], 16)
    ctx.cnt["sA"] = sb_[0] + 16 * 16
    ctx.cnt["sB"] = sb_[1] + 16 * 16


def emit_l1_attention(ctx, R, dram):
    block, sem = ctx.block, ctx.sem
    pf = [R[f"pf{i}"] for i in range(4)]
    xbf, buf2, scrA = R["xbf"], R["buf2"], R["scrA"]

    kout_v = dram["k_out"].ap().rearrange("(rk mc ts r) (dd t) -> rk mc ts (r dd) t",
                                          rk=2, mc=16, ts=4, r=32, dd=4, t=512)
    vout_v = dram["v_out"].ap().rearrange("(rk pu p) e -> rk pu p e", rk=2, pu=16)
    qtile_v = tiles4(dram["q_t"].ap(), 16)

    qtile = buf2[:, 0:8192]
    pT = buf2[:, 8192:8192 + 32 * 512]
    vcol = lambda h: buf2[:, 24576 + (h % 2) * 4096: 24576 + (h % 2) * 4096 + 4096]
    kb = lambda h: scrA[:, (h % 2) * 4096:((h % 2) + 1) * 4096]
    rlbc = R["gbuf"][:, 0:512]
    rlrow = R["rowbf"][0:1, 0:512]
    mm_hist = []

    for p in range(NBK):
        nblk = p + 1
        nslot = 8 * nblk
        q0 = ctx.cnt["qld"]

        @block.sync
        def _(s):
            if p == 0:
                s.wait_ge(sem["sA"], ctx.cnt["sA"])
                s.wait_ge(sem["sB"], ctx.cnt["sB"])
                s.wait_ge(sem["st"], ctx.cnt["st"])
            s.dma_start(out=qtile.rearrange("p (mc t) -> p mc t", mc=16),
                        in_=qtile_v[:, p].rearrange("mc p t -> p mc t")).then_inc(sem["qld"], 16)
        ctx.cnt["qld"] = q0 + 16

        for h in range(NH):
            kvp = "kvA" if h % 2 == 0 else "kvB"
            vp = "vA" if h % 2 == 0 else "vB"
            kvb = ctx.cnt[kvp]
            vb = ctx.cnt[vp]
            mmb = ctx.cnt["mm"]
            evb = ctx.cnt["ev"]
            dvb = ctx.cnt["dv"]

            @block.sync
            def _(s, h=h, p=p, kvp=kvp, vp=vp, mmb=mmb):
                if len(mm_hist) >= 2:
                    s.wait_ge(sem["mm"], mm_hist[-2])
                for rk in range(2):
                    s.dma_start(out=kb(h)[:, rk * nblk * 512:(rk + 1) * nblk * 512
                                          ].rearrange("p (b t) -> p b t", b=nblk),
                                in_=kout_v[rk, h, 0:nblk].rearrange("b p t -> p b t")
                                ).then_inc(sem[kvp], 16)
                for rk in range(2):
                    for bh in range(2):
                        n1 = (nblk * 4) // 2 if nblk * 4 >= 2 else nblk * 4
                        lo = bh * n1
                        hi = nblk * 4 if bh == 1 else n1
                        if lo >= hi:
                            continue
                        s.dma_start(
                            out=vcol(h)[:, (rk * 4 * nblk + lo) * 128:(rk * 4 * nblk + hi) * 128
                                        ].rearrange("p (sl e) -> p sl e", e=128),
                            in_=vout_v[rk, lo:hi, :, h * 128:(h + 1) * 128
                                       ].rearrange("sl p e -> p sl e"),
                        ).then_inc(sem[vp], 16)

            nvd = sum(1 for rk in range(2) for bh in range(2)
                      if (bh * ((nblk * 4) // 2 if nblk * 4 >= 2 else nblk * 4)) <
                      (nblk * 4 if bh == 1 else ((nblk * 4) // 2 if nblk * 4 >= 2 else nblk * 4)))
            ctx.cnt[kvp] = kvb + 32
            ctx.cnt[vp] = vb + 16 * nvd

            @block.tensor
            def _(te, h=h, p=p, nslot=nslot, kvb=kvb, evb=evb, mmb=mmb, q0=q0):
                te.wait_ge(sem["qld"], q0 + 16)
                te.wait_ge(sem[kvp], kvb + 32)
                for sl in range(nslot):
                    if sl >= 2:
                        te.wait_ge(sem["ev"], evb + sl - 1)
                    te.matmul(pf[sl % 2][:, 0:512], kb(h)[:, sl * 128:(sl + 1) * 128],
                              qtile[:, h * 512:(h + 1) * 512],
                              start=True, stop=True).then_inc(sem["mm"], 1)
            ctx.cnt["mm"] = mmb + nslot

            @block.scalar
            def _(sc, nslot=nslot, mmb=mmb, evb=evb):
                for sl in range(nslot):
                    sc.wait_ge(sem["mm"], mmb + sl + 1)
                    sc.activation(pT[:, sl * 512:(sl + 1) * 512], pf[sl % 2][:, 0:512],
                                  AF.Exp, bias=R["constf"][0:128, 4:5], scale=HSCALE
                                  ).then_inc(sem["ev"], 1)
            ctx.cnt["ev"] = evb + nslot

            @block.vector
            def _(v, nblk=nblk, evb=evb, nslot=nslot):
                v.wait_ge(sem["ev"], evb + nslot)
                for rk in range(2):
                    msk = R["mska"] if rk == 0 else R["mskb"]
                    for u in range(4):
                        slot = rk * 4 * nblk + (nblk - 1) * 4 + u
                        v.tensor_tensor(pT[:, slot * 512:(slot + 1) * 512],
                                        pT[:, slot * 512:(slot + 1) * 512],
                                        msk[:, u * 512:(u + 1) * 512], OP.mult
                                        ).then_inc(sem["dv"], 1)
            ctx.cnt["dv"] = dvb + 8

            mml = ctx.cnt["mm"]

            @block.tensor
            def _(te, nslot=nslot, dvb=dvb, mml=mml):
                te.wait_ge(sem["dv"], dvb + 8)
                te.wait_ge(sem["ev"], ctx.cnt["ev"])
                for sl in range(nslot):
                    last = te.matmul(pf[2][0:1, 0:512], R["onec"][:],
                                     pT[:, sl * 512:(sl + 1) * 512],
                                     start=(sl == 0), stop=(sl == nslot - 1))
                last.then_inc(sem["mm"], 1)
            ctx.cnt["mm"] = mml + 1

            dvl = ctx.cnt["dv"]

            @block.vector
            def _(v, mml=mml, dvl=dvl):
                v.wait_ge(sem["mm"], mml + 1)
                v.reciprocal(R["statf"][0:1, 0:512], pf[2][0:1, 0:512]).then_inc(sem["dv"], 1)
                v.wait_ge(sem["dv"], dvl + 1)
                v.tensor_copy(rlrow, R["statf"][0:1, 0:512]).then_inc(sem["dv"], 1)
            ctx.cnt["dv"] = dvl + 2

            mmr = ctx.cnt["mm"]

            @block.tensor
            def _(te, dvl=dvl, mmr=mmr):
                te.wait_ge(sem["dv"], dvl + 2)
                te.matmul(pf[2][:, 0:512], R["oner"][:], rlrow,
                          start=True, stop=True).then_inc(sem["mm"], 1)
            ctx.cnt["mm"] = mmr + 1

            evr = ctx.cnt["ev"]

            @block.scalar
            def _(sc, mmr=mmr):
                sc.wait_ge(sem["mm"], mmr + 1)
                sc.activation(rlbc, pf[2][:, 0:512], AF.Copy, bias=0.0, scale=1.0
                              ).then_inc(sem["ev"], 1)
            ctx.cnt["ev"] = evr + 1

            mma = ctx.cnt["mm"]

            @block.tensor
            def _(te, nslot=nslot, vb=vb, mma=mma, h=h, nvd=nvd):
                te.wait_ge(sem[vp], vb + 16 * nvd)
                for sl in range(nslot):
                    last = te.matmul(pf[3][:, 0:512], vcol(h)[:, sl * 128:(sl + 1) * 128],
                                     pT[:, sl * 512:(sl + 1) * 512],
                                     start=(sl == 0), stop=(sl == nslot - 1))
                last.then_inc(sem["mm"], 1)
            ctx.cnt["mm"] = mma + 1

            @block.vector
            def _(v, mma=mma, evr=evr, h=h, p=p):
                v.wait_ge(sem["mm"], mma + 1)
                v.wait_ge(sem["ev"], evr + 1)
                v.tensor_tensor(xbf[:, h * T + p * 512: h * T + (p + 1) * 512],
                                pf[3][:, 0:512], rlbc, OP.mult).then_inc(sem["dv"], 1)
            ctx.cnt["dv"] += 1
            mm_hist.append(ctx.cnt["mm"])
        ctx.barrier()


def emit_l1_attn_block(ctx, R, io, dram):
    block, sem = ctx.block, ctx.sem
    xbf = R["xbf"]

    @block.gpsimd
    def _(g):
        g.dma_start(out=xbf[:].rearrange("p (kc t) -> p kc t", kc=KC),
                    in_=io["x_in"].rearrange("(kc p) t -> p kc t", p=128)).then_inc(sem["xs"], 16)
    ctx.cnt["xs"] += 16

    @block.tensor
    def _(te):
        te.wait_ge(sem["xs"], ctx.cnt["xs"])
        te.wait_ge(sem["aux"], ctx.cnt["aux"])
        te.wait_ge(sem["ax2"], ctx.cnt["ax2"])
    ctx.barrier()

    emit_ln_affine(ctx, R, "ln")
    ctx.barrier()

    qt4 = tiles4(dram["q_t"].ap(), 16)
    kt4 = tiles4(dram["k_in"].ap(), 16)
    emit_l1_qk_pass(ctx, R, io["wqk"], qt4, kt4)
    ctx.barrier()
    emit_l0_v_pass(ctx, R, io["wv"], dram["v_in"].ap(), xbf, act_fn=AF.Copy)

    @block.gpsimd
    def _(g):
        g.wait_ge(sem["st"], ctx.cnt["st"])
        g.wait_ge(sem["sA"], ctx.cnt["sA"])
        g.wait_ge(sem["sB"], ctx.cnt["sB"])
        g.collective_compute("AllGather", OP.bypass, replica_groups=PAIRS,
                             ins=[dram["k_in"].ap().opt()], outs=[dram["k_out"].ap().opt()]
                             ).then_inc(sem["cc"])
        g.collective_compute("AllGather", OP.bypass, replica_groups=PAIRS,
                             ins=[dram["v_in"].ap().opt()], outs=[dram["v_out"].ap().opt()]
                             ).then_inc(sem["cc"])
        g.wait_ge(sem["cc"], ctx.cnt["cc"] + 2)
    ctx.cnt["cc"] += 2
    ctx.barrier()

    emit_l1_attention(ctx, R, dram)
    emit_outproj_residual(ctx, R, io["wout"], xbf, io["x_in"], io["x_out"])
    ctx.barrier()


def build_l1_attn():
    nc = bass.Bass("TRN2", target_bir_lowering=False)
    io = {}
    io["x_in"] = nc.dram_tensor("x_in", [D, T], F32, kind="ExternalInput").ap()
    io["x_out"] = nc.dram_tensor("x_out", [D, T], F32, kind="ExternalOutput").ap()
    for nm, shp in [("ident", [128, 128]), ("maskA", [4, 128, BS]), ("maskB", [4, 128, BS]),
                    ("cosT", [32, T]), ("sinT", [32, T]), ("perm32", [128, 128])]:
        io[nm] = nc.dram_tensor(nm, shp, F32, kind="ExternalInput").ap()
    io["wqk"] = nc.dram_tensor("wqk", [32, 128, KC * 128], F32, kind="ExternalInput").ap()
    io["wv"] = nc.dram_tensor("wv", [16, 128, KC * 128], F32, kind="ExternalInput").ap()
    io["wout"] = nc.dram_tensor("wout", [16, 128, KC * 128], F32, kind="ExternalInput").ap()
    dram = {
        "k_in": nc.dram_tensor("k_cc_in", [D, T], BF16),
        "k_out": nc.dram_tensor("k_cc_out", [2 * D, T], BF16),
        "v_in": nc.dram_tensor("v_cc_in", [T, D], BF16),
        "v_out": nc.dram_tensor("v_cc_out", [2 * T, D], BF16),
        "q_t": nc.dram_tensor("q_t", [D, T], BF16),
    }
    with contextlib.ExitStack() as stack:
        sems, R = alloc_shared(nc, stack, kind="l1a")
        with nc.Block() as block:
            ctx = Ctx(nc, block, sems)
            ctx.dummy = R["constf"][0:1, 2:4]

            @block.gpsimd
            def _(g):
                g.memset(R["constf"][:, 4:5], EXPSHIFT).then_inc(sems["ax2"], 1)
            ctx.cnt["ax2"] += 1

            emit_const_loads(ctx, R, io, kind="l1a")
            emit_l1_attn_block(ctx, R, io, dram)

            @block.sync
            def _(s):
                s.wait_ge(sems["sA"], ctx.cnt["sA"])
                s.wait_ge(sems["sB"], ctx.cnt["sB"])
    return nc


def rot_tables(r):
    ROT = 32
    inv_freq = 1.0 / (10000.0 ** (np.arange(0, ROT, 2, dtype=np.float64) / ROT))
    pos_all = np.arange(4096, dtype=np.float64)[:, None] * inv_freq[None, :]
    pos_all = np.concatenate([pos_all, pos_all], axis=1)
    loc = np.concatenate([np.arange(g * BS, (g + 1) * BS) for g in core_blocks(r)])
    pos = pos_all[loc]
    return (np.ascontiguousarray(np.cos(pos).T.astype(np.float32)),
            np.ascontiguousarray(np.sin(pos).T.astype(np.float32)))


def perm32_mat():
    P = np.zeros((32, 32), dtype=np.float32)
    for j in range(16):
        P[j, j + 16] = -1.0
        P[j + 16, j] = 1.0
    L = np.zeros((128, 128), dtype=np.float32)
    L[0:32, 0:32] = P.T
    return L


def run_l1_attn(x_shards, w_qkv, w_out):
    if "l1a" not in _BUILD_CACHE:
        _BUILD_CACHE["l1a"] = build_l1_attn()
    nc = _BUILD_CACHE["l1a"]
    qkv_t = lhsT_tiles(np.asarray(w_qkv))
    wt = dict(wqk=np.ascontiguousarray(qkv_t[0:32]),
              wv=np.ascontiguousarray(qkv_t[32:48]),
              wout=lhsT_tiles(np.asarray(w_out)),
              ident=np.eye(128, dtype=np.float32),
              perm32=perm32_mat())
    in_maps = []
    for c in range(8):
        r = c % 2
        mA, mB = make_masks(r)
        cosT, sinT = rot_tables(r)
        m = dict(x_in=x_shards[c], maskA=mA, maskB=mB, cosT=cosT, sinT=sinT)
        m.update(wt)
        in_maps.append(m)
    res = run_bass_kernel_spmd(nc, in_maps, core_ids=list(range(8)))
    return [res.results[c]["x_out"] for c in range(8)]



def emit_l1_ffn_block(ctx, R, io, dram):
    block, sem = ctx.block, ctx.sem
    xbf, buf2, scrA = R["xbf"], R["buf2"], R["scrA"]
    pf = [R[f"pf{i}"] for i in range(4)]
    pt = [R[f"pt{i}"] for i in range(4)]
    statf = R["statf"]
    S = lambda i: statf[0:1, i * 2048:(i + 1) * 2048]

    @block.gpsimd
    def _(g):
        g.dma_start(out=xbf[:].rearrange("p (kc t) -> p kc t", kc=KC),
                    in_=io["x_in"].rearrange("(kc p) t -> p kc t", p=128)).then_inc(sem["xs"], 16)
        g.dma_start(out=R["gcol"][:, 0:16], in_=io["gamma"].rearrange("(kc p) -> p kc", p=128),
                    allow_slow_non_contiguous=True).then_inc(sem["xs"], 16)
        g.dma_start(out=R["gcol"][:, 16:32], in_=io["beta"].rearrange("(kc p) -> p kc", p=128),
                    allow_slow_non_contiguous=True).then_inc(sem["xs"], 16)
        g.dma_start(out=R["srow"][0:1, 0:1], in_=io["scale"].rearrange("(one a) -> one a", one=1)
                    ).then_inc(sem["xs"], 16)
    ctx.cnt["xs"] += 64

    @block.tensor
    def _(te):
        te.wait_ge(sem["xs"], ctx.cnt["xs"])
        te.wait_ge(sem["aux"], ctx.cnt["aux"])
        te.wait_ge(sem["ax2"], ctx.cnt["ax2"])
    ctx.barrier()

    emit_ln_affine(ctx, R, "ln_ln")

    @block.vector
    def _(v):
        v.wait_ge(sem["dv"], ctx.cnt["dv"])
        for kc in range(KC):
            v.tensor_scalar(xbf[:, kc * T:(kc + 1) * T], xbf[:, kc * T:(kc + 1) * T],
                            R["gcol"][:, kc:kc + 1], R["gcol"][:, 16 + kc:17 + kc],
                            OP.mult, OP.add).then_inc(sem["dv"], 1)
            ctx.cnt["dv"] += 1
    ctx.barrier()

    mm0 = ctx.cnt["mm"]
    dv0 = ctx.cnt["dv"]
    sqb = buf2

    @block.scalar
    def _(sc):
        for kc in range(KC):
            if kc >= 2:
                sc.wait_ge(sem["mm"], mm0 + kc - 1)
            sc.activation(sqb[:, (kc % 2) * T:(kc % 2 + 1) * T], xbf[:, kc * T:(kc + 1) * T],
                          AF.Square, bias=0.0, scale=1.0).then_inc(sem["dv"], 1)
    ctx.cnt["dv"] = dv0 + KC

    @block.tensor
    def _(te):
        for kc in range(KC):
            te.wait_ge(sem["dv"], dv0 + kc + 1)
            for ts in range(4):
                last = te.matmul(pf[ts][0:1, 0:512], R["onec"][:],
                                 sqb[:, (kc % 2) * T + ts * 512:(kc % 2) * T + ts * 512 + 512],
                                 start=(kc == 0), stop=(kc == KC - 1))
            last.then_inc(sem["mm"], 1)
    ctx.cnt["mm"] = mm0 + KC

    ev0 = ctx.cnt["ev"]

    @block.scalar
    def _(sc):
        sc.wait_ge(sem["mm"], mm0 + KC)
        for ts in range(4):
            last = sc.activation(S(0)[:, ts * 512:(ts + 1) * 512], pf[ts][0:1, 0:512],
                                 AF.Copy, bias=0.0, scale=1.0)
        last.then_inc(sem["ev"], 1)
    ctx.cnt["ev"] = ev0 + 1

    def vstep(op, *args):
        @block.vector
        def _(v):
            v.wait_ge(sem["dv"], ctx.cnt["dv"])
            getattr(v, op)(*args).then_inc(sem["dv"], 1)
        ctx.cnt["dv"] += 1

    @block.scalar
    def _(sc):
        sc.wait_ge(sem["ev"], ev0 + 1)
        sc.activation(S(0), S(0), AF.Sqrt, bias=0.0, scale=1.0).then_inc(sem["dv"], 1)
        ctx.cnt["dv"] += 1
        sc.wait_ge(sem["dv"], ctx.cnt["dv"])
        sc.activation(R["srow"][0:1, 1:2], R["srow"][0:1, 0:1], AF.Abs,
                      bias=0.0, scale=1.0).then_inc(sem["dv"], 1)
        ctx.cnt["dv"] += 1
        sc.wait_ge(sem["dv"], ctx.cnt["dv"])
        sc.activation(R["srow"][0:1, 2:3], R["srow"][0:1, 1:2], AF.Copy,
                      bias=0.0, scale=-1.0).then_inc(sem["dv"], 1)
        ctx.cnt["dv"] += 1
    vstep("tensor_scalar_max", S(0), S(0), 1e-12)
    vstep("reciprocal", S(0), S(0))
    @block.vector
    def _(v):
        v.wait_ge(sem["dv"], ctx.cnt["dv"])
        v.tensor_scalar(S(0), S(0), R["srow"][0:1, 0:1], None, OP.mult).then_inc(sem["dv"], 1)
        ctx.cnt["dv"] += 1

    mmc = ctx.cnt["mm"]
    dvc = ctx.cnt["dv"]
    vstep("tensor_copy", R["rowbf"][0:1, 0:2048], S(0))

    evc = ctx.cnt["ev"]
    mmc = ctx.cnt["mm"]

    @block.tensor
    def _(te):
        te.wait_ge(sem["dv"], ctx.cnt["dv"])
        for i in range(16):
            if i >= 4:
                te.wait_ge(sem["ev"], evc + i - 3)
            te.transpose(pt[i % 4][0:128, 0:1], R["rowbf"][0:1, i * 128:(i + 1) * 128],
                         R["identb"][0:1, 0:1]).then_inc(sem["mm"], 1)
    ctx.cnt["mm"] = mmc + 16

    @block.scalar
    def _(sc):
        for i in range(16):
            sc.wait_ge(sem["mm"], mmc + i + 1)
            sc.activation(R["ccol"][:, i:i + 1], pt[i % 4][0:128, 0:1], AF.Copy,
                          bias=0.0, scale=1.0).then_inc(sem["ev"], 1)
    ctx.cnt["ev"] = evc + 16

    @block.vector
    def _(v):
        v.wait_ge(sem["dv"], ctx.cnt["dv"])
        v.tensor_copy(R["rowbf"][0:1, 2048:2049], R["srow"][0:1, 2:3]).then_inc(sem["dv"], 1)
        ctx.cnt["dv"] += 1
    mmq = ctx.cnt["mm"]

    @block.tensor
    def _(te):
        te.wait_ge(sem["dv"], ctx.cnt["dv"])
        te.matmul(pf[0][:, 0:1], R["oner"][:], R["rowbf"][0:1, 2048:2049],
                  start=True, stop=True).then_inc(sem["mm"], 1)
    ctx.cnt["mm"] = mmq + 1

    @block.scalar
    def _(sc):
        sc.wait_ge(sem["mm"], mmq + 1)
        sc.activation(R["nsb"][:, 0:1], pf[0][:, 0:1], AF.Copy, bias=0.0, scale=1.0
                      ).then_inc(sem["ev"], 1)
    ctx.cnt["ev"] += 1
    ctx.barrier()

    muv = io["mu_t"]
    mmn = ctx.cnt["mm"]
    dvn = ctx.cnt["dv"]
    evn = ctx.cnt["ev"]
    wbn = [ctx.cnt["wA"], ctx.cnt["wB"]]

    @block.gpsimd
    def _(g):
        for esc in range(16):
            if esc >= 2:
                g.wait_ge(sem["dv"], dvn + (esc - 1))
            g.dma_start(out=scrA[:, (esc % 2) * 2048:(esc % 2 + 1) * 2048],
                        in_=muv[esc]).then_inc(sem["wA" if esc % 2 == 0 else "wB"], 16)
    ctx.cnt["wA"] = wbn[0] + 16 * 8
    ctx.cnt["wB"] = wbn[1] + 16 * 8

    @block.vector
    def _(v):
        for esc in range(16):
            v.wait_ge(sem["wA" if esc % 2 == 0 else "wB"], wbn[esc % 2] + 16 * (esc // 2 + 1))
            if esc >= 2:
                v.wait_ge(sem["mm"], mmn + esc - 1)
            v.tensor_tensor(sqb[:, (esc % 2) * T:(esc % 2) * T + 2048],
                            scrA[:, (esc % 2) * 2048:(esc % 2 + 1) * 2048],
                            scrA[:, (esc % 2) * 2048:(esc % 2 + 1) * 2048],
                            OP.mult).then_inc(sem["dv"], 1)
    ctx.cnt["dv"] = dvn + 16

    @block.tensor
    def _(te):
        for esc in range(16):
            te.wait_ge(sem["dv"], dvn + esc + 1)
            if esc >= 2:
                te.wait_ge(sem["ev"], evn + esc - 1)
            for kc in range(KC):
                last = te.matmul(pf[esc % 2][0:1, 0:128], R["onec"][:],
                                 sqb[:, (esc % 2) * T + kc * 128:(esc % 2) * T + (kc + 1) * 128],
                                 start=(kc == 0), stop=(kc == KC - 1))
            last.then_inc(sem["mm"], 1)
    ctx.cnt["mm"] = mmn + 16

    @block.scalar
    def _(sc):
        for esc in range(16):
            sc.wait_ge(sem["mm"], mmn + esc + 1)
            sc.activation(S(1)[:, esc * 128:(esc + 1) * 128], pf[esc % 2][0:1, 0:128],
                          AF.Copy, bias=0.0, scale=1.0).then_inc(sem["ev"], 1)
    ctx.cnt["ev"] = evn + 16

    @block.scalar
    def _(sc):
        sc.wait_ge(sem["ev"], evn + 16)
        sc.wait_ge(sem["dv"], ctx.cnt["dv"])
        sc.activation(S(1), S(1), AF.Sqrt, bias=0.0, scale=1.0).then_inc(sem["dv"], 1)
        ctx.cnt["dv"] += 1
    vstep("tensor_scalar_max", S(1), S(1), 1e-12)
    vstep("reciprocal", S(1), S(1))
    vstep("tensor_copy", R["rowbf"][0:1, 2048:4096], S(1))

    mmg = ctx.cnt["mm"]

    @block.tensor
    def _(te):
        te.wait_ge(sem["dv"], ctx.cnt["dv"])
        for i in range(4):
            te.matmul(pf[i][:, 0:512], R["oner"][:], R["rowbf"][0:1, 2048 + i * 512:2048 + (i + 1) * 512],
                      start=True, stop=True).then_inc(sem["mm"], 1)
    ctx.cnt["mm"] = mmg + 4

    evg = ctx.cnt["ev"]

    @block.scalar
    def _(sc):
        for i in range(4):
            sc.wait_ge(sem["mm"], mmg + i + 1)
            sc.activation(R["gbuf"][:, i * 512:(i + 1) * 512], pf[i][:, 0:512], AF.Copy,
                          bias=0.0, scale=1.0).then_inc(sem["ev"], 1)
    ctx.cnt["ev"] = evg + 4
    ctx.barrier()

    mur = io["mu_r"]
    for ess in range(4):
        wbl = [ctx.cnt["wA"], ctx.cnt["wB"]]
        mml = ctx.cnt["mm"]
        evl = ctx.cnt["ev"]
        dvl = ctx.cnt["dv"]

        @block.gpsimd
        def _(g, ess=ess):
            if ess >= 1:
                g.wait_ge(sem["mm"], ctx.cnt["mm"])
            g.dma_start(out=scrA[:, 0:8192].rearrange("p (kc e) -> p kc e", kc=KC),
                        in_=mur[ess]).then_inc(sem["wA"], 16)
        ctx.cnt["wA"] = wbl[0] + 16

        @block.tensor
        def _(te, ess=ess, mml=mml, evl=evl, wbl=wbl):
            te.wait_ge(sem["wA"], wbl[0] + 16)
            for tc in range(16):
                if ess >= 1 or tc >= 4:
                    te.wait_ge(sem["ev"], evl + tc - 3)
                for kc in range(KC):
                    last = te.matmul(pf[tc % 4][:, 0:512],
                                     xbf[:, kc * T + tc * 128: kc * T + (tc + 1) * 128],
                                     scrA[:, kc * 512:(kc + 1) * 512],
                                     start=(kc == 0), stop=(kc == KC - 1))
                last.then_inc(sem["mm"], 1)
        ctx.cnt["mm"] = mml + 16

        @block.scalar
        def _(sc, ess=ess, mml=mml):
            for tc in range(16):
                sc.wait_ge(sem["mm"], mml + tc + 1)
                if tc >= 14:
                    pass
                sc.activation(buf2[:, tc * T + ess * 512: tc * T + ess * 512 + 512],
                              pf[tc % 4][:, 0:512], AF.Copy, bias=0.0, scale=1.0
                              ).then_inc(sem["ev"], 1)
        ctx.cnt["ev"] = evl + 16

        @block.vector
        def _(v, ess=ess, evl=evl, dvl=dvl):
            for tc in range(16):
                v.wait_ge(sem["ev"], evl + tc + 1)
                v.tensor_tensor(buf2[:, tc * T + ess * 512: tc * T + ess * 512 + 512],
                                buf2[:, tc * T + ess * 512: tc * T + ess * 512 + 512],
                                R["gbuf"][:, ess * 512:(ess + 1) * 512], OP.mult
                                ).then_inc(sem["dv"], 1)
        ctx.cnt["dv"] = dvl + 16
    ctx.barrier()

    dve = ctx.cnt["dv"]
    eve = ctx.cnt["ev"]

    @block.scalar
    def _(sc):
        sc.wait_ge(sem["dv"], dve)
        for tc in range(16):
            sc.activation(buf2[:, tc * T:(tc + 1) * T], buf2[:, tc * T:(tc + 1) * T],
                          AF.Exp, bias=R["nsb"][:, 0:1], scale=R["ccol"][:, tc:tc + 1]
                          ).then_inc(sem["ev"], 1)
    ctx.cnt["ev"] = eve + 16

    @block.vector
    def _(v):
        for tc in range(16):
            v.wait_ge(sem["ev"], eve + tc + 1)
            v.tensor_reduce(R["lcol"][:, tc:tc + 1], buf2[:, tc * T:(tc + 1) * T],
                            mybir.AxisListType.X, OP.add).then_inc(sem["dv"], 1)
            ctx.cnt["dv"] += 1
            v.wait_ge(sem["dv"], ctx.cnt["dv"])
            v.reciprocal(R["lcol"][:, 16 + tc:17 + tc], R["lcol"][:, tc:tc + 1]).then_inc(sem["dv"], 1)
            ctx.cnt["dv"] += 1

    mmD = ctx.cnt["mm"]

    @block.tensor
    def _(te):
        te.wait_ge(sem["ev"], eve + 16)
        for tc in range(16):
            for ts in range(4):
                last = te.matmul(pf[ts][0:1, 0:512], R["onec"][:],
                                 buf2[:, tc * T + ts * 512: tc * T + ts * 512 + 512],
                                 start=(tc == 0), stop=(tc == 15))
        last.then_inc(sem["mm"], 1)
    ctx.cnt["mm"] = mmD + 1

    evD = ctx.cnt["ev"]

    @block.scalar
    def _(sc):
        sc.wait_ge(sem["mm"], mmD + 1)
        for ts in range(4):
            last = sc.activation(S(2)[:, ts * 512:(ts + 1) * 512], pf[ts][0:1, 0:512],
                                 AF.Copy, bias=0.0, scale=1.0)
        last.then_inc(sem["ev"], 1)
    ctx.cnt["ev"] = evD + 1

    @block.sync
    def _(s):
        s.wait_ge(sem["ev"], evD + 1)
        s.dma_start(out=dram["d_in"].ap().rearrange("(one a) e -> one (a e)", one=1, a=16),
                    in_=S(2)).then_inc(sem["st"], 16)
    ctx.cnt["st"] += 16
    ctx.barrier()

    mmT = ctx.cnt["mm"]
    evT = ctx.cnt["ev"]
    pairs = []
    for a in range(KC):
        for b in range(a, KC):
            pairs.append((a, b))

    @block.tensor
    def _(te):
        n = 0
        for (a, b) in pairs:
            if n >= 1:
                te.wait_ge(sem["ev"], evT + n)
            te.transpose(pt[0][:, 0:128], xbf[:, a * T + b * 128: a * T + b * 128 + 128],
                         R["identb"][:]).then_inc(sem["mm"], 1)
            if a != b:
                te.transpose(pt[1][:, 0:128], xbf[:, b * T + a * 128: b * T + a * 128 + 128],
                             R["identb"][:]).then_inc(sem["mm"], 1)
            n += 1
    nsw = len(pairs)
    ntr = sum(2 if a != b else 1 for (a, b) in pairs)
    ctx.cnt["mm"] = mmT + ntr

    @block.scalar
    def _(sc):
        k = 0
        n = 0
        for (a, b) in pairs:
            if a != b:
                sc.wait_ge(sem["mm"], mmT + k + 2)
                sc.activation(xbf[:, a * T + b * 128: a * T + b * 128 + 128], pt[1][:, 0:128],
                              AF.Copy, bias=0.0, scale=1.0)
                sc.activation(xbf[:, b * T + a * 128: b * T + a * 128 + 128], pt[0][:, 0:128],
                              AF.Copy, bias=0.0, scale=1.0).then_inc(sem["ev"], 1)
                k += 2
            else:
                sc.wait_ge(sem["mm"], mmT + k + 1)
                sc.activation(xbf[:, a * T + b * 128: a * T + b * 128 + 128], pt[0][:, 0:128],
                              AF.Copy, bias=0.0, scale=1.0).then_inc(sem["ev"], 1)
                k += 1
            n += 1
    ctx.cnt["ev"] = evT + nsw
    ctx.barrier()

    sdt = dram["s_in"].ap().rearrange("(esc kc r) (dd m) -> esc kc (r dd) m",
                                      esc=16, kc=16, r=8, dd=16, m=128)
    mmS = ctx.cnt["mm"]
    evS = ctx.cnt["ev"]
    stS = [ctx.cnt["sA"], ctx.cnt["sB"]]

    @block.tensor
    def _(te):
        idx = 0
        for kcd in range(16):
            for ds in range(4):
                if idx >= 4:
                    te.wait_ge(sem["ev"], evS + idx - 3)
                for tc in range(16):
                    last = te.matmul(pf[idx % 4][:, 0:512],
                                     xbf[:, tc * T + kcd * 128: tc * T + (kcd + 1) * 128],
                                     buf2[:, tc * T + ds * 512: tc * T + ds * 512 + 512],
                                     start=(tc == 0), stop=(tc == 15))
                last.then_inc(sem["mm"], 1)
                idx += 1
    ctx.cnt["mm"] = mmS + 64

    for idx in range(64):
        kcd, ds = divmod(idx, 4)
        sp = "sA" if idx % 2 == 0 else "sB"

        @block.scalar
        def _(sc, idx=idx):
            sc.wait_ge(sem["mm"], mmS + idx + 1)
            if idx >= 2:
                sc.wait_ge(sem["s" + ("A" if idx % 2 == 0 else "B")],
                           stS[idx % 2] + 16 * (idx // 2))
            sc.activation(R["sstage"][:, (idx % 2) * 512:(idx % 2) * 512 + 512],
                          pf[idx % 4][:, 0:512], AF.Copy, bias=0.0, scale=1.0
                          ).then_inc(sem["ev"], 1)
        ctx.cnt["ev"] += 1

        @block.sync
        def _(s, idx=idx, kcd=kcd, ds=ds, sp=sp):
            s.wait_ge(sem["ev"], evS + idx + 1)
            s.dma_start(out=sdt[ds * 4:(ds + 1) * 4, kcd].rearrange("e p m -> p e m"),
                        in_=R["sstage"][:, (idx % 2) * 512:(idx % 2) * 512 + 512
                                        ].rearrange("p (e m) -> p e m", e=4)
                        ).then_inc(sem[sp], 16)
        ctx.cnt[sp] += 16

    @block.gpsimd
    def _(g):
        g.wait_ge(sem["sA"], ctx.cnt["sA"])
        g.wait_ge(sem["sB"], ctx.cnt["sB"])
        g.wait_ge(sem["st"], ctx.cnt["st"])
        g.collective_compute("AllGather", OP.bypass, replica_groups=PAIRS,
                             ins=[dram["s_in"].ap().opt()], outs=[dram["s_out"].ap().opt()]
                             ).then_inc(sem["cc"])
        g.collective_compute("AllGather", OP.bypass, replica_groups=PAIRS,
                             ins=[dram["d_in"].ap().opt()], outs=[dram["d_out"].ap().opt()]
                             ).then_inc(sem["cc"])
    ctx.cnt["cc"] += 2

    dvC = ctx.cnt["dv"]

    @block.vector
    def _(v):
        v.wait_ge(sem["mm"], ctx.cnt["mm"])
        for tc in range(16):
            v.tensor_scalar(buf2[:, tc * T:(tc + 1) * T], buf2[:, tc * T:(tc + 1) * T],
                            R["lcol"][:, 16 + tc:17 + tc], None, OP.mult).then_inc(sem["dv"], 1)
    ctx.cnt["dv"] = dvC + 16
    ctx.barrier()

    mmC = ctx.cnt["mm"]
    evC = ctx.cnt["ev"]

    @block.tensor
    def _(te):
        n = 0
        for esc in range(16):
            for tc in range(16):
                if n >= 4:
                    te.wait_ge(sem["ev"], evC + n - 3)
                te.transpose(pt[n % 4][:, 0:128],
                             buf2[:, tc * T + esc * 128: tc * T + esc * 128 + 128],
                             R["identb"][:]).then_inc(sem["mm"], 1)
                n += 1
    ctx.cnt["mm"] = mmC + 256

    @block.scalar
    def _(sc):
        n = 0
        for esc in range(16):
            for tc in range(16):
                sc.wait_ge(sem["mm"], mmC + n + 1)
                sc.activation(xbf[:, esc * T + tc * 128: esc * T + tc * 128 + 128],
                              pt[n % 4][:, 0:128], AF.Copy, bias=0.0, scale=1.0
                              ).then_inc(sem["ev"], 1)
                n += 1
    ctx.cnt["ev"] = evC + 256
    ctx.barrier()

    @block.gpsimd
    def _(g):
        g.wait_ge(sem["cc"], ctx.cnt["cc"])
        g.dma_start(out=R["sstage"][0:1, 0:2048],
                    in_=dram["d_out"].ap().rearrange("(rk a) e -> rk (a e)", rk=2)[0:1]
                    ).then_inc(sem["xs"], 16)
        g.dma_start(out=R["gbuf"][0:1, 0:2048],
                    in_=dram["d_out"].ap().rearrange("(rk a) e -> rk (a e)", rk=2)[1:2]
                    ).then_inc(sem["xs"], 16)
    ctx.cnt["xs"] += 32

    @block.vector
    def _(v):
        v.wait_ge(sem["xs"], ctx.cnt["xs"])
        v.tensor_tensor(S(1), R["sstage"][0:1, 0:2048], R["gbuf"][0:1, 0:2048],
                        OP.add).then_inc(sem["dv"], 1)
        ctx.cnt["dv"] += 1
        v.wait_ge(sem["dv"], ctx.cnt["dv"])
        v.tensor_scalar_max(S(1), S(1), 1e-30).then_inc(sem["dv"], 1)
        ctx.cnt["dv"] += 1
        v.wait_ge(sem["dv"], ctx.cnt["dv"])
        v.reciprocal(S(1), S(1)).then_inc(sem["dv"], 1)
        ctx.cnt["dv"] += 1
        v.wait_ge(sem["dv"], ctx.cnt["dv"])
        v.tensor_copy(R["rowbf"][0:1, 0:2048], S(1)).then_inc(sem["dv"], 1)
        ctx.cnt["dv"] += 1

    mmd = ctx.cnt["mm"]

    evd = ctx.cnt["ev"]

    @block.tensor
    def _(te):
        te.wait_ge(sem["dv"], ctx.cnt["dv"])
        for i in range(16):
            if i >= 4:
                te.wait_ge(sem["ev"], evd + i - 3)
            te.transpose(pt[i % 4][0:128, 0:1], R["rowbf"][0:1, i * 128:(i + 1) * 128],
                         R["identb"][0:1, 0:1]).then_inc(sem["mm"], 1)
    ctx.cnt["mm"] = mmd + 16

    @block.scalar
    def _(sc):
        for i in range(16):
            sc.wait_ge(sem["mm"], mmd + i + 1)
            sc.activation(R["dcol"][:, i:i + 1], pt[i % 4][0:128, 0:1], AF.Copy,
                          bias=0.0, scale=1.0).then_inc(sem["ev"], 1)
    ctx.cnt["ev"] = evd + 16
    ctx.barrier()

    sot = dram["so_t"].ap().rearrange("(dc esc r) (dd m) -> dc esc (r dd) m",
                                      dc=16, esc=16, r=8, dd=16, m=128)
    sov = dram["s_out"].ap().rearrange("(rk esc kc r) (dd m) -> rk esc kc (r dd) m",
                                       rk=2, esc=16, kc=16, r=8, dd=16, m=128)
    SA = buf2[:, 0:2048]
    SB_ = buf2[:, 2048:4096]
    SE = buf2[:, 4096:6144]
    W1b = lambda h: buf2[:, 6144:14336]
    W2b = lambda h: buf2[:, 14336:22528]
    h1sb = buf2[:, 22528:23040]
    h1t = buf2[:, 23040:23552]
    sostage = buf2[:, 23552:25600]

    mm_e = []
    for e in range(NE):
        kvb = [ctx.cnt["kvA"], ctx.cnt["kvB"]]
        wbe = [ctx.cnt["wA"], ctx.cnt["wB"]]
        mme = ctx.cnt["mm"]
        mm_e.append(mme)
        eve2 = ctx.cnt["ev"]
        dve2 = ctx.cnt["dv"]
        vbe = [ctx.cnt["wC"], ctx.cnt["wD"]]
        ste = [ctx.cnt["sA"], ctx.cnt["sB"]]

        @block.sync
        def _(s, e=e, mme=mme):
            if e >= 1:
                s.wait_ge(sem["mm"], mm_e[-2] + 1 if e >= 2 else mme - 8)
            for rk in range(2):
                s.dma_start(out=(SA if rk == 0 else SB_).rearrange("p (kc m) -> p kc m", kc=16),
                            in_=sov[rk, e].rearrange("kc p m -> p kc m")
                            ).then_inc(sem["kvA" if rk == 0 else "kvB"], 16)

        ctx.cnt["kvA"] = kvb[0] + 16
        ctx.cnt["kvB"] = kvb[1] + 16
        ctx.cnt["wC"] = vbe[0] + 16
        ctx.cnt["wD"] = vbe[1] + 16

        @block.gpsimd
        def _(g, e=e, mme=mme):
            if e >= 1:
                g.wait_ge(sem["mm"], mme - 8)
            g.dma_start(out=R["sstage"][0:1, 2048:2560],
                        in_=io["b1"][e].rearrange("(one h) -> one h", one=1)).then_inc(sem["wC"], 16)
            g.dma_start(out=W1b(e % 2).rearrange("p (kc h) -> p kc h", kc=16),
                        in_=io["w1"][e].rearrange("(kc p) h -> p kc h", p=128)
                        ).then_inc(sem["wA"], 16)
            if e >= 1:
                g.wait_ge(sem["mm"], mme)
            g.dma_start(out=R["sstage"][0:1, 0:2048],
                        in_=io["b2"][e].rearrange("(one d) -> one d", one=1)).then_inc(sem["wD"], 16)
            g.dma_start(out=W2b(e % 2).rearrange("p (hc d) -> p hc d", hc=4),
                        in_=io["w2e"][e].rearrange("(hc p) d -> p hc d", p=128)
                        ).then_inc(sem["wB"], 16)
        ctx.cnt["wA"] = wbe[0] + 16
        ctx.cnt["wB"] = wbe[1] + 16

        @block.vector
        def _(v, e=e, kvb=kvb, dve2=dve2, mme=mme):
            v.wait_ge(sem["kvA"], kvb[0] + 16)
            v.wait_ge(sem["kvB"], kvb[1] + 16)
            if e >= 1:
                v.wait_ge(sem["mm"], mme - 8)
            v.tensor_tensor(SE, SA, SB_, OP.add).then_inc(sem["dv"], 1)
            v.wait_ge(sem["dv"], dve2 + 1)
            v.tensor_scalar(SE, SE, R["dcol"][:, e:e + 1], None, OP.mult).then_inc(sem["dv"], 1)
        ctx.cnt["dv"] = dve2 + 2

        @block.tensor
        def _(te, e=e, mme=mme, dve2=dve2, wbe=wbe, vbe=vbe, eve2=eve2):
            te.wait_ge(sem["dv"], dve2 + 2)
            te.wait_ge(sem["ev"], eve2)
            te.wait_ge(sem["wA"], wbe[0] + 16)
            te.wait_ge(sem["wC"], vbe[0] + 16)
            for kc in range(16):
                te.matmul(pf[0][:, 0:512], SE[:, kc * 128:(kc + 1) * 128],
                          W1b(e % 2)[:, kc * 512:(kc + 1) * 512],
                          start=(kc == 0), stop=False)
            te.matmul(pf[0][:, 0:512], R["oner"][:],
                      R["sstage"][0:1, 2048:2560],
                      start=False, stop=True).then_inc(sem["mm"], 1)
            te.wait_ge(sem["ev"], eve2 + 1)
            for hc in range(4):
                te.transpose(pt[hc][:, 0:128], h1sb[:, hc * 128:(hc + 1) * 128],
                             R["identb"][:]).then_inc(sem["mm"], 1)
            te.wait_ge(sem["ev"], eve2 + 1 + 4)
            te.wait_ge(sem["wB"], wbe[1] + 16)
            te.wait_ge(sem["wD"], vbe[1] + 16)
            for ds in range(4):
                for hc in range(4):
                    te.matmul(pf[ds][:, 0:512], h1t[:, hc * 128:(hc + 1) * 128],
                              W2b(e % 2)[:, hc * 2048 + ds * 512: hc * 2048 + ds * 512 + 512],
                              start=(hc == 0), stop=False)
                te.matmul(pf[ds][:, 0:512], R["oner"][:],
                          R["sstage"][0:1, ds * 512:(ds + 1) * 512],
                          start=False, stop=True).then_inc(sem["mm"], 1)
        ctx.cnt["mm"] = mme + 9

        @block.scalar
        def _(sc, e=e, mme=mme, eve2=eve2, ste=ste):
            sc.wait_ge(sem["mm"], mme + 1)
            sc.activation(h1sb, pf[0][:, 0:512], AF.Gelu, bias=0.0, scale=1.0
                          ).then_inc(sem["ev"], 1)
            for hc in range(4):
                sc.wait_ge(sem["mm"], mme + 1 + hc + 1)
                sc.activation(h1t[:, hc * 128:(hc + 1) * 128], pt[hc][:, 0:128],
                              AF.Copy, bias=0.0, scale=1.0).then_inc(sem["ev"], 1)
            if e >= 1:
                sc.wait_ge(sem["sA"], ste[0])
                sc.wait_ge(sem["sB"], ste[1])
            for ds in range(4):
                sc.wait_ge(sem["mm"], mme + 5 + ds + 1)
                sc.activation(sostage[:, ds * 512:(ds + 1) * 512], pf[ds][:, 0:512],
                              AF.Copy, bias=0.0, scale=1.0).then_inc(sem["ev"], 1)
        ctx.cnt["ev"] = eve2 + 9

        sp = "sA" if e % 2 == 0 else "sB"

        @block.sync
        def _(s, e=e, eve2=eve2, sp=sp):
            s.wait_ge(sem["ev"], eve2 + 9)
            s.dma_start(out=sot[:, e].rearrange("dc p m -> p dc m"),
                        in_=sostage.rearrange("p (dc m) -> p dc m", dc=16)
                        ).then_inc(sem[sp], 16)
        ctx.cnt[sp] += 16
    ctx.barrier()

    @block.sync
    def _(s):
        s.wait_ge(sem["sA"], ctx.cnt["sA"])
        s.wait_ge(sem["sB"], ctx.cnt["sB"])
    ctx.barrier()

    so_w = sot.rearrange("dc esc p m -> dc p esc m")
    emit_outproj_residual(ctx, R, so_w, xbf, io["x_in"], io["x_out"])
    ctx.barrier()


def build_l1_ffn():
    nc = bass.Bass("TRN2", target_bir_lowering=False)
    io = {}
    io["x_in"] = nc.dram_tensor("x_in", [D, T], F32, kind="ExternalInput").ap()
    io["x_out"] = nc.dram_tensor("x_out", [D, T], F32, kind="ExternalOutput").ap()
    io["ident"] = nc.dram_tensor("ident", [128, 128], F32, kind="ExternalInput").ap()
    io["gamma"] = nc.dram_tensor("gamma", [D], F32, kind="ExternalInput").ap()
    io["beta"] = nc.dram_tensor("beta", [D], F32, kind="ExternalInput").ap()
    io["scale"] = nc.dram_tensor("scale", [1], F32, kind="ExternalInput").ap()
    io["mu_t"] = nc.dram_tensor("mu_t", [16, 128, KC * 128], F32, kind="ExternalInput").ap()
    io["mu_r"] = nc.dram_tensor("mu_r", [4, 128, KC * 512], F32, kind="ExternalInput").ap()
    io["w1"] = nc.dram_tensor("w1", [16, 2048, 512], F32, kind="ExternalInput").ap()
    io["b1"] = nc.dram_tensor("b1", [16, 512], F32, kind="ExternalInput").ap()
    io["w2e"] = nc.dram_tensor("w2e", [16, 512, 2048], F32, kind="ExternalInput").ap()
    io["b2"] = nc.dram_tensor("b2", [16, 2048], F32, kind="ExternalInput").ap()
    dram = {
        "s_in": nc.dram_tensor("s_cc_in", [D, T], BF16),
        "s_out": nc.dram_tensor("s_cc_out", [2 * D, T], BF16),
        "d_in": nc.dram_tensor("d_cc_in", [16, 128], F32),
        "d_out": nc.dram_tensor("d_cc_out", [32, 128], F32),
        "so_t": nc.dram_tensor("so_t", [D, T], BF16),
    }
    with contextlib.ExitStack() as stack:
        sems, R = alloc_shared(nc, stack, kind="l1f")
        with nc.Block() as block:
            ctx = Ctx(nc, block, sems)
            ctx.dummy = R["constf"][0:1, 2:4]
            emit_const_loads(ctx, R, io, kind="l1f")
            emit_l1_ffn_block(ctx, R, io, dram)

            @block.sync
            def _(s):
                s.wait_ge(sems["sA"], ctx.cnt["sA"])
                s.wait_ge(sems["sB"], ctx.cnt["sB"])
    return nc


def run_l1_ffn(x_shards, gamma, beta, mu, scale, W1, b1, W2, b2):
    if "l1f" not in _BUILD_CACHE:
        _BUILD_CACHE["l1f"] = build_l1_ffn()
    nc = _BUILD_CACHE["l1f"]
    mu2d = np.ascontiguousarray(np.asarray(mu).reshape(D, 2048).astype(np.float32))
    mu_t = lhsT_tiles(mu2d.T)
    mu_r = np.ascontiguousarray(
        mu2d.reshape(KC, 128, 4, 512).transpose(2, 1, 0, 3).reshape(4, 128, KC * 512))
    wt = dict(ident=np.eye(128, dtype=np.float32),
              gamma=np.asarray(gamma, np.float32), beta=np.asarray(beta, np.float32),
              scale=np.asarray(scale, np.float32),
              mu_t=mu_t, mu_r=mu_r,
              w1=np.asarray(W1, np.float32), b1=np.asarray(b1, np.float32),
              w2e=np.asarray(W2, np.float32), b2=np.asarray(b2, np.float32))
    in_maps = []
    for c in range(8):
        m = dict(x_in=x_shards[c])
        m.update(wt)
        in_maps.append(m)
    res = run_bass_kernel_spmd(nc, in_maps, core_ids=list(range(8)))
    return [res.results[c]["x_out"] for c in range(8)]



def build_fused():
    nc = bass.Bass("TRN2", target_bir_lowering=False)
    io = {}
    io["x_in"] = nc.dram_tensor("x_in", [D, T], F32, kind="ExternalInput").ap()
    io["x_out"] = nc.dram_tensor("x_out", [D, T], F32, kind="ExternalOutput").ap()
    for nm, shp in [("ident", [128, 128]), ("maskA", [4, 128, BS]), ("maskB", [4, 128, BS]),
                    ("cosT", [32, T]), ("sinT", [32, T]), ("perm32", [128, 128]),
                    ("h128", [128, 128]), ("h64b", [128, 128])]:
        io[nm] = nc.dram_tensor(nm, shp, F32, kind="ExternalInput").ap()
    for nm in ["wq", "wk", "wv0", "wg", "wp"]:
        io[nm] = nc.dram_tensor(nm, [16, 128, KC * 128], F32, kind="ExternalInput").ap()
    io["f_w1"] = nc.dram_tensor("f_w1", [64, 128, KC * 128], F32, kind="ExternalInput").ap()
    io["f_w3"] = nc.dram_tensor("f_w3", [64, 128, KC * 128], F32, kind="ExternalInput").ap()
    io["f_w2"] = nc.dram_tensor("f_w2", [16, 2, 128, 32 * 128], F32, kind="ExternalInput").ap()
    io["wqk"] = nc.dram_tensor("wqk", [32, 128, KC * 128], F32, kind="ExternalInput").ap()
    io["wv1"] = nc.dram_tensor("wv1", [16, 128, KC * 128], F32, kind="ExternalInput").ap()
    io["wout"] = nc.dram_tensor("wout", [16, 128, KC * 128], F32, kind="ExternalInput").ap()
    io["gamma"] = nc.dram_tensor("gamma", [D], F32, kind="ExternalInput").ap()
    io["beta"] = nc.dram_tensor("beta", [D], F32, kind="ExternalInput").ap()
    io["scale"] = nc.dram_tensor("scale", [1], F32, kind="ExternalInput").ap()
    io["mu_t"] = nc.dram_tensor("mu_t", [16, 128, KC * 128], F32, kind="ExternalInput").ap()
    io["mu_r"] = nc.dram_tensor("mu_r", [4, 128, KC * 512], F32, kind="ExternalInput").ap()
    io["w1"] = nc.dram_tensor("w1", [16, 2048, 512], F32, kind="ExternalInput").ap()
    io["b1"] = nc.dram_tensor("b1", [16, 512], F32, kind="ExternalInput").ap()
    io["w2e"] = nc.dram_tensor("w2e", [16, 512, 2048], F32, kind="ExternalInput").ap()
    io["b2"] = nc.dram_tensor("b2", [16, 2048], F32, kind="ExternalInput").ap()

    xb1 = nc.dram_tensor("xb1", [D, T], F32)
    xb2 = nc.dram_tensor("xb2", [D, T], F32)
    xb3 = nc.dram_tensor("xb3", [D, T], F32)
    dram = {
        "k_in": nc.dram_tensor("k_cc_in", [D, T], BF16),
        "k_out": nc.dram_tensor("k_cc_out", [2 * D, T], BF16),
        "v_in": nc.dram_tensor("v_cc_in", [T, D], BF16),
        "v_out": nc.dram_tensor("v_cc_out", [2 * T, D], BF16),
        "q_t": nc.dram_tensor("q_t", [D, T], BF16),
        "g_t": nc.dram_tensor("g_t", [D, T], BF16),
        "h2": nc.dram_tensor("h2", [64 * 128, T], BF16),
        "s_in": nc.dram_tensor("s_cc_in", [D, T], BF16),
        "s_out": nc.dram_tensor("s_cc_out", [2 * D, T], BF16),
        "d_in": nc.dram_tensor("d_cc_in", [16, 128], F32),
        "d_out": nc.dram_tensor("d_cc_out", [32, 128], F32),
        "so_t": nc.dram_tensor("so_t", [D, T], BF16),
    }

    with contextlib.ExitStack() as stack:
        sems, R = alloc_shared(nc, stack, kind="fused")
        with nc.Block() as block:
            ctx = Ctx(nc, block, sems)
            ctx.dummy = R["constf"][0:1, 2:4]
            emit_const_loads(ctx, R, io, kind="fused")

            def glue():
                @block.sync
                def _(s):
                    s.wait_ge(sems["sA"], ctx.cnt["sA"])
                    s.wait_ge(sems["sB"], ctx.cnt["sB"])
                    s.wait_ge(sems["st"], ctx.cnt["st"])
                ctx.barrier()

            io0 = dict(io)
            io0["x_out"] = xb1.ap()
            io0["wv"] = io["wv0"]
            emit_l0_attn_block(ctx, R, io0, dram)
            glue()

            io1 = dict(io)
            io1["x_in"] = xb1.ap()
            io1["x_out"] = xb2.ap()
            io1["w1"] = io["f_w1"]
            io1["w3"] = io["f_w3"]
            io1["w2"] = io["f_w2"]
            emit_l0_ffn_block(ctx, R, io1, dram)
            glue()

            io2 = dict(io)
            io2["x_in"] = xb2.ap()
            io2["x_out"] = xb3.ap()
            io2["wv"] = io["wv1"]
            emit_l1_attn_block(ctx, R, io2, dram)
            glue()

            io3 = dict(io)
            io3["x_in"] = xb3.ap()
            emit_l1_ffn_block(ctx, R, io3, dram)
            glue()
    return nc


def prep_fused_inputs(inputs):
    qkv_t = lhsT_tiles(np.asarray(inputs["L1_attn_w_qkv"]))
    shared = dict(
        ident=np.eye(128, dtype=np.float32),
        h128=hadamard(128),
        h64b=np.kron(np.eye(2, dtype=np.float32), hadamard(64)),
        perm32=perm32_mat(),
        wq=lhsT_tiles(np.asarray(inputs["L0_attn_Wq"])),
        wk=lhsT_tiles(np.asarray(inputs["L0_attn_Wk"])),
        wv0=lhsT_tiles(np.asarray(inputs["L0_attn_Wv"])),
        wg=lhsT_tiles(np.asarray(inputs["L0_attn_Wg"])),
        wp=lhsT_tiles(np.asarray(inputs["L0_attn_Wp"])),
        f_w1=lhsT_tiles(np.asarray(inputs["L0_ffn_W1"])),
        f_w3=lhsT_tiles(np.asarray(inputs["L0_ffn_W3"])),
        f_w2=w2_tiles(np.asarray(inputs["L0_ffn_W2"])),
        wqk=np.ascontiguousarray(qkv_t[0:32]),
        wv1=np.ascontiguousarray(qkv_t[32:48]),
        wout=lhsT_tiles(np.asarray(inputs["L1_attn_w_out"])),
        gamma=np.asarray(inputs["L1_ffn_gamma"], np.float32),
        beta=np.asarray(inputs["L1_ffn_beta"], np.float32),
        scale=np.asarray(inputs["L1_ffn_scale"], np.float32),
        w1=np.asarray(inputs["L1_ffn_W1"], np.float32),
        b1=np.asarray(inputs["L1_ffn_b1"], np.float32),
        w2e=np.asarray(inputs["L1_ffn_W2"], np.float32),
        b2=np.asarray(inputs["L1_ffn_b2"], np.float32),
    )
    mu2d = np.ascontiguousarray(np.asarray(inputs["L1_ffn_mu"]).reshape(D, 2048).astype(np.float32))
    shared["mu_t"] = lhsT_tiles(mu2d.T)
    shared["mu_r"] = np.ascontiguousarray(
        mu2d.reshape(KC, 128, 4, 512).transpose(2, 1, 0, 3).reshape(4, 128, KC * 512))
    percore = []
    for c in range(8):
        r = c % 2
        mA, mB = make_masks(r)
        cosT, sinT = rot_tables(r)
        percore.append(dict(maskA=mA, maskB=mB, cosT=cosT, sinT=sinT))
    return shared, percore


def kernel(**inputs):
    x = np.asarray(inputs["L0_attn_x"], np.float32)
    shards = shard_x(x)
    if "fused" not in _BUILD_CACHE:
        _BUILD_CACHE["fused"] = build_fused()
    nc = _BUILD_CACHE["fused"]
    shared, percore = prep_fused_inputs(inputs)
    in_maps = []
    for c in range(8):
        m = dict(x_in=shards[c])
        m.update(shared)
        m.update(percore[c])
        in_maps.append(m)
    res = run_bass_kernel_spmd(nc, in_maps, core_ids=list(range(8)))
    return unshard_x([res.results[c]["x_out"] for c in range(8)]).astype(np.float32)


def kernel_unfused(**inputs):
    x = np.asarray(inputs["L0_attn_x"], np.float32)
    shards = shard_x(x)
    Ws0 = {k: np.asarray(inputs[f"L0_attn_W{k}"]) for k in ["q", "k", "v", "g", "p"]}
    shards = run_l0_attn(shards, Ws0)
    shards = run_l0_ffn(shards, inputs["L0_ffn_W1"], inputs["L0_ffn_W2"], inputs["L0_ffn_W3"])
    shards = run_l1_attn(shards, inputs["L1_attn_w_qkv"], inputs["L1_attn_w_out"])
    shards = run_l1_ffn(shards, inputs["L1_ffn_gamma"], inputs["L1_ffn_beta"],
                        inputs["L1_ffn_mu"], inputs["L1_ffn_scale"],
                        inputs["L1_ffn_W1"], inputs["L1_ffn_b1"],
                        inputs["L1_ffn_W2"], inputs["L1_ffn_b2"])
    return unshard_x(shards).astype(np.float32)
